# version 04
# baseline (speedup 1.0000x reference)
import contextlib

import numpy as np
import ml_dtypes

import concourse.bass as bass
import concourse.mybir as mybir
from concourse.bass_utils import run_bass_kernel_spmd

F32 = mybir.dt.float32
BF16 = mybir.dt.bfloat16
I16 = mybir.dt.int16
U16 = mybir.dt.uint16
Alu = mybir.AluOpType
Act = mybir.ActivationFunctionType
AX = mybir.AxisListType

NCORES = 8
H = 1024
I3 = 3072
NT = 128
LF = 2048
ND = 17
RING = 18
NMAX = 16
TQ = 512
CH = 2048
INV7 = float(np.float32(1.0 / 7.0))
INVQ = float(np.float32(1.0 / 32767.0))
Q15 = 32767.0
Q16 = 65536.0
INV65536 = float(np.float32(1.0 / 65536.0))
INVH = float(np.float32(1.0 / H))
EPS12 = 1e-12

BF = ml_dtypes.bfloat16


def build_toeplitz(filt):
    out = np.zeros((ND, NT, NT), np.float32)
    j, i = np.meshgrid(np.arange(NT), np.arange(NT), indexing="ij")
    for d in range(ND):
        idx = NT * d + j - i
        m = (idx >= 0) & (idx < LF)
        out[d][m] = filt[idx[m]]
    return np.ascontiguousarray(out.reshape(ND * NT, NT))


class Plan:

    ENGS = ("sy", "gp", "v", "a", "t")

    def __init__(self):
        self.ops = {k: [] for k in self.ENGS}
        self.count = {}
        for k in ("v", "a", "t"):
            self.count[k] = 0
        self.lastw = {k: {} for k in self.ENGS}
        self.chan_eng = {}
        self.chan_last = {}

    def _record(self, eng, sem, emit, waits, inc):
        seen = {}
        for tok in waits:
            if tok is None:
                continue
            s, v = tok
            if v <= 0 or self.lastw[eng].get(s, 0) >= v:
                continue
            seen[s] = max(seen.get(s, 0), v)
        w = []
        for s, v in seen.items():
            w.append((s, v))
            self.lastw[eng][s] = v
        self.count[sem] = self.count.get(sem, 0) + inc
        self.ops[eng].append((w, emit, sem, inc))
        return (sem, self.count[sem])

    def op(self, eng, emit, waits=()):
        return self._record(eng, eng, emit, waits, 1)

    def dma(self, chan, emit, waits=(), eng="sy"):
        if chan not in self.chan_eng:
            self.chan_eng[chan] = eng
        eng = self.chan_eng[chan]
        w = list(waits) + [self.chan_last.get(chan)]
        tok = self._record(eng, chan, emit, w, 16)
        self.chan_last[chan] = tok
        return tok

    def barrier_all(self, eng="gp"):
        waits = [(s, c) for s, c in self.count.items() if c > 0]
        self._record(eng, eng if eng in self.count else "v",
                     lambda e: None, waits, 0)

    def emit_block(self, nc, es):
        sems = {s: es.enter_context(nc.semaphore("sem_" + s))
                for s in self.count}
        with nc.Block() as block:
            def runner(engname):
                def run(engobj):
                    for (waits, emit, sem, inc) in self.ops[engname]:
                        for (s, v) in waits:
                            engobj.wait_ge(sems[s], v)
                        ins = emit(engobj)
                        if ins is not None and inc > 0:
                            ins.then_inc(sems[sem], inc)
                return run
            if self.ops["sy"]:
                block.sync(runner("sy"))
            if self.ops["gp"]:
                block.gpsimd(runner("gp"))
            if self.ops["v"]:
                block.vector(runner("v"))
            if self.ops["a"]:
                block.scalar(runner("a"))
            if self.ops["t"]:
                block.tensor(runner("t"))


def maxes_prologue(nc, es, p, names):
    tiles, toks = {}, {}
    for n in names:
        inp = nc.dram_tensor(n, [1, 8 * NMAX], F32, kind="ExternalInput")
        scr = nc.dram_tensor("scr_" + n, [1, NMAX], F32, kind="Internal")
        mh = es.enter_context(nc.sbuf_tensor("mh_" + n, [1, 8 * NMAX], F32))
        m4 = es.enter_context(nc.sbuf_tensor("m4_" + n, [1, 4 * NMAX], F32))
        m2 = es.enter_context(nc.sbuf_tensor("m2_" + n, [1, 2 * NMAX], F32))
        m1 = es.enter_context(nc.sbuf_tensor("m1_" + n, [1, NMAX], F32))
        mx = es.enter_context(nc.sbuf_tensor("mx_" + n, [NT, NMAX], F32))
        t = p.dma("mis", lambda e, inp=inp, mh=mh: e.dma_start(
            out=mh[:], in_=inp.ap()))
        t = p.op("v", lambda e, mh=mh, m4=m4: e.tensor_tensor(
            m4[:], mh[:, :4 * NMAX], mh[:, 4 * NMAX:], Alu.max), waits=[t])
        t = p.op("v", lambda e, m4=m4, m2=m2: e.tensor_tensor(
            m2[:], m4[:, :2 * NMAX], m4[:, 2 * NMAX:], Alu.max), waits=[t])
        t = p.op("v", lambda e, m2=m2, m1=m1: e.tensor_tensor(
            m1[:], m2[:, :NMAX], m2[:, NMAX:], Alu.max), waits=[t])
        t = p.dma("mis", lambda e, scr=scr, m1=m1: e.dma_start(
            out=scr.ap(), in_=m1[:]), waits=[t])
        t = p.dma("mis", lambda e, mx=mx, scr=scr: e.dma_start(
            out=mx[:], in_=scr.ap().broadcast_to([NT, NMAX])), waits=[t])
        tiles[n] = mx
        toks[n] = t
    return tiles, toks


def max_finalize(nc, es, p, maxacc, idf_sb, out_name, extra_waits=()):
    mout = nc.dram_tensor(out_name, [NMAX, 1], F32, kind="ExternalOutput")
    psf = es.enter_context(nc.psum_tensor("ps_mfin", [NT, NT], F32))
    mfs = es.enter_context(nc.sbuf_tensor("mfin_s", [NMAX, NT], F32))
    mfr = es.enter_context(nc.sbuf_tensor("mfin_r", [NMAX, 1], F32))
    w = list(extra_waits) + [("v", p.count["v"])]
    t = p.op("t", lambda e: e.transpose(psf[:NMAX, :], maxacc[:], idf_sb[:]),
             waits=w)
    t = p.op("v", lambda e: e.tensor_copy(mfs[:], psf[:NMAX, :]), waits=[t])
    t = p.op("v", lambda e: e.tensor_reduce(mfr[:], mfs[:], AX.X, Alu.max),
             waits=[t])
    return p.dma("mis", lambda e: e.dma_start(out=mout.ap(), in_=mfr[:]),
                 waits=[t])



def plan_round_signed(p, t_tok, tbuf, ibuf, dbuf, out_emit):
    tk = p.op("v", lambda e: e.tensor_copy(ibuf, tbuf), waits=[t_tok])
    tk = p.op("v", lambda e: e.tensor_tensor(dbuf, tbuf, ibuf, Alu.subtract),
              waits=[tk])
    tk = p.op("v", lambda e: e.scalar_tensor_tensor(
        tbuf, dbuf, 0.5, ibuf, Alu.is_ge, Alu.add), waits=[tk])
    tk = p.op("v", lambda e: out_emit(e, dbuf, tbuf), waits=[tk])
    return tk


def plan_round_u16(p, t_tok, tbuf, ubuf, dbuf, out, clip_hi):
    tk = p.op("v", lambda e: e.tensor_copy(ubuf, tbuf), waits=[t_tok])
    tk = p.op("v", lambda e: e.tensor_tensor(dbuf, tbuf, ubuf, Alu.subtract),
              waits=[tk])
    tk = p.op("v", lambda e: e.scalar_tensor_tensor(
        tbuf, dbuf, 0.5, ubuf, Alu.is_ge, Alu.add), waits=[tk])
    tk = p.op("v", lambda e: e.tensor_scalar(
        out, tbuf, float(clip_hi), None, Alu.min), waits=[tk])
    return tk


def dequant_weights(nc, es, p, specs, idb_sb, tag, psw=None):
    CT = 1024
    wbuf = [es.enter_context(nc.sbuf_tensor(f"wb{tag}{k}", [NT, CT], F32))
            for k in range(2)]
    wqb = [es.enter_context(nc.sbuf_tensor(f"wq{tag}{k}", [NT, CT], BF16))
           for k in range(2)]
    absr = es.enter_context(nc.sbuf_tensor(f"ab{tag}", [NT, CT // 32], F32))
    sblk = es.enter_context(nc.sbuf_tensor(f"sb{tag}", [NT, CT // 32], F32))
    sinv = es.enter_context(nc.sbuf_tensor(f"si{tag}", [NT, CT // 32], F32))
    tqb = es.enter_context(nc.sbuf_tensor(f"tq{tag}", [NT, CT], F32))
    dqb = es.enter_context(nc.sbuf_tensor(f"dq{tag}", [NT, CT], F32))
    kqb = es.enter_context(nc.sbuf_tensor(f"kq{tag}", [NT, CT], I16))
    if psw is None:
        psw = [es.enter_context(nc.psum_tensor(f"pw{tag}{k}", [NT, NT], BF16))
               for k in range(2)]
    last_evac = None
    tok_consumed = [None, None]
    tok_wq_read = [None, None]
    nblk = CT // 32
    n = 0
    for (w_dram, nrows, ncols, wt_sb) in specs:
        for ii in range(nrows // NT):
            for cc in range(ncols // CT):
                par = n % 2
                n += 1
                wb, wq = wbuf[par], wqb[par]
                tl = p.dma("wld", lambda e, wb=wb, w_dram=w_dram, ii=ii, cc=cc:
                           e.dma_start(
                               out=wb[:],
                               in_=w_dram.ap()[ii * NT:(ii + 1) * NT,
                                               cc * CT:(cc + 1) * CT]),
                           waits=[tok_consumed[par]])
                tv = p.op("v", lambda e, wb=wb: e.tensor_reduce(
                    absr[:].rearrange("p (b o) -> p b o", o=1),
                    wb[:].rearrange("p (b e) -> p b e", e=32),
                    AX.X, Alu.max, apply_absolute_value=True), waits=[tl])
                tv = p.op("v", lambda e: e.tensor_scalar(
                    sblk[:], absr[:], INV7, EPS12, Alu.mult, Alu.max),
                    waits=[tv])
                tv = p.op("v", lambda e: e.reciprocal(sinv[:], sblk[:]),
                          waits=[tv])
                tv = p.op("v", lambda e: e.tensor_scalar(
                    sinv[:], sinv[:], -1.0, None, Alu.mult), waits=[tv])
                tv = p.op("v", lambda e, wb=wb: e.tensor_tensor(
                    tqb[:].rearrange("p (b e) -> p b e", e=32),
                    wb[:].rearrange("p (b e) -> p b e", e=32),
                    sinv[:].rearrange("p (b o) -> p b o", o=1)
                    .broadcast_to([NT, nblk, 32]), Alu.mult), waits=[tv])
                tok_consumed[par] = tv
                tv = plan_round_signed(
                    p, tv, tqb[:], kqb[:], dqb[:],
                    lambda e, d, s: e.scalar_tensor_tensor(
                        kqb[:], d, -0.5, s, Alu.is_le, Alu.subtract))
                tv = p.op("v", lambda e, wq=wq: e.tensor_tensor(
                    wq[:].rearrange("p (b e) -> p b e", e=32),
                    kqb[:].rearrange("p (b e) -> p b e", e=32),
                    sblk[:].rearrange("p (b o) -> p b o", o=1)
                    .broadcast_to([NT, nblk, 32]), Alu.mult),
                    waits=[tv, tok_wq_read[par]])
                for ci in range(CT // NT):
                    cig = cc * (CT // NT) + ci
                    ps = psw[ci % 2]
                    ttr = p.op("t", lambda e, ps=ps, wq=wq, ci=ci: e.transpose(
                        ps[:], wq[:, ci * NT:(ci + 1) * NT], idb_sb[:]),
                        waits=[tv, last_evac])
                    last_evac = p.op(
                        "a", lambda e, ps=ps, wt_sb=wt_sb, cig=cig, ii=ii,
                        nrows=nrows: e.activation(
                            wt_sb[:, cig * nrows + ii * NT:
                                  cig * nrows + (ii + 1) * NT],
                            ps[:], Act.Identity), waits=[ttr])
                tok_wq_read[par] = last_evac
    return last_evac



def build_L1(T):
    ntiles = T // NT
    nc = bass.Bass("TRN2", target_bir_lowering=False)
    xs = nc.dram_tensor("xs", [T, H], F32, kind="ExternalInput")
    Fz = nc.dram_tensor("Fz", [ND * NT, NT], F32, kind="ExternalInput")
    xh_d = nc.dram_tensor("xh", [T, H], BF16, kind="ExternalOutput")
    xl_d = nc.dram_tensor("xl", [T, H], BF16, kind="ExternalOutput")
    part_d = nc.dram_tensor("partial", [T, H], F32, kind="ExternalOutput")
    idf_c = nc.inline_tensor(np.eye(NT, dtype=np.float32), name="idf")

    p = Plan()
    es = contextlib.ExitStack()
    with es:
        xr = es.enter_context(nc.sbuf_tensor("xr", [NT, RING * H], F32))
        ff = es.enter_context(nc.sbuf_tensor("ff", [NT, ND * NT], F32))
        fh = es.enter_context(nc.sbuf_tensor("fh", [NT, ND * NT], BF16))
        fl = es.enter_context(nc.sbuf_tensor("fl", [NT, ND * NT], BF16))
        xhr = es.enter_context(nc.sbuf_tensor("xhr", [NT, RING * H], BF16))
        xlr = es.enter_context(nc.sbuf_tensor("xlr", [NT, RING * H], BF16))
        xn = es.enter_context(nc.sbuf_tensor("xn", [NT, H], F32))
        dump = es.enter_context(nc.sbuf_tensor("dump", [NT, H], F32))
        smv = es.enter_context(nc.sbuf_tensor("smv", [NT, 1], F32))
        sqv = es.enter_context(nc.sbuf_tensor("sqv", [NT, 1], F32))
        muv = es.enter_context(nc.sbuf_tensor("muv", [NT, 1], F32))
        ex2 = es.enter_context(nc.sbuf_tensor("ex2", [NT, 1], F32))
        mu2 = es.enter_context(nc.sbuf_tensor("mu2", [NT, 1], F32))
        varv = es.enter_context(nc.sbuf_tensor("varv", [NT, 1], F32))
        rrv = es.enter_context(nc.sbuf_tensor("rrv", [NT, 1], F32))
        rsv = es.enter_context(nc.sbuf_tensor("rsv", [NT, 1], F32))
        mtv = es.enter_context(nc.sbuf_tensor("mtv", [NT, 1], F32))
        maxacc = es.enter_context(nc.sbuf_tensor("maxacc", [NT, NMAX], F32))
        idf_sb = es.enter_context(nc.sbuf_tensor("idf_sb", [NT, NT], F32))
        parts = [es.enter_context(nc.sbuf_tensor(f"part{k}", [NT, H], F32))
                 for k in range(2)]
        psc = [es.enter_context(nc.psum_tensor(f"psc{k}", [NT, H], F32))
               for k in range(2)]

        t_id = p.dma("mis", lambda e: e.dma_start(
            out=idf_sb[:], in_=idf_c.ap()))
        t_f = p.dma("mis", lambda e: e.dma_start(
            out=ff[:].rearrange("p (d i) -> p d i", i=NT),
            in_=Fz.ap().rearrange("(d j) i -> j d i", j=NT)))
        t_fh = p.op("v", lambda e: e.tensor_copy(fh[:], ff[:]), waits=[t_f])
        t_fl = p.op("v", lambda e: e.tensor_tensor(
            fl[:], ff[:], fh[:], Alu.subtract), waits=[t_fh])
        t_mz = p.op("v", lambda e: e.memset(maxacc[:], 0.0))

        tok_load = [None] * ntiles
        tok_xl = [None] * ntiles
        tok_store_hl = [None] * ntiles
        tok_evac = [None] * ntiles
        tok_store = [None] * ntiles
        tok_statsfree = [None] * ntiles

        def emit_stats(i):
            sl = (i % RING) * H
            xb = xr[:, sl:sl + H]
            w = [tok_load[i]]
            if i >= 1:
                w.append(tok_statsfree[i - 1])
            ta = p.op("a", lambda e, xb=xb: e.activation(
                dump[:], xb, Act.Square, accum_out=sqv[:]), waits=w)
            ta = p.op("a", lambda e, xb=xb: e.activation(
                dump[:], xb, Act.Identity, accum_out=smv[:]), waits=[ta])
            tv = p.op("v", lambda e: e.tensor_scalar(
                muv[:], smv[:], INVH, None, Alu.mult), waits=[ta])
            tv = p.op("v", lambda e: e.tensor_scalar(
                ex2[:], sqv[:], INVH, None, Alu.mult), waits=[tv])
            tok_statsfree[i] = tv
            tv = p.op("v", lambda e: e.tensor_tensor(
                mu2[:], muv[:], muv[:], Alu.mult), waits=[tv])
            tv = p.op("v", lambda e: e.scalar_tensor_tensor(
                varv[:], ex2[:], 1e-5, mu2[:], Alu.add, Alu.subtract),
                waits=[tv])
            ta2 = p.op("a", lambda e: e.activation(
                rrv[:], varv[:], Act.Sqrt), waits=[tv])
            tv = p.op("v", lambda e: e.reciprocal(rsv[:], rrv[:]), waits=[ta2])
            tv = p.op("v", lambda e, xb=xb: e.scalar_tensor_tensor(
                xn[:], xb, muv[:], rsv[:].broadcast_to([NT, H]),
                Alu.subtract, Alu.mult), waits=[tv])
            tv = p.op("v", lambda e: e.tensor_reduce(
                mtv[:], xn[:], AX.X, Alu.max, apply_absolute_value=True),
                waits=[tv])
            tv = p.op("v", lambda e: e.tensor_tensor(
                maxacc[:, 0:1], maxacc[:, 0:1], mtv[:], Alu.max),
                waits=[tv, t_mz])
            wh = [tv, t_fl]
            if i >= RING:
                wh.extend(tok_store_hl[i - RING])
            th = p.op("v", lambda e, sl=sl: e.tensor_copy(
                xhr[:, sl:sl + H], xn[:]), waits=wh)
            tl = p.op("v", lambda e, sl=sl: e.tensor_tensor(
                xlr[:, sl:sl + H], xn[:], xhr[:, sl:sl + H], Alu.subtract),
                waits=[th])
            tok_xl[i] = tl
            t1 = p.dma("sxh", lambda e, sl=sl, i=i: e.dma_start(
                out=xh_d.ap()[i * NT:(i + 1) * NT, :], in_=xhr[:, sl:sl + H]),
                waits=[tl], eng="gp")
            t2 = p.dma("sxl", lambda e, sl=sl, i=i: e.dma_start(
                out=xl_d.ap()[i * NT:(i + 1) * NT, :], in_=xlr[:, sl:sl + H]),
                waits=[tl], eng="gp")
            tok_store_hl[i] = (t1, t2)

        def emit_conv(j):
            ps = psc[j % 2]
            dmax = min(ND - 1, ntiles - 1 - j)
            w = [tok_xl[j + dmax], t_fl]
            if j >= 2:
                w.append(tok_evac[j - 2])
            n_terms = 3 * (dmax + 1)
            cnt = 0
            first = [True, True]
            last = None
            for d in range(dmax + 1):
                sl = ((j + d) % RING) * H
                for (fm, xm) in ((fh, xhr), (fl, xhr), (fh, xlr)):
                    cnt += 1
                    stop = (cnt == n_terms)
                    for half in range(2):
                        cs = slice(half * 512, half * 512 + 512)
                        st = first[half]
                        first[half] = False
                        last = p.op("t", lambda e, ps=ps, fm=fm, xm=xm, d=d,
                                    sl=sl, cs=cs, st=st, stop=stop: e.matmul(
                                        ps[:, cs], fm[:, d * NT:(d + 1) * NT],
                                        xm[:, sl + cs.start:sl + cs.stop],
                                        start=st, stop=stop), waits=w)
                        w = []
            sl = (j % RING) * H
            pb = parts[j % 2]
            we = [last, tok_load[j]]
            if j >= 2:
                we.append(tok_store[j - 2])
            te = p.op("v", lambda e, ps=ps, sl=sl, pb=pb: e.tensor_tensor(
                pb[:], ps[:], xr[:, sl:sl + H], Alu.add), waits=we)
            tok_evac[j] = te
            tok_store[j] = p.dma("spt", lambda e, pb=pb, j=j: e.dma_start(
                out=part_d.ap()[j * NT:(j + 1) * NT, :], in_=pb[:]),
                waits=[te], eng="gp")

        for i in range(ntiles):
            w = []
            if i >= RING:
                w.append(tok_evac[i - RING])
            tok_load[i] = p.dma(f"xi{i % 2}", lambda e, i=i: e.dma_start(
                out=xr[:, (i % RING) * H:(i % RING) * H + H],
                in_=xs.ap()[i * NT:(i + 1) * NT, :]), waits=w)
            emit_stats(i)
            if i >= ND - 1:
                emit_conv(i - (ND - 1))
        for j in range(max(0, ntiles - (ND - 1)), ntiles):
            emit_conv(j)

        max_finalize(nc, es, p, maxacc, idf_sb, "m1loc", [t_id])
        p.barrier_all()
        p.emit_block(nc, es)
    return nc



def build_L2(T):
    ngrp = T // TQ
    ICH = I3 // NT
    nc = bass.Bass("TRN2", target_bir_lowering=False)
    xh_d = nc.dram_tensor("xh", [T, H], BF16, kind="ExternalInput")
    xl_d = nc.dram_tensor("xl", [T, H], BF16, kind="ExternalInput")
    wu_d = nc.dram_tensor("wu", [I3, H], F32, kind="ExternalInput")
    wg_d = nc.dram_tensor("wg", [I3, H], F32, kind="ExternalInput")
    ut_d = nc.dram_tensor("ut", [I3, T], F32, kind="ExternalOutput")
    gt_d = nc.dram_tensor("gt", [I3, T], F32, kind="ExternalOutput")
    idb_c = nc.inline_tensor(np.eye(NT, dtype=BF), name="idb")
    idf_c = nc.inline_tensor(np.eye(NT, dtype=np.float32), name="idf")

    p = Plan()
    es = contextlib.ExitStack()
    with es:
        mxt, mxtok = maxes_prologue(nc, es, p, ["m1all"])
        mx1 = mxt["m1all"]
        idb_sb = es.enter_context(nc.sbuf_tensor("idb_sb", [NT, NT], BF16))
        idf_sb = es.enter_context(nc.sbuf_tensor("idf_sb", [NT, NT], F32))
        s1c = es.enter_context(nc.sbuf_tensor("s1c", [NT, 1], F32))
        c1c = es.enter_context(nc.sbuf_tensor("c1c", [NT, 1], F32))
        wut = es.enter_context(nc.sbuf_tensor("wut", [NT, 8 * I3], BF16))
        wgt = es.enter_context(nc.sbuf_tensor("wgt", [NT, 8 * I3], BF16))
        xhg = [es.enter_context(nc.sbuf_tensor(f"xhg{k}", [NT, 4 * H], BF16))
               for k in range(2)]
        xlg = [es.enter_context(nc.sbuf_tensor(f"xlg{k}", [NT, 4 * H], BF16))
               for k in range(2)]
        xqt = [es.enter_context(nc.sbuf_tensor(f"xqt{k}", [NT, 8 * TQ], BF16))
               for k in range(2)]
        xnb = es.enter_context(nc.sbuf_tensor("xnb", [NT, H], F32))
        t1b = es.enter_context(nc.sbuf_tensor("t1b", [NT, H], F32))
        d1b = es.enter_context(nc.sbuf_tensor("d1b", [NT, H], F32))
        r1i = es.enter_context(nc.sbuf_tensor("r1i", [NT, H], I16))
        r1f = [es.enter_context(nc.sbuf_tensor(f"r1f{k}", [NT, H], BF16))
               for k in range(2)]
        ue = [es.enter_context(nc.sbuf_tensor(f"ue{k}", [NT, TQ], F32))
              for k in range(2)]
        ge = [es.enter_context(nc.sbuf_tensor(f"ge{k}", [NT, TQ], F32))
              for k in range(2)]
        urd = es.enter_context(nc.sbuf_tensor("urd", [NT, 1], F32))
        grd = es.enter_context(nc.sbuf_tensor("grd", [NT, 1], F32))
        maxacc = es.enter_context(nc.sbuf_tensor("maxacc", [NT, NMAX], F32))
        psu = [es.enter_context(nc.psum_tensor(f"pu{k}", [NT, TQ], F32))
               for k in range(2)]
        psg = [es.enter_context(nc.psum_tensor(f"pg{k}", [NT, TQ], F32))
               for k in range(2)]
        psx = [es.enter_context(nc.psum_tensor(f"px{k}", [NT, NT], BF16))
               for k in range(2)]

        t_id = p.dma("mis", lambda e: e.dma_start(
            out=idb_sb[:], in_=idb_c.ap()))
        t_idf = p.dma("mis", lambda e: e.dma_start(
            out=idf_sb[:], in_=idf_c.ap()))
        t_mz = p.op("v", lambda e: e.memset(maxacc[:], 0.0))
        tv = p.op("v", lambda e: e.tensor_scalar(
            s1c[:], mx1[:, 0:1], INVQ, EPS12, Alu.mult, Alu.max),
            waits=[mxtok["m1all"]])
        t_c1 = p.op("v", lambda e: e.reciprocal(c1c[:], s1c[:]), waits=[tv])

        tok_w = dequant_weights(
            nc, es, p, [(wu_d, I3, H, wut), (wg_d, I3, H, wgt)], idb_sb, "w",
            psw=psx)

        tok_mm = {"u": [[None] * ICH for _ in range(ngrp)],
                  "g": [[None] * ICH for _ in range(ngrp)]}
        tok_ev = {"u": [[None] * ICH for _ in range(ngrp)],
                  "g": [[None] * ICH for _ in range(ngrp)]}
        tok_st = {"u": [[None] * ICH for _ in range(ngrp)],
                  "g": [[None] * ICH for _ in range(ngrp)]}
        tok_rd = {"u": [[None] * ICH for _ in range(ngrp)],
                  "g": [[None] * ICH for _ in range(ngrp)]}
        tok_rf = [None, None]
        tok_xqt = [None] * ngrp
        last_xev = tok_w

        for q in range(ngrp):
            w = []
            if q >= 2:
                w += [tok_mm["u"][q - 2][ICH - 1], tok_mm["g"][q - 2][ICH - 1]]
            tl = p.dma(f"xi{q % 2}", lambda e, q=q: e.dma_start(
                out=xhg[q % 2][:].rearrange("p (k h) -> p k h", h=H),
                in_=xh_d.ap()[q * TQ:(q + 1) * TQ, :].rearrange(
                    "(k p) h -> p k h", p=NT)), waits=w)
            tl = p.dma(f"xi{q % 2}", lambda e, q=q: e.dma_start(
                out=xlg[q % 2][:].rearrange("p (k h) -> p k h", h=H),
                in_=xl_d.ap()[q * TQ:(q + 1) * TQ, :].rearrange(
                    "(k p) h -> p k h", p=NT)), waits=[tl])
            lastev = None
            for k in range(4):
                cs = slice(k * H, (k + 1) * H)
                tv = p.op("v", lambda e, q=q, cs=cs: e.tensor_tensor(
                    xnb[:], xhg[q % 2][:, cs], xlg[q % 2][:, cs], Alu.add),
                    waits=[tl, t_c1])
                tv = p.op("v", lambda e: e.scalar_tensor_tensor(
                    t1b[:], xnb[:], 1.0, c1c[:].broadcast_to([NT, H]),
                    Alu.bypass, Alu.mult), waits=[tv])
                rf = r1f[k % 2]
                if tok_rf[k % 2] is not None:
                    tv = p.op("v", lambda e: e.engine_nop(),
                              waits=[tv, tok_rf[k % 2]])
                tv = plan_round_signed(
                    p, tv, t1b[:], r1i[:], d1b[:],
                    lambda e, d, s, rf=rf: e.scalar_tensor_tensor(
                        rf[:], d, -0.5, s, Alu.is_le, Alu.subtract))
                for ci in range(8):
                    ps = psx[ci % 2]
                    wt = [tv, t_id, last_xev]
                    if q >= 2 and k == 0 and ci == 0:
                        wt += [tok_mm["u"][q - 2][ICH - 1],
                               tok_mm["g"][q - 2][ICH - 1]]
                    ttr = p.op("t", lambda e, ps=ps, rf=rf, ci=ci: e.transpose(
                        ps[:], rf[:, ci * NT:(ci + 1) * NT], idb_sb[:]),
                        waits=wt)
                    lastev = p.op("a", lambda e, ps=ps, q=q, k=k, ci=ci:
                                  e.activation(
                                      xqt[q % 2][:, ci * TQ + k * NT:
                                                 ci * TQ + (k + 1) * NT],
                                      ps[:], Act.Identity), waits=[ttr])
                    last_xev = lastev
                tok_rf[k % 2] = lastev
            tok_xqt[q] = lastev
            for ii in range(ICH):
                for key, wt_sb, psv, ev, rd, dst, col in (
                    ("u", wut, psu, ue, urd, ut_d, 0),
                    ("g", wgt, psg, ge, grd, gt_d, 1),
                ):
                    ps = psv[ii % 2]
                    wm = [tok_xqt[q], tok_w]
                    if ii >= 2:
                        wm.append(tok_ev[key][q][ii - 2])
                    elif q >= 1:
                        wm.append(tok_ev[key][q - 1][ICH - 2 + ii])
                    last = None
                    for ci in range(8):
                        last = p.op("t", lambda e, ps=ps, wt_sb=wt_sb, ci=ci,
                                    ii=ii, q=q: e.matmul(
                                        ps[:],
                                        wt_sb[:, ci * I3 + ii * NT:
                                              ci * I3 + (ii + 1) * NT],
                                        xqt[q % 2][:, ci * TQ:(ci + 1) * TQ],
                                        start=(ci == 0), stop=(ci == 7)),
                                    waits=wm)
                        wm = []
                    tok_mm[key][q][ii] = last
                    eb = ev[ii % 2]
                    we = [last]
                    if ii >= 2:
                        we.append(tok_st[key][q][ii - 2])
                        we.append(tok_rd[key][q][ii - 2])
                    elif q >= 1:
                        we.append(tok_st[key][q - 1][ICH - 2 + ii])
                        we.append(tok_rd[key][q - 1][ICH - 2 + ii])
                    tev = p.op("a", lambda e, ps=ps, eb=eb: e.activation(
                        eb[:], ps[:], Act.Identity), waits=we)
                    tok_ev[key][q][ii] = tev
                    tv = p.op("v", lambda e, eb=eb, rd=rd: e.tensor_reduce(
                        rd[:], eb[:], AX.X, Alu.max,
                        apply_absolute_value=True), waits=[tev])
                    tok_rd[key][q][ii] = tv
                    p.op("v", lambda e, rd=rd, col=col: e.tensor_tensor(
                        maxacc[:, col:col + 1], maxacc[:, col:col + 1], rd[:],
                        Alu.max), waits=[tv, t_mz])
                    tok_st[key][q][ii] = p.dma(
                        "s" + key, lambda e, eb=eb, ii=ii, q=q, dst=dst:
                        e.dma_start(out=dst.ap()[ii * NT:(ii + 1) * NT,
                                                 q * TQ:(q + 1) * TQ],
                                    in_=eb[:]), waits=[tev], eng="gp")

        max_finalize(nc, es, p, maxacc, idf_sb, "m23loc", [t_idf])
        p.barrier_all()
        p.emit_block(nc, es)
    return nc



def build_L3(T):
    ch = min(CH, T)
    nc = bass.Bass("TRN2", target_bir_lowering=False)
    gt_d = nc.dram_tensor("gt", [I3, T], F32, kind="ExternalInput")
    P_d = nc.dram_tensor("P", [I3, T], F32, kind="ExternalOutput")
    idf_c = nc.inline_tensor(np.eye(NT, dtype=np.float32), name="idf")

    p = Plan()
    es = contextlib.ExitStack()
    with es:
        mxt, mxtok = maxes_prologue(nc, es, p, ["m1all", "m23all"])
        mx1, mx23 = mxt["m1all"], mxt["m23all"]
        idf_sb = es.enter_context(nc.sbuf_tensor("idf_sb", [NT, NT], F32))
        s1c = es.enter_context(nc.sbuf_tensor("s1c", [NT, 1], F32))
        s3c = es.enter_context(nc.sbuf_tensor("s3c", [NT, 1], F32))
        c3c = es.enter_context(nc.sbuf_tensor("c3c", [NT, 1], F32))
        tmpc = es.enter_context(nc.sbuf_tensor("tmpc", [NT, 1], F32))
        gb = [es.enter_context(nc.sbuf_tensor(f"gb{k}", [NT, ch], F32))
              for k in range(2)]
        t3b = [es.enter_context(nc.sbuf_tensor(f"t3b{k}", [NT, ch], F32))
               for k in range(2)]
        d3b = [es.enter_context(nc.sbuf_tensor(f"d3b{k}", [NT, ch], F32))
               for k in range(2)]
        r3i = [es.enter_context(nc.sbuf_tensor(f"r3i{k}", [NT, ch], I16))
               for k in range(2)]
        r3b = [es.enter_context(nc.sbuf_tensor(f"r3b{k}", [NT, ch], F32))
               for k in range(2)]
        sgb = [es.enter_context(nc.sbuf_tensor(f"sgb{k}", [NT, ch], F32))
               for k in range(2)]
        t4b = [es.enter_context(nc.sbuf_tensor(f"t4b{k}", [NT, ch], F32))
               for k in range(2)]
        d4b = [es.enter_context(nc.sbuf_tensor(f"d4b{k}", [NT, ch], F32))
               for k in range(2)]
        r4u = [es.enter_context(nc.sbuf_tensor(f"r4u{k}", [NT, ch], U16))
               for k in range(2)]
        r4b = [es.enter_context(nc.sbuf_tensor(f"r4b{k}", [NT, ch], F32))
               for k in range(2)]
        Pb = [es.enter_context(nc.sbuf_tensor(f"Pb{k}", [NT, ch], F32))
              for k in range(2)]
        rdb = es.enter_context(nc.sbuf_tensor("rdb", [NT, 1], F32))
        maxacc = es.enter_context(nc.sbuf_tensor("maxacc", [NT, NMAX], F32))

        t_idf = p.dma("mis", lambda e: e.dma_start(
            out=idf_sb[:], in_=idf_c.ap()))
        t_mz = p.op("v", lambda e: e.memset(maxacc[:], 0.0))
        wv = [mxtok["m1all"], mxtok["m23all"]]
        tv = p.op("v", lambda e: e.tensor_scalar(
            s1c[:], mx1[:, 0:1], INVQ, EPS12, Alu.mult, Alu.max), waits=wv)
        tv = p.op("v", lambda e: e.tensor_tensor(
            tmpc[:], s1c[:], mx23[:, 1:2], Alu.mult), waits=[tv])
        t_s3 = p.op("v", lambda e: e.tensor_scalar(
            s3c[:], tmpc[:], INVQ, EPS12, Alu.mult, Alu.max), waits=[tv])
        tv = p.op("v", lambda e: e.reciprocal(c3c[:], mx23[:, 1:2]),
                  waits=[t_s3])
        t_c3 = p.op("v", lambda e: e.tensor_scalar(
            c3c[:], c3c[:], Q15, None, Alu.mult), waits=[tv])

        nchunk = (I3 // NT) * (T // ch)
        ncol = T // ch
        tok = {k: [None] * nchunk for k in
               ("ld", "t3", "c3", "d3", "su3", "r3", "sg", "t4", "c4", "d4",
                "su4", "min", "P", "red", "st")}

        def rc(c):
            ii, qq = divmod(c, ncol)
            return slice(ii * NT, (ii + 1) * NT), slice(qq * ch, (qq + 1) * ch)

        def tk(k, c):
            return tok[k][c] if c >= 0 else None

        def stage1(c):
            rows, cols = rc(c)
            b = c % 2
            tok["ld"][c] = p.dma(f"in{c % 2}", lambda e, c=c, rows=rows,
                                 cols=cols: e.dma_start(
                out=gb[c % 2][:], in_=gt_d.ap()[rows, cols]),
                waits=[tk("t3", c - 2)])
            tok["t3"][c] = p.op("a", lambda e, c=c, b=b: e.activation(
                t3b[b][:], gb[c % 2][:], Act.Identity, bias=0.0, scale=c3c[:]),
                waits=[tok["ld"][c], t_c3, tk("r3", c - 2)])
            tok["c3"][c] = p.op("a", lambda e, b=b: e.activation(
                r3i[b][:], t3b[b][:], Act.Identity),
                waits=[tok["t3"][c], tk("su3", c - 2)])
            tok["d3"][c] = p.op("gp", lambda e, b=b: e.tensor_tensor(
                d3b[b][:], t3b[b][:], r3i[b][:], Alu.subtract),
                waits=[tok["c3"][c], tk("r3", c - 2)])
            tok["su3"][c] = p.op("v", lambda e, b=b: e.scalar_tensor_tensor(
                t3b[b][:], d3b[b][:], 0.5, r3i[b][:], Alu.is_ge, Alu.add),
                waits=[tok["d3"][c]])
            tok["r3"][c] = p.op("v", lambda e, b=b: e.scalar_tensor_tensor(
                r3b[b][:], d3b[b][:], -0.5, t3b[b][:], Alu.is_le,
                Alu.subtract), waits=[tok["su3"][c], tk("P", c - 2)])

        def stage2(c):
            rows, cols = rc(c)
            b = c % 2
            tok["sg"][c] = p.op("a", lambda e, b=b: e.activation(
                sgb[b][:], r3b[b][:], Act.Sigmoid, bias=0.0, scale=s3c[:]),
                waits=[tok["r3"][c], t_s3, tk("t4", c - 2)])
            tok["t4"][c] = p.op("a", lambda e, b=b: e.activation(
                t4b[b][:], sgb[b][:], Act.Identity, bias=0.0, scale=65536.0),
                waits=[tok["sg"][c], tk("min", c - 2)])
            tok["c4"][c] = p.op("a", lambda e, b=b: e.activation(
                r4u[b][:], t4b[b][:], Act.Identity),
                waits=[tok["t4"][c], tk("su4", c - 2)])
            tok["d4"][c] = p.op("gp", lambda e, b=b: e.tensor_tensor(
                d4b[b][:], t4b[b][:], r4u[b][:], Alu.subtract),
                waits=[tok["c4"][c], tk("su4", c - 2)])
            tok["su4"][c] = p.op("v", lambda e, b=b: e.scalar_tensor_tensor(
                t4b[b][:], d4b[b][:], 0.5, r4u[b][:], Alu.is_ge, Alu.add),
                waits=[tok["d4"][c]])
            tok["min"][c] = p.op("gp", lambda e, b=b: e.tensor_scalar(
                r4b[b][:], t4b[b][:], 65535.0, None, Alu.min),
                waits=[tok["su4"][c], tk("P", c - 2)])
            tok["P"][c] = p.op("v", lambda e, c=c, b=b: e.tensor_tensor(
                Pb[c % 2][:], r3b[b][:], r4b[b][:], Alu.mult),
                waits=[tok["min"][c], tk("st", c - 2), tk("red", c - 2)])
            tv = p.op("v", lambda e, c=c: e.tensor_reduce(
                rdb[:], Pb[c % 2][:], AX.X, Alu.max,
                apply_absolute_value=True), waits=[tok["P"][c]])
            tok["red"][c] = p.op("v", lambda e: e.tensor_tensor(
                maxacc[:, 0:1], maxacc[:, 0:1], rdb[:], Alu.max),
                waits=[tv, t_mz])
            tok["st"][c] = p.dma(f"sP{c % 2}", lambda e, c=c, rows=rows,
                                 cols=cols: e.dma_start(
                out=P_d.ap()[rows, cols], in_=Pb[c % 2][:]),
                waits=[tok["P"][c]], eng="a")

        for c in range(nchunk):
            stage1(c)
            if c >= 1:
                stage2(c - 1)
        stage2(nchunk - 1)

        max_finalize(nc, es, p, maxacc, idf_sb, "mPloc", [t_idf])
        p.barrier_all()
        p.emit_block(nc, es)
    return nc



def build_L4(T):
    ch = min(CH, T)
    nc = bass.Bass("TRN2", target_bir_lowering=False)
    ut_d = nc.dram_tensor("ut", [I3, T], F32, kind="ExternalInput")
    P_d = nc.dram_tensor("P", [I3, T], F32, kind="ExternalInput")
    P2_d = nc.dram_tensor("P2", [I3, T], F32, kind="ExternalOutput")
    idf_c = nc.inline_tensor(np.eye(NT, dtype=np.float32), name="idf")

    p = Plan()
    es = contextlib.ExitStack()
    with es:
        mxt, mxtok = maxes_prologue(nc, es, p, ["m23all", "mPall"])
        mx23, mxP = mxt["m23all"], mxt["mPall"]
        idf_sb = es.enter_context(nc.sbuf_tensor("idf_sb", [NT, NT], F32))
        c2c = es.enter_context(nc.sbuf_tensor("c2c", [NT, 1], F32))
        c5c = es.enter_context(nc.sbuf_tensor("c5c", [NT, 1], F32))
        ub = [es.enter_context(nc.sbuf_tensor(f"ub{k}", [NT, ch], F32))
              for k in range(2)]
        Pb = [es.enter_context(nc.sbuf_tensor(f"Pb{k}", [NT, ch], F32))
              for k in range(2)]
        t2b = [es.enter_context(nc.sbuf_tensor(f"t2b{k}", [NT, ch], F32))
               for k in range(2)]
        d2b = [es.enter_context(nc.sbuf_tensor(f"d2b{k}", [NT, ch], F32))
               for k in range(2)]
        r2i = [es.enter_context(nc.sbuf_tensor(f"r2i{k}", [NT, ch], I16))
               for k in range(2)]
        r2b = [es.enter_context(nc.sbuf_tensor(f"r2b{k}", [NT, ch], F32))
               for k in range(2)]
        t5b = [es.enter_context(nc.sbuf_tensor(f"t5b{k}", [NT, ch], F32))
               for k in range(2)]
        d5b = [es.enter_context(nc.sbuf_tensor(f"d5b{k}", [NT, ch], F32))
               for k in range(2)]
        r5i = [es.enter_context(nc.sbuf_tensor(f"r5i{k}", [NT, ch], I16))
               for k in range(2)]
        r5b = [es.enter_context(nc.sbuf_tensor(f"r5b{k}", [NT, ch], F32))
               for k in range(2)]
        P2b = [es.enter_context(nc.sbuf_tensor(f"P2b{k}", [NT, ch], F32))
               for k in range(2)]
        rdb = es.enter_context(nc.sbuf_tensor("rdb", [NT, 1], F32))
        maxacc = es.enter_context(nc.sbuf_tensor("maxacc", [NT, NMAX], F32))

        t_idf = p.dma("mis", lambda e: e.dma_start(
            out=idf_sb[:], in_=idf_c.ap()))
        t_mz = p.op("v", lambda e: e.memset(maxacc[:], 0.0))
        wv = [mxtok["m23all"], mxtok["mPall"]]
        tv = p.op("v", lambda e: e.reciprocal(c2c[:], mx23[:, 0:1]), waits=wv)
        t_c2 = p.op("v", lambda e: e.tensor_scalar(
            c2c[:], c2c[:], Q15, None, Alu.mult), waits=[tv])
        tv = p.op("v", lambda e: e.reciprocal(c5c[:], mxP[:, 0:1]),
                  waits=[t_c2])
        t_c5 = p.op("v", lambda e: e.tensor_scalar(
            c5c[:], c5c[:], Q15, None, Alu.mult), waits=[tv])

        nchunk = (I3 // NT) * (T // ch)
        ncol = T // ch
        tok = {k: [None] * nchunk for k in
               ("ldu", "ldp", "t2", "c2", "d2", "su2", "r2", "t5", "c5",
                "d5", "su5", "r5", "P2", "red", "st")}

        def rc(c):
            ii, qq = divmod(c, ncol)
            return slice(ii * NT, (ii + 1) * NT), slice(qq * ch, (qq + 1) * ch)

        def tk(k, c):
            return tok[k][c] if c >= 0 else None

        def stage1(c):
            rows, cols = rc(c)
            tok["ldu"][c] = p.dma(f"inu{c % 2}", lambda e, c=c, rows=rows,
                                  cols=cols: e.dma_start(
                out=ub[c % 2][:], in_=ut_d.ap()[rows, cols]),
                waits=[tk("t2", c - 2)])
            tok["ldp"][c] = p.dma(f"inp{c % 2}", lambda e, c=c, rows=rows,
                                  cols=cols: e.dma_start(
                out=Pb[c % 2][:], in_=P_d.ap()[rows, cols]),
                waits=[tk("t5", c - 2)])
            b = c % 2
            tok["t2"][c] = p.op("a", lambda e, c=c, b=b: e.activation(
                t2b[b][:], ub[c % 2][:], Act.Identity, bias=0.0, scale=c2c[:]),
                waits=[tok["ldu"][c], t_c5, tk("r2", c - 2)])
            tok["c2"][c] = p.op("a", lambda e, b=b: e.activation(
                r2i[b][:], t2b[b][:], Act.Identity),
                waits=[tok["t2"][c], tk("su2", c - 2)])
            tok["d2"][c] = p.op("gp", lambda e, b=b: e.tensor_tensor(
                d2b[b][:], t2b[b][:], r2i[b][:], Alu.subtract),
                waits=[tok["c2"][c], tk("r2", c - 2)])
            tok["su2"][c] = p.op("v", lambda e, b=b: e.scalar_tensor_tensor(
                t2b[b][:], d2b[b][:], 0.5, r2i[b][:], Alu.is_ge, Alu.add),
                waits=[tok["d2"][c]])
            tok["r2"][c] = p.op("v", lambda e, b=b: e.scalar_tensor_tensor(
                r2b[b][:], d2b[b][:], -0.5, t2b[b][:], Alu.is_le,
                Alu.subtract), waits=[tok["su2"][c], tk("P2", c - 2)])

        def stage2(c):
            rows, cols = rc(c)
            b = c % 2
            tok["t5"][c] = p.op("a", lambda e, c=c, b=b: e.activation(
                t5b[b][:], Pb[c % 2][:], Act.Identity, bias=0.0, scale=c5c[:]),
                waits=[tok["ldp"][c], t_c5, tk("r5", c - 2)])
            tok["c5"][c] = p.op("a", lambda e, b=b: e.activation(
                r5i[b][:], t5b[b][:], Act.Identity),
                waits=[tok["t5"][c], tk("su5", c - 2)])
            tok["d5"][c] = p.op("gp", lambda e, b=b: e.tensor_tensor(
                d5b[b][:], t5b[b][:], r5i[b][:], Alu.subtract),
                waits=[tok["c5"][c], tk("r5", c - 2)])
            tok["su5"][c] = p.op("v", lambda e, b=b: e.scalar_tensor_tensor(
                t5b[b][:], d5b[b][:], 0.5, r5i[b][:], Alu.is_ge, Alu.add),
                waits=[tok["d5"][c]])
            tok["r5"][c] = p.op("v", lambda e, b=b: e.scalar_tensor_tensor(
                r5b[b][:], d5b[b][:], -0.5, t5b[b][:], Alu.is_le,
                Alu.subtract), waits=[tok["su5"][c], tk("P2", c - 2)])
            tok["P2"][c] = p.op("v", lambda e, c=c, b=b: e.tensor_tensor(
                P2b[c % 2][:], r5b[b][:], r2b[b][:], Alu.mult),
                waits=[tok["r5"][c], tok["r2"][c], tk("st", c - 2),
                       tk("red", c - 2)])
            tv = p.op("v", lambda e, c=c: e.tensor_reduce(
                rdb[:], P2b[c % 2][:], AX.X, Alu.max,
                apply_absolute_value=True), waits=[tok["P2"][c]])
            tok["red"][c] = p.op("v", lambda e: e.tensor_tensor(
                maxacc[:, 0:1], maxacc[:, 0:1], rdb[:], Alu.max),
                waits=[tv, t_mz])
            tok["st"][c] = p.dma(f"sP{c % 2}", lambda e, c=c, rows=rows,
                                 cols=cols: e.dma_start(
                out=P2_d.ap()[rows, cols], in_=P2b[c % 2][:]),
                waits=[tok["P2"][c]], eng="a")

        for c in range(nchunk):
            stage1(c)
            if c >= 1:
                stage2(c - 1)
        stage2(nchunk - 1)

        max_finalize(nc, es, p, maxacc, idf_sb, "mP2loc", [t_idf])
        p.barrier_all()
        p.emit_block(nc, es)
    return nc



def build_L5(T):
    ngrp = T // TQ
    ICH = I3 // NT
    nc = bass.Bass("TRN2", target_bir_lowering=False)
    P2_d = nc.dram_tensor("P2", [I3, T], F32, kind="ExternalInput")
    wd_d = nc.dram_tensor("wd", [H, I3], F32, kind="ExternalInput")
    part_d = nc.dram_tensor("partial", [T, H], F32, kind="ExternalInput")
    out_d = nc.dram_tensor("out", [T, H], F32, kind="ExternalOutput")
    idb_c = nc.inline_tensor(np.eye(NT, dtype=BF), name="idb")
    idf_c = nc.inline_tensor(np.eye(NT, dtype=np.float32), name="idf")

    p = Plan()
    es = contextlib.ExitStack()
    with es:
        mxt, mxtok = maxes_prologue(
            nc, es, p, ["m1all", "m23all", "mPall", "mP2all"])
        mx1, mx23 = mxt["m1all"], mxt["m23all"]
        mxP, mxP2 = mxt["mPall"], mxt["mP2all"]
        idb_sb = es.enter_context(nc.sbuf_tensor("idb_sb", [NT, NT], BF16))
        idf_sb = es.enter_context(nc.sbuf_tensor("idf_sb", [NT, NT], F32))
        s1c = es.enter_context(nc.sbuf_tensor("s1c", [NT, 1], F32))
        s2c = es.enter_context(nc.sbuf_tensor("s2c", [NT, 1], F32))
        s3c = es.enter_context(nc.sbuf_tensor("s3c", [NT, 1], F32))
        s4c = es.enter_context(nc.sbuf_tensor("s4c", [NT, 1], F32))
        s6c = es.enter_context(nc.sbuf_tensor("s6c", [NT, 1], F32))
        c6c = es.enter_context(nc.sbuf_tensor("c6c", [NT, 1], F32))
        tmpc = es.enter_context(nc.sbuf_tensor("tmpc", [NT, 1], F32))
        wdt = es.enter_context(nc.sbuf_tensor("wdt", [NT, ICH * H], BF16))
        P2b = [es.enter_context(nc.sbuf_tensor(f"P2b{k}", [NT, TQ], F32))
               for k in range(2)]
        t6b = es.enter_context(nc.sbuf_tensor("t6b", [NT, TQ], F32))
        d6b = es.enter_context(nc.sbuf_tensor("d6b", [NT, TQ], F32))
        r6i = es.enter_context(nc.sbuf_tensor("r6i", [NT, TQ], I16))
        r6g = [es.enter_context(nc.sbuf_tensor(f"r6g{k}", [NT, ICH * TQ], BF16))
               for k in range(2)]
        dch = [es.enter_context(nc.sbuf_tensor(f"dch{k}", [NT, 8 * TQ], F32))
               for k in range(2)]
        prt = [es.enter_context(nc.sbuf_tensor(f"prt{k}", [NT, H], F32))
               for k in range(2)]
        osb = [es.enter_context(nc.sbuf_tensor(f"osb{k}", [NT, H], F32))
               for k in range(2)]
        fsb = [es.enter_context(nc.sbuf_tensor(f"fsb{k}", [NT, H], F32))
               for k in range(2)]
        psd = [es.enter_context(nc.psum_tensor(f"pd{k}", [NT, TQ], F32))
               for k in range(2)]
        pst = [es.enter_context(nc.psum_tensor(f"pt{k}", [NT, NT], F32))
               for k in range(2)]

        t_id = p.dma("mis", lambda e: e.dma_start(
            out=idb_sb[:], in_=idb_c.ap()))
        t_idf = p.dma("mis", lambda e: e.dma_start(
            out=idf_sb[:], in_=idf_c.ap()))
        wv = [mxtok[n] for n in ("m1all", "m23all", "mPall", "mP2all")]
        tv = p.op("v", lambda e: e.tensor_scalar(
            s1c[:], mx1[:, 0:1], INVQ, EPS12, Alu.mult, Alu.max), waits=wv)
        tv = p.op("v", lambda e: e.tensor_tensor(
            tmpc[:], s1c[:], mx23[:, 0:1], Alu.mult), waits=[tv])
        tv = p.op("v", lambda e: e.tensor_scalar(
            s2c[:], tmpc[:], INVQ, EPS12, Alu.mult, Alu.max), waits=[tv])
        tv = p.op("v", lambda e: e.tensor_tensor(
            tmpc[:], s1c[:], mx23[:, 1:2], Alu.mult), waits=[tv])
        tv = p.op("v", lambda e: e.tensor_scalar(
            s3c[:], tmpc[:], INVQ, EPS12, Alu.mult, Alu.max), waits=[tv])
        tv = p.op("v", lambda e: e.tensor_tensor(
            tmpc[:], s3c[:], mxP[:, 0:1], Alu.mult), waits=[tv])
        tv = p.op("v", lambda e: e.tensor_scalar(
            tmpc[:], tmpc[:], INV65536, None, Alu.mult), waits=[tv])
        tv = p.op("v", lambda e: e.tensor_scalar(
            s4c[:], tmpc[:], INVQ, EPS12, Alu.mult, Alu.max), waits=[tv])
        tv = p.op("v", lambda e: e.tensor_tensor(
            tmpc[:], s4c[:], s2c[:], Alu.mult), waits=[tv])
        tv = p.op("v", lambda e: e.tensor_tensor(
            tmpc[:], tmpc[:], mxP2[:, 0:1], Alu.mult), waits=[tv])
        t_s6 = p.op("v", lambda e: e.tensor_scalar(
            s6c[:], tmpc[:], INVQ, EPS12, Alu.mult, Alu.max), waits=[tv])
        tv = p.op("v", lambda e: e.reciprocal(c6c[:], mxP2[:, 0:1]),
                  waits=[t_s6])
        t_c6 = p.op("v", lambda e: e.tensor_scalar(
            c6c[:], c6c[:], Q15, None, Alu.mult), waits=[tv])

        tok_wd = dequant_weights(nc, es, p, [(wd_d, H, I3, wdt)], idb_sb, "d")

        tok_mm = [[None] * 8 for _ in range(ngrp)]
        tok_dch = [[None] * 8 for _ in range(ngrp)]
        tok_out = [[None] * 4 for _ in range(ngrp)]
        tok_stf = [[None] * 4 for _ in range(ngrp)]
        tok_lasttr = [None] * ngrp
        tok_p2read = [None, None]

        for q in range(ngrp):
            lastv = None
            prev_chain = None
            for ii in range(ICH):
                par = ii % 2
                w = [t_c6, tok_p2read[par]]
                if q >= 2 and ii == 0:
                    w.append(tok_mm[q - 2][7])
                tl = p.dma(f"inp{par}", lambda e, q=q, ii=ii, par=par:
                           e.dma_start(
                               out=P2b[par][:],
                               in_=P2_d.ap()[ii * NT:(ii + 1) * NT,
                                             q * TQ:(q + 1) * TQ]), waits=w)
                tv = p.op("v", lambda e, par=par: e.scalar_tensor_tensor(
                    t6b[:], P2b[par][:], 1.0, c6c[:].broadcast_to([NT, TQ]),
                    Alu.bypass, Alu.mult), waits=[tl, prev_chain])
                tok_p2read[par] = tv
                if q >= 2 and ii == 0:
                    tv = p.op("v", lambda e: e.engine_nop(),
                              waits=[tv, tok_mm[q - 2][7]])
                lastv = plan_round_signed(
                    p, tv, t6b[:], r6i[:], d6b[:],
                    lambda e, d, s, q=q, ii=ii: e.scalar_tensor_tensor(
                        r6g[q % 2][:, ii * TQ:(ii + 1) * TQ], d, -0.5, s,
                        Alu.is_le, Alu.subtract))
                prev_chain = lastv
            tok_r6 = lastv
            for hh in range(8):
                ps = psd[hh % 2]
                wm = [tok_r6, tok_wd]
                if hh >= 2:
                    wm.append(tok_dch[q][hh - 2])
                elif q >= 1:
                    wm.append(tok_dch[q - 1][6 + hh])
                last = None
                for ii in range(ICH):
                    last = p.op("t", lambda e, ps=ps, hh=hh, ii=ii, q=q:
                                e.matmul(
                                    ps[:],
                                    wdt[:, ii * H + hh * NT:
                                        ii * H + (hh + 1) * NT],
                                    r6g[q % 2][:, ii * TQ:(ii + 1) * TQ],
                                    start=(ii == 0), stop=(ii == ICH - 1)),
                                waits=wm)
                    wm = []
                tok_mm[q][hh] = last
                we = [last, t_s6]
                if q >= 2 and hh == 0:
                    we.append(tok_lasttr[q - 2])
                tok_dch[q][hh] = p.op(
                    "a", lambda e, ps=ps, q=q, hh=hh: e.activation(
                        dch[q % 2][:, hh * TQ:(hh + 1) * TQ], ps[:],
                        Act.Identity, bias=0.0, scale=s6c[:]), waits=we)
            prev_lasta = None
            for tt in range(4):
                wl = []
                if tt >= 2:
                    wl.append(tok_stf[q][tt - 2])
                elif q >= 1:
                    wl.append(tok_stf[q - 1][2 + tt])
                tok_ld = p.dma(f"inq{tt % 2}", lambda e, q=q, tt=tt:
                               e.dma_start(
                    out=prt[tt % 2][:],
                    in_=part_d.ap()[q * TQ + tt * NT:q * TQ + (tt + 1) * NT, :]),
                    waits=wl)
                lasta = None
                for hh in range(8):
                    pt_ = pst[hh % 2]
                    wt = [tok_dch[q][hh], t_idf, prev_lasta]
                    if hh >= 2:
                        wt.append(lasta)
                    if hh == 0:
                        if tt >= 2:
                            wt.append(tok_out[q][tt - 2])
                        elif q >= 1:
                            wt.append(tok_out[q - 1][2 + tt])
                    ttr = p.op("t", lambda e, pt_=pt_, q=q, hh=hh, tt=tt:
                               e.transpose(
                                   pt_[:],
                                   dch[q % 2][:, hh * TQ + tt * NT:
                                              hh * TQ + (tt + 1) * NT],
                                   idf_sb[:]), waits=wt)
                    lasta = p.op("a", lambda e, pt_=pt_, tt=tt, hh=hh:
                                 e.activation(
                                     osb[tt % 2][:, hh * NT:(hh + 1) * NT],
                                     pt_[:], Act.Identity), waits=[ttr])
                prev_lasta = lasta
                if tt == 3:
                    tok_lasttr[q] = lasta
                wf = [lasta, tok_ld]
                if tt >= 2:
                    wf.append(tok_stf[q][tt - 2])
                elif q >= 1:
                    wf.append(tok_stf[q - 1][2 + tt])
                tf = p.op("v", lambda e, tt=tt: e.tensor_tensor(
                    fsb[tt % 2][:], osb[tt % 2][:], prt[tt % 2][:], Alu.add),
                    waits=wf)
                tok_out[q][tt] = tf
                tok_stf[q][tt] = p.dma(f"sou{tt % 2}", lambda e, q=q, tt=tt:
                                       e.dma_start(
                    out=out_d.ap()[q * TQ + tt * NT:q * TQ + (tt + 1) * NT, :],
                    in_=fsb[tt % 2][:]), waits=[tf], eng="gp")

        p.barrier_all()
        p.emit_block(nc, es)
    return nc



def _gather_max(results, key):
    m = np.stack([np.asarray(r[key], np.float32).reshape(NMAX)
                  for r in results])
    return np.ascontiguousarray(m.reshape(1, 8 * NMAX))


def kernel(L0_attn_x, L0_attn_filt, L0_ffn_w_gate, L0_ffn_w_up, L0_ffn_w_down):
    x = np.ascontiguousarray(np.asarray(L0_attn_x, np.float32))
    filt = np.ascontiguousarray(np.asarray(L0_attn_filt, np.float32))
    wg = np.ascontiguousarray(np.asarray(L0_ffn_w_gate, np.float32))
    wu = np.ascontiguousarray(np.asarray(L0_ffn_w_up, np.float32))
    wd = np.ascontiguousarray(np.asarray(L0_ffn_w_down, np.float32))
    B, S, _ = x.shape
    T = S
    cores = list(range(NCORES))
    Fz = build_toeplitz(filt)

    r1 = run_bass_kernel_spmd(
        build_L1(T), [{"xs": x[c], "Fz": Fz} for c in range(NCORES)], cores)
    m1all = _gather_max(r1.results, "m1loc")

    r2 = run_bass_kernel_spmd(
        build_L2(T),
        [{"xh": np.asarray(r["xh"]), "xl": np.asarray(r["xl"]),
          "wu": wu, "wg": wg, "m1all": m1all} for r in r1.results], cores)
    m23all = _gather_max(r2.results, "m23loc")

    r3 = run_bass_kernel_spmd(
        build_L3(T),
        [{"gt": np.asarray(r["gt"]), "m1all": m1all, "m23all": m23all}
         for r in r2.results], cores)
    mPall = _gather_max(r3.results, "mPloc")

    r4 = run_bass_kernel_spmd(
        build_L4(T),
        [{"ut": np.asarray(r2.results[c]["ut"]),
          "P": np.asarray(r3.results[c]["P"]),
          "m23all": m23all, "mPall": mPall} for c in range(NCORES)], cores)
    mP2all = _gather_max(r4.results, "mP2loc")

    r5 = run_bass_kernel_spmd(
        build_L5(T),
        [{"P2": np.asarray(r4.results[c]["P2"]), "wd": wd,
          "partial": np.asarray(r1.results[c]["partial"]),
          "m1all": m1all, "m23all": m23all, "mPall": mPall,
          "mP2all": mP2all} for c in range(NCORES)], cores)

    out = np.stack([np.asarray(r["out"], np.float32) for r in r5.results])
    return np.ascontiguousarray(out)


# version 06
# speedup vs baseline: 1.2009x; 1.2009x over previous
import contextlib

import numpy as np
import ml_dtypes

import concourse.bass as bass
import concourse.mybir as mybir
from concourse.bass_utils import run_bass_kernel_spmd

F32 = mybir.dt.float32
BF16 = mybir.dt.bfloat16
I16 = mybir.dt.int16
U16 = mybir.dt.uint16
Alu = mybir.AluOpType
Act = mybir.ActivationFunctionType
AX = mybir.AxisListType

NCORES = 8
H = 1024
I3 = 3072
NT = 128
LF = 2048
ND = 17
RING = 18
NMAX = 16
TQ = 512
CH = 2048
INV7 = float(np.float32(1.0 / 7.0))
INVQ = float(np.float32(1.0 / 32767.0))
Q15 = 32767.0
Q16 = 65536.0
INV65536 = float(np.float32(1.0 / 65536.0))
INVH = float(np.float32(1.0 / H))
EPS12 = 1e-12

BF = ml_dtypes.bfloat16


def build_toeplitz(filt):
    out = np.zeros((ND, NT, NT), np.float32)
    j, i = np.meshgrid(np.arange(NT), np.arange(NT), indexing="ij")
    for d in range(ND):
        idx = NT * d + j - i
        m = (idx >= 0) & (idx < LF)
        out[d][m] = filt[idx[m]]
    return np.ascontiguousarray(out.reshape(ND * NT, NT))


class Plan:

    ENGS = ("sy", "gp", "v", "a", "t")

    def __init__(self):
        self.ops = {k: [] for k in self.ENGS}
        self.count = {}
        for k in ("v", "a", "t"):
            self.count[k] = 0
        self.lastw = {k: {} for k in self.ENGS}
        self.chan_eng = {}
        self.chan_last = {}

    def _record(self, eng, sem, emit, waits, inc):
        seen = {}
        for tok in waits:
            if tok is None:
                continue
            s, v = tok
            if v <= 0 or self.lastw[eng].get(s, 0) >= v:
                continue
            seen[s] = max(seen.get(s, 0), v)
        w = []
        for s, v in seen.items():
            w.append((s, v))
            self.lastw[eng][s] = v
        self.count[sem] = self.count.get(sem, 0) + inc
        self.ops[eng].append((w, emit, sem, inc))
        return (sem, self.count[sem])

    def op(self, eng, emit, waits=(), quiet=False):
        if quiet:
            return self._record(eng, eng, emit, waits, 0)
        return self._record(eng, eng, emit, waits, 1)

    def dma(self, chan, emit, waits=(), eng="sy"):
        if chan not in self.chan_eng:
            self.chan_eng[chan] = eng
        eng = self.chan_eng[chan]
        w = list(waits) + [self.chan_last.get(chan)]
        tok = self._record(eng, chan, emit, w, 16)
        self.chan_last[chan] = tok
        return tok

    def barrier_all(self, eng="gp"):
        waits = [(s, c) for s, c in self.count.items() if c > 0]
        self._record(eng, eng if eng in self.count else "v",
                     lambda e: None, waits, 0)

    def emit_block(self, nc, es):
        sems = {s: es.enter_context(nc.semaphore("sem_" + s))
                for s in self.count}
        with nc.Block() as block:
            def runner(engname):
                def run(engobj):
                    for (waits, emit, sem, inc) in self.ops[engname]:
                        for (s, v) in waits:
                            engobj.wait_ge(sems[s], v)
                        ins = emit(engobj)
                        if ins is not None and inc > 0:
                            ins.then_inc(sems[sem], inc)
                return run
            if self.ops["sy"]:
                block.sync(runner("sy"))
            if self.ops["gp"]:
                block.gpsimd(runner("gp"))
            if self.ops["v"]:
                block.vector(runner("v"))
            if self.ops["a"]:
                block.scalar(runner("a"))
            if self.ops["t"]:
                block.tensor(runner("t"))


def maxes_prologue(nc, es, p, names):
    tiles, toks = {}, {}
    for n in names:
        inp = nc.dram_tensor(n, [1, 8 * NMAX], F32, kind="ExternalInput")
        scr = nc.dram_tensor("scr_" + n, [1, NMAX], F32, kind="Internal")
        mh = es.enter_context(nc.sbuf_tensor("mh_" + n, [1, 8 * NMAX], F32))
        m4 = es.enter_context(nc.sbuf_tensor("m4_" + n, [1, 4 * NMAX], F32))
        m2 = es.enter_context(nc.sbuf_tensor("m2_" + n, [1, 2 * NMAX], F32))
        m1 = es.enter_context(nc.sbuf_tensor("m1_" + n, [1, NMAX], F32))
        mx = es.enter_context(nc.sbuf_tensor("mx_" + n, [NT, NMAX], F32))
        t = p.dma("mis", lambda e, inp=inp, mh=mh: e.dma_start(
            out=mh[:], in_=inp.ap()))
        t = p.op("v", lambda e, mh=mh, m4=m4: e.tensor_tensor(
            m4[:], mh[:, :4 * NMAX], mh[:, 4 * NMAX:], Alu.max), waits=[t])
        t = p.op("v", lambda e, m4=m4, m2=m2: e.tensor_tensor(
            m2[:], m4[:, :2 * NMAX], m4[:, 2 * NMAX:], Alu.max), waits=[t])
        t = p.op("v", lambda e, m2=m2, m1=m1: e.tensor_tensor(
            m1[:], m2[:, :NMAX], m2[:, NMAX:], Alu.max), waits=[t])
        t = p.dma("mis", lambda e, scr=scr, m1=m1: e.dma_start(
            out=scr.ap(), in_=m1[:]), waits=[t])
        t = p.dma("mis", lambda e, mx=mx, scr=scr: e.dma_start(
            out=mx[:], in_=scr.ap().broadcast_to([NT, NMAX])), waits=[t])
        tiles[n] = mx
        toks[n] = t
    return tiles, toks


def max_finalize(nc, es, p, maxacc, idf_sb, out_name, extra_waits=()):
    mout = nc.dram_tensor(out_name, [NMAX, 1], F32, kind="ExternalOutput")
    psf = es.enter_context(nc.psum_tensor("ps_mfin", [NT, NT], F32))
    mfs = es.enter_context(nc.sbuf_tensor("mfin_s", [NMAX, NT], F32))
    mfr = es.enter_context(nc.sbuf_tensor("mfin_r", [NMAX, 1], F32))
    w = list(extra_waits) + [("v", p.count["v"])]
    t = p.op("t", lambda e: e.transpose(psf[:NMAX, :], maxacc[:], idf_sb[:]),
             waits=w)
    t = p.op("v", lambda e: e.tensor_copy(mfs[:], psf[:NMAX, :]), waits=[t])
    t = p.op("v", lambda e: e.tensor_reduce(mfr[:], mfs[:], AX.X, Alu.max),
             waits=[t])
    return p.dma("mis", lambda e: e.dma_start(out=mout.ap(), in_=mfr[:]),
                 waits=[t])



def plan_round_signed(p, t_tok, tbuf, ibuf, dbuf, out_emit):
    tk = p.op("v", lambda e: e.tensor_copy(ibuf, tbuf), waits=[t_tok])
    tk = p.op("v", lambda e: e.tensor_tensor(dbuf, tbuf, ibuf, Alu.subtract),
              waits=[tk])
    tk = p.op("v", lambda e: e.scalar_tensor_tensor(
        tbuf, dbuf, 0.5, ibuf, Alu.is_ge, Alu.add), waits=[tk])
    tk = p.op("v", lambda e: out_emit(e, dbuf, tbuf), waits=[tk])
    return tk


def plan_round_u16(p, t_tok, tbuf, ubuf, dbuf, out, clip_hi):
    tk = p.op("v", lambda e: e.tensor_copy(ubuf, tbuf), waits=[t_tok])
    tk = p.op("v", lambda e: e.tensor_tensor(dbuf, tbuf, ubuf, Alu.subtract),
              waits=[tk])
    tk = p.op("v", lambda e: e.scalar_tensor_tensor(
        tbuf, dbuf, 0.5, ubuf, Alu.is_ge, Alu.add), waits=[tk])
    tk = p.op("v", lambda e: e.tensor_scalar(
        out, tbuf, float(clip_hi), None, Alu.min), waits=[tk])
    return tk


def dequant_weights(nc, es, p, specs, idb_sb, tag, psw=None):
    CT = 1024
    wbuf = [es.enter_context(nc.sbuf_tensor(f"wb{tag}{k}", [NT, CT], F32))
            for k in range(2)]
    wqb = [es.enter_context(nc.sbuf_tensor(f"wq{tag}{k}", [NT, CT], BF16))
           for k in range(2)]
    absr = es.enter_context(nc.sbuf_tensor(f"ab{tag}", [NT, CT // 32], F32))
    sblk = es.enter_context(nc.sbuf_tensor(f"sb{tag}", [NT, CT // 32], F32))
    sinv = es.enter_context(nc.sbuf_tensor(f"si{tag}", [NT, CT // 32], F32))
    tqb = es.enter_context(nc.sbuf_tensor(f"tq{tag}", [NT, CT], F32))
    dqb = es.enter_context(nc.sbuf_tensor(f"dq{tag}", [NT, CT], F32))
    kqb = es.enter_context(nc.sbuf_tensor(f"kq{tag}", [NT, CT], I16))
    if psw is None:
        psw = [es.enter_context(nc.psum_tensor(f"pw{tag}{k}", [NT, NT], BF16))
               for k in range(2)]
    last_evac = None
    tok_consumed = [None, None]
    tok_wq_read = [None, None]
    nblk = CT // 32
    n = 0
    for (w_dram, nrows, ncols, wt_sb) in specs:
        for ii in range(nrows // NT):
            for cc in range(ncols // CT):
                par = n % 2
                n += 1
                wb, wq = wbuf[par], wqb[par]
                tl = p.dma("wld", lambda e, wb=wb, w_dram=w_dram, ii=ii, cc=cc:
                           e.dma_start(
                               out=wb[:],
                               in_=w_dram.ap()[ii * NT:(ii + 1) * NT,
                                               cc * CT:(cc + 1) * CT]),
                           waits=[tok_consumed[par]])
                tv = p.op("v", lambda e, wb=wb: e.tensor_reduce(
                    absr[:].rearrange("p (b o) -> p b o", o=1),
                    wb[:].rearrange("p (b e) -> p b e", e=32),
                    AX.X, Alu.max, apply_absolute_value=True), waits=[tl])
                tv = p.op("v", lambda e: e.tensor_scalar(
                    sblk[:], absr[:], INV7, EPS12, Alu.mult, Alu.max),
                    waits=[tv])
                tv = p.op("v", lambda e: e.reciprocal(sinv[:], sblk[:]),
                          waits=[tv])
                tv = p.op("v", lambda e: e.tensor_scalar(
                    sinv[:], sinv[:], -1.0, None, Alu.mult), waits=[tv])
                tv = p.op("v", lambda e, wb=wb: e.tensor_tensor(
                    tqb[:].rearrange("p (b e) -> p b e", e=32),
                    wb[:].rearrange("p (b e) -> p b e", e=32),
                    sinv[:].rearrange("p (b o) -> p b o", o=1)
                    .broadcast_to([NT, nblk, 32]), Alu.mult), waits=[tv])
                tok_consumed[par] = tv
                tv = plan_round_signed(
                    p, tv, tqb[:], kqb[:], dqb[:],
                    lambda e, d, s: e.scalar_tensor_tensor(
                        kqb[:], d, -0.5, s, Alu.is_le, Alu.subtract))
                tv = p.op("v", lambda e, wq=wq: e.tensor_tensor(
                    wq[:].rearrange("p (b e) -> p b e", e=32),
                    kqb[:].rearrange("p (b e) -> p b e", e=32),
                    sblk[:].rearrange("p (b o) -> p b o", o=1)
                    .broadcast_to([NT, nblk, 32]), Alu.mult),
                    waits=[tv, tok_wq_read[par]])
                for ci in range(CT // NT):
                    cig = cc * (CT // NT) + ci
                    ps = psw[ci % len(psw)]
                    ttr = p.op("t", lambda e, ps=ps, wq=wq, ci=ci: e.transpose(
                        ps[:], wq[:, ci * NT:(ci + 1) * NT], idb_sb[:]),
                        waits=[tv, last_evac])
                    last_evac = p.op(
                        "a", lambda e, ps=ps, wt_sb=wt_sb, cig=cig, ii=ii,
                        nrows=nrows: e.activation(
                            wt_sb[:, cig * nrows + ii * NT:
                                  cig * nrows + (ii + 1) * NT],
                            ps[:], Act.Identity), waits=[ttr])
                tok_wq_read[par] = last_evac
    return last_evac



def build_L1(T):
    ntiles = T // NT
    nc = bass.Bass("TRN2", target_bir_lowering=False)
    xs = nc.dram_tensor("xs", [T, H], F32, kind="ExternalInput")
    Fz = nc.dram_tensor("Fz", [ND * NT, NT], F32, kind="ExternalInput")
    xh_d = nc.dram_tensor("xh", [T, H], BF16, kind="ExternalOutput")
    xl_d = nc.dram_tensor("xl", [T, H], BF16, kind="ExternalOutput")
    part_d = nc.dram_tensor("partial", [T, H], F32, kind="ExternalOutput")
    idf_c = nc.inline_tensor(np.eye(NT, dtype=np.float32), name="idf")

    p = Plan()
    es = contextlib.ExitStack()
    with es:
        xr = es.enter_context(nc.sbuf_tensor("xr", [NT, RING * H], F32))
        ff = es.enter_context(nc.sbuf_tensor("ff", [NT, ND * NT], F32))
        fh = es.enter_context(nc.sbuf_tensor("fh", [NT, ND * NT], BF16))
        fl = es.enter_context(nc.sbuf_tensor("fl", [NT, ND * NT], BF16))
        xhr = es.enter_context(nc.sbuf_tensor("xhr", [NT, RING * H], BF16))
        xlr = es.enter_context(nc.sbuf_tensor("xlr", [NT, RING * H], BF16))
        xn = es.enter_context(nc.sbuf_tensor("xn", [NT, H], F32))
        dump = es.enter_context(nc.sbuf_tensor("dump", [NT, H], F32))
        smv = es.enter_context(nc.sbuf_tensor("smv", [NT, 1], F32))
        sqv = es.enter_context(nc.sbuf_tensor("sqv", [NT, 1], F32))
        muv = es.enter_context(nc.sbuf_tensor("muv", [NT, 1], F32))
        ex2 = es.enter_context(nc.sbuf_tensor("ex2", [NT, 1], F32))
        mu2 = es.enter_context(nc.sbuf_tensor("mu2", [NT, 1], F32))
        varv = es.enter_context(nc.sbuf_tensor("varv", [NT, 1], F32))
        rrv = es.enter_context(nc.sbuf_tensor("rrv", [NT, 1], F32))
        rsv = es.enter_context(nc.sbuf_tensor("rsv", [NT, 1], F32))
        mtv = es.enter_context(nc.sbuf_tensor("mtv", [NT, 1], F32))
        maxacc = es.enter_context(nc.sbuf_tensor("maxacc", [NT, NMAX], F32))
        idf_sb = es.enter_context(nc.sbuf_tensor("idf_sb", [NT, NT], F32))
        parts = [es.enter_context(nc.sbuf_tensor(f"part{k}", [NT, H], F32))
                 for k in range(2)]
        psc = [es.enter_context(nc.psum_tensor(f"psc{k}", [NT, H], F32))
               for k in range(3)]

        t_id = p.dma("mis", lambda e: e.dma_start(
            out=idf_sb[:], in_=idf_c.ap()))
        t_f = p.dma("mis", lambda e: e.dma_start(
            out=ff[:].rearrange("p (d i) -> p d i", i=NT),
            in_=Fz.ap().rearrange("(d j) i -> j d i", j=NT)))
        t_fh = p.op("v", lambda e: e.tensor_copy(fh[:], ff[:]), waits=[t_f])
        t_fl = p.op("v", lambda e: e.tensor_tensor(
            fl[:], ff[:], fh[:], Alu.subtract), waits=[t_fh])
        t_mz = p.op("v", lambda e: e.memset(maxacc[:], 0.0))

        tok_load = [None] * ntiles
        tok_xl = [None] * ntiles
        tok_store_hl = [None] * ntiles
        tok_evac = [None] * ntiles
        tok_store = [None] * ntiles
        tok_statsfree = [None] * ntiles

        def emit_stats(i):
            sl = (i % RING) * H
            xb = xr[:, sl:sl + H]
            w = [tok_load[i]]
            if i >= 1:
                w.append(tok_statsfree[i - 1])
            ta = p.op("a", lambda e, xb=xb: e.activation(
                dump[:], xb, Act.Square, accum_out=sqv[:]), waits=w)
            ta = p.op("a", lambda e, xb=xb: e.activation(
                dump[:], xb, Act.Identity, accum_out=smv[:]), waits=[ta])
            tv = p.op("v", lambda e: e.tensor_scalar(
                muv[:], smv[:], INVH, None, Alu.mult), waits=[ta])
            tv = p.op("v", lambda e: e.tensor_scalar(
                ex2[:], sqv[:], INVH, None, Alu.mult), waits=[tv])
            tok_statsfree[i] = tv
            tv = p.op("v", lambda e: e.tensor_tensor(
                mu2[:], muv[:], muv[:], Alu.mult), waits=[tv])
            tv = p.op("v", lambda e: e.scalar_tensor_tensor(
                varv[:], ex2[:], 1e-5, mu2[:], Alu.add, Alu.subtract),
                waits=[tv])
            ta2 = p.op("a", lambda e: e.activation(
                rrv[:], varv[:], Act.Sqrt), waits=[tv])
            tv = p.op("v", lambda e: e.reciprocal(rsv[:], rrv[:]), waits=[ta2])
            tv = p.op("v", lambda e, xb=xb: e.scalar_tensor_tensor(
                xn[:], xb, muv[:], rsv[:].broadcast_to([NT, H]),
                Alu.subtract, Alu.mult), waits=[tv])
            tv = p.op("v", lambda e: e.tensor_reduce(
                mtv[:], xn[:], AX.X, Alu.max, apply_absolute_value=True),
                waits=[tv])
            tv = p.op("v", lambda e: e.tensor_tensor(
                maxacc[:, 0:1], maxacc[:, 0:1], mtv[:], Alu.max),
                waits=[tv, t_mz])
            wh = [tv, t_fl]
            if i >= RING:
                wh.extend(tok_store_hl[i - RING])
            th = p.op("v", lambda e, sl=sl: e.tensor_copy(
                xhr[:, sl:sl + H], xn[:]), waits=wh)
            tl = p.op("v", lambda e, sl=sl: e.tensor_tensor(
                xlr[:, sl:sl + H], xn[:], xhr[:, sl:sl + H], Alu.subtract),
                waits=[th])
            tok_xl[i] = tl
            t1 = p.dma("sxh", lambda e, sl=sl, i=i: e.dma_start(
                out=xh_d.ap()[i * NT:(i + 1) * NT, :], in_=xhr[:, sl:sl + H]),
                waits=[tl], eng="gp")
            t2 = p.dma("sxl", lambda e, sl=sl, i=i: e.dma_start(
                out=xl_d.ap()[i * NT:(i + 1) * NT, :], in_=xlr[:, sl:sl + H]),
                waits=[tl], eng="gp")
            tok_store_hl[i] = (t1, t2)

        def emit_conv(j):
            ps = psc[j % 3]
            dmax = min(ND - 1, ntiles - 1 - j)
            w = [tok_xl[j + dmax], t_fl]
            if j >= 3:
                w.append(tok_evac[j - 3])
            n_terms = 3 * (dmax + 1)
            cnt = 0
            first = [True, True]
            last = None
            for d in range(dmax + 1):
                sl = ((j + d) % RING) * H
                for (fm, xm) in ((fh, xhr), (fl, xhr), (fh, xlr)):
                    cnt += 1
                    stop = (cnt == n_terms)
                    for half in range(2):
                        cs = slice(half * 512, half * 512 + 512)
                        st = first[half]
                        first[half] = False
                        is_last = (cnt == n_terms and half == 1)
                        last = p.op("t", lambda e, ps=ps, fm=fm, xm=xm, d=d,
                                    sl=sl, cs=cs, st=st, stop=stop: e.matmul(
                                        ps[:, cs], fm[:, d * NT:(d + 1) * NT],
                                        xm[:, sl + cs.start:sl + cs.stop],
                                        start=st, stop=stop), waits=w,
                                    quiet=not is_last)
                        w = []
            sl = (j % RING) * H
            pb = parts[j % 2]
            we = [last, tok_load[j]]
            if j >= 2:
                we.append(tok_store[j - 2])
            te = p.op("v", lambda e, ps=ps, sl=sl, pb=pb: e.tensor_tensor(
                pb[:], ps[:], xr[:, sl:sl + H], Alu.add), waits=we)
            tok_evac[j] = te
            tok_store[j] = p.dma("spt", lambda e, pb=pb, j=j: e.dma_start(
                out=part_d.ap()[j * NT:(j + 1) * NT, :], in_=pb[:]),
                waits=[te], eng="gp")

        for i in range(ntiles):
            w = []
            if i >= RING:
                w.append(tok_evac[i - RING])
            tok_load[i] = p.dma(f"xi{i % 2}", lambda e, i=i: e.dma_start(
                out=xr[:, (i % RING) * H:(i % RING) * H + H],
                in_=xs.ap()[i * NT:(i + 1) * NT, :]), waits=w)
            emit_stats(i)
            if i >= ND - 1:
                emit_conv(i - (ND - 1))
        for j in range(max(0, ntiles - (ND - 1)), ntiles):
            emit_conv(j)

        max_finalize(nc, es, p, maxacc, idf_sb, "m1loc", [t_id])
        p.barrier_all()
        p.emit_block(nc, es)
    return nc



def build_L2(T):
    ngrp = T // TQ
    ICH = I3 // NT
    nc = bass.Bass("TRN2", target_bir_lowering=False)
    xh_d = nc.dram_tensor("xh", [T, H], BF16, kind="ExternalInput")
    xl_d = nc.dram_tensor("xl", [T, H], BF16, kind="ExternalInput")
    wu_d = nc.dram_tensor("wu", [I3, H], F32, kind="ExternalInput")
    wg_d = nc.dram_tensor("wg", [I3, H], F32, kind="ExternalInput")
    ut_d = nc.dram_tensor("ut", [I3, T], F32, kind="ExternalOutput")
    gt_d = nc.dram_tensor("gt", [I3, T], F32, kind="ExternalOutput")
    idb_c = nc.inline_tensor(np.eye(NT, dtype=BF), name="idb")
    idf_c = nc.inline_tensor(np.eye(NT, dtype=np.float32), name="idf")

    p = Plan()
    es = contextlib.ExitStack()
    with es:
        mxt, mxtok = maxes_prologue(nc, es, p, ["m1all"])
        mx1 = mxt["m1all"]
        idb_sb = es.enter_context(nc.sbuf_tensor("idb_sb", [NT, NT], BF16))
        idf_sb = es.enter_context(nc.sbuf_tensor("idf_sb", [NT, NT], F32))
        s1c = es.enter_context(nc.sbuf_tensor("s1c", [NT, 1], F32))
        c1c = es.enter_context(nc.sbuf_tensor("c1c", [NT, 1], F32))
        wut = es.enter_context(nc.sbuf_tensor("wut", [NT, 8 * I3], BF16))
        wgt = es.enter_context(nc.sbuf_tensor("wgt", [NT, 8 * I3], BF16))
        xhg = [es.enter_context(nc.sbuf_tensor(f"xhg{k}", [NT, 4 * H], BF16))
               for k in range(2)]
        xlg = [es.enter_context(nc.sbuf_tensor(f"xlg{k}", [NT, 4 * H], BF16))
               for k in range(2)]
        xqt = [es.enter_context(nc.sbuf_tensor(f"xqt{k}", [NT, 8 * TQ], BF16))
               for k in range(2)]
        xnb = es.enter_context(nc.sbuf_tensor("xnb", [NT, H], F32))
        t1b = es.enter_context(nc.sbuf_tensor("t1b", [NT, H], F32))
        d1b = es.enter_context(nc.sbuf_tensor("d1b", [NT, H], F32))
        r1i = es.enter_context(nc.sbuf_tensor("r1i", [NT, H], I16))
        r1f = [es.enter_context(nc.sbuf_tensor(f"r1f{k}", [NT, H], BF16))
               for k in range(2)]
        ue = [es.enter_context(nc.sbuf_tensor(f"ue{k}", [NT, TQ], F32))
              for k in range(2)]
        ge = [es.enter_context(nc.sbuf_tensor(f"ge{k}", [NT, TQ], F32))
              for k in range(2)]
        nch_mm = (T // TQ) * (I3 // NT)
        urd = es.enter_context(nc.sbuf_tensor("urd", [NT, nch_mm], F32))
        grd = es.enter_context(nc.sbuf_tensor("grd", [NT, nch_mm], F32))
        maxacc = es.enter_context(nc.sbuf_tensor("maxacc", [NT, NMAX], F32))
        psu = [es.enter_context(nc.psum_tensor(f"pu{k}", [NT, TQ], F32))
               for k in range(3)]
        psg = [es.enter_context(nc.psum_tensor(f"pg{k}", [NT, TQ], F32))
               for k in range(3)]
        psx = [es.enter_context(nc.psum_tensor(f"px{k}", [NT, NT], BF16))
               for k in range(1)]

        t_id = p.dma("mis", lambda e: e.dma_start(
            out=idb_sb[:], in_=idb_c.ap()))
        t_idf = p.dma("mis", lambda e: e.dma_start(
            out=idf_sb[:], in_=idf_c.ap()))
        t_mz = p.op("v", lambda e: e.memset(maxacc[:], 0.0))
        tv = p.op("v", lambda e: e.tensor_scalar(
            s1c[:], mx1[:, 0:1], INVQ, EPS12, Alu.mult, Alu.max),
            waits=[mxtok["m1all"]])
        t_c1 = p.op("v", lambda e: e.reciprocal(c1c[:], s1c[:]), waits=[tv])

        tok_w = dequant_weights(
            nc, es, p, [(wu_d, I3, H, wut), (wg_d, I3, H, wgt)], idb_sb, "w",
            psw=psx)

        tok_mm = {"u": [[None] * ICH for _ in range(ngrp)],
                  "g": [[None] * ICH for _ in range(ngrp)]}
        tok_ev = {"u": [[None] * ICH for _ in range(ngrp)],
                  "g": [[None] * ICH for _ in range(ngrp)]}
        tok_st = {"u": [[None] * ICH for _ in range(ngrp)],
                  "g": [[None] * ICH for _ in range(ngrp)]}
        tok_rd = {"u": [[None] * ICH for _ in range(ngrp)],
                  "g": [[None] * ICH for _ in range(ngrp)]}
        tok_rf = [None, None]
        tok_xqt = [None] * ngrp
        last_xev = tok_w

        for q in range(ngrp):
            w = []
            if q >= 2:
                w += [tok_mm["u"][q - 2][ICH - 1], tok_mm["g"][q - 2][ICH - 1]]
            tl = p.dma(f"xi{q % 2}", lambda e, q=q: e.dma_start(
                out=xhg[q % 2][:].rearrange("p (k h) -> p k h", h=H),
                in_=xh_d.ap()[q * TQ:(q + 1) * TQ, :].rearrange(
                    "(k p) h -> p k h", p=NT)), waits=w)
            tl = p.dma(f"xi{q % 2}", lambda e, q=q: e.dma_start(
                out=xlg[q % 2][:].rearrange("p (k h) -> p k h", h=H),
                in_=xl_d.ap()[q * TQ:(q + 1) * TQ, :].rearrange(
                    "(k p) h -> p k h", p=NT)), waits=[tl])
            lastev = None
            for k in range(4):
                cs = slice(k * H, (k + 1) * H)
                tv = p.op("v", lambda e, q=q, cs=cs: e.tensor_tensor(
                    xnb[:], xhg[q % 2][:, cs], xlg[q % 2][:, cs], Alu.add),
                    waits=[tl, t_c1])
                rf = r1f[k % 2]
                ta1 = p.op("a", lambda e: e.activation(
                    t1b[:], xnb[:], Act.Identity, bias=0.0, scale=c1c[:]),
                    waits=[tv, tok_rf[k % 2]])
                ta1 = p.op("a", lambda e: e.activation(
                    r1i[:], t1b[:], Act.Identity), waits=[ta1])
                tg1 = p.op("gp", lambda e: e.tensor_tensor(
                    d1b[:], t1b[:], r1i[:], Alu.subtract), waits=[ta1])
                tv = p.op("v", lambda e: e.scalar_tensor_tensor(
                    t1b[:], d1b[:], 0.5, r1i[:], Alu.is_ge, Alu.add),
                    waits=[tg1])
                tv = p.op("v", lambda e, rf=rf: e.scalar_tensor_tensor(
                    rf[:], d1b[:], -0.5, t1b[:], Alu.is_le, Alu.subtract),
                    waits=[tv])
                for ci in range(8):
                    ps = psx[0]
                    wt = [tv, t_id, last_xev]
                    if q >= 2 and k == 0 and ci == 0:
                        wt += [tok_mm["u"][q - 2][ICH - 1],
                               tok_mm["g"][q - 2][ICH - 1]]
                    ttr = p.op("t", lambda e, ps=ps, rf=rf, ci=ci: e.transpose(
                        ps[:], rf[:, ci * NT:(ci + 1) * NT], idb_sb[:]),
                        waits=wt)
                    lastev = p.op("a", lambda e, ps=ps, q=q, k=k, ci=ci:
                                  e.activation(
                                      xqt[q % 2][:, ci * TQ + k * NT:
                                                 ci * TQ + (k + 1) * NT],
                                      ps[:], Act.Identity), waits=[ttr])
                    last_xev = lastev
                tok_rf[k % 2] = lastev
            tok_xqt[q] = lastev
            for ii in range(ICH):
                for key, wt_sb, psv, ev, rd, dst, col in (
                    ("u", wut, psu, ue, urd, ut_d, 0),
                    ("g", wgt, psg, ge, grd, gt_d, 1),
                ):
                    ps = psv[ii % 3]
                    wm = [tok_xqt[q], tok_w]
                    if ii >= 3:
                        wm.append(tok_ev[key][q][ii - 3])
                    elif q >= 1:
                        wm.append(tok_ev[key][q - 1][ICH - 3 + ii])
                    last = None
                    for ci in range(8):
                        last = p.op("t", lambda e, ps=ps, wt_sb=wt_sb, ci=ci,
                                    ii=ii, q=q: e.matmul(
                                        ps[:],
                                        wt_sb[:, ci * I3 + ii * NT:
                                              ci * I3 + (ii + 1) * NT],
                                        xqt[q % 2][:, ci * TQ:(ci + 1) * TQ],
                                        start=(ci == 0), stop=(ci == 7)),
                                    waits=wm, quiet=(ci != 7))
                        wm = []
                    tok_mm[key][q][ii] = last
                    eb = ev[ii % 2]
                    we = [last]
                    if ii >= 2:
                        we.append(tok_st[key][q][ii - 2])
                        we.extend(tok_rd[key][q][ii - 2])
                    elif q >= 1:
                        we.append(tok_st[key][q - 1][ICH - 2 + ii])
                        we.extend(tok_rd[key][q - 1][ICH - 2 + ii])
                    tev = p.op("a", lambda e, ps=ps, eb=eb: e.activation(
                        eb[:], ps[:], Act.Identity), waits=we)
                    tok_ev[key][q][ii] = tev
                    idx = q * ICH + ii
                    tn = p.op("v", lambda e, eb=eb, rd=rd, idx=idx:
                              e.tensor_reduce(
                                  rd[:, idx:idx + 1], eb[:], AX.X, Alu.max,
                                  apply_absolute_value=True), waits=[tev])
                    tok_rd[key][q][ii] = (tn,)
                    tok_st[key][q][ii] = p.dma(
                        "s" + key, lambda e, eb=eb, ii=ii, q=q, dst=dst:
                        e.dma_start(out=dst.ap()[ii * NT:(ii + 1) * NT,
                                                 q * TQ:(q + 1) * TQ],
                                    in_=eb[:]), waits=[tev], eng="a")

        wlast = list(tok_rd["u"][ngrp - 1][ICH - 1]) + \
            list(tok_rd["g"][ngrp - 1][ICH - 1])
        tfr = p.op("v", lambda e: e.tensor_reduce(
            maxacc[:, 0:1], urd[:], AX.X, Alu.max, apply_absolute_value=True),
            waits=wlast)
        p.op("v", lambda e: e.tensor_reduce(
            maxacc[:, 1:2], grd[:], AX.X, Alu.max, apply_absolute_value=True),
            waits=[tfr])
        max_finalize(nc, es, p, maxacc, idf_sb, "m23loc", [t_idf])
        p.barrier_all()
        p.emit_block(nc, es)
    return nc



def build_L3(T):
    ch = min(CH, T)
    nc = bass.Bass("TRN2", target_bir_lowering=False)
    gt_d = nc.dram_tensor("gt", [I3, T], F32, kind="ExternalInput")
    P_d = nc.dram_tensor("P", [I3, T], F32, kind="ExternalOutput")
    idf_c = nc.inline_tensor(np.eye(NT, dtype=np.float32), name="idf")

    p = Plan()
    es = contextlib.ExitStack()
    with es:
        mxt, mxtok = maxes_prologue(nc, es, p, ["m1all", "m23all"])
        mx1, mx23 = mxt["m1all"], mxt["m23all"]
        idf_sb = es.enter_context(nc.sbuf_tensor("idf_sb", [NT, NT], F32))
        s1c = es.enter_context(nc.sbuf_tensor("s1c", [NT, 1], F32))
        s3c = es.enter_context(nc.sbuf_tensor("s3c", [NT, 1], F32))
        c3c = es.enter_context(nc.sbuf_tensor("c3c", [NT, 1], F32))
        tmpc = es.enter_context(nc.sbuf_tensor("tmpc", [NT, 1], F32))
        gb = [es.enter_context(nc.sbuf_tensor(f"gb{k}", [NT, ch], F32))
              for k in range(2)]
        t3b = [es.enter_context(nc.sbuf_tensor(f"t3b{k}", [NT, ch], F32))
               for k in range(2)]
        d3b = [es.enter_context(nc.sbuf_tensor(f"d3b{k}", [NT, ch], F32))
               for k in range(2)]
        r3i = [es.enter_context(nc.sbuf_tensor(f"r3i{k}", [NT, ch], I16))
               for k in range(2)]
        r3b = [es.enter_context(nc.sbuf_tensor(f"r3b{k}", [NT, ch], F32))
               for k in range(2)]
        sgb = [es.enter_context(nc.sbuf_tensor(f"sgb{k}", [NT, ch], F32))
               for k in range(2)]
        t4b = [es.enter_context(nc.sbuf_tensor(f"t4b{k}", [NT, ch], F32))
               for k in range(2)]
        d4b = [es.enter_context(nc.sbuf_tensor(f"d4b{k}", [NT, ch], F32))
               for k in range(2)]
        r4u = [es.enter_context(nc.sbuf_tensor(f"r4u{k}", [NT, ch], U16))
               for k in range(2)]
        r4b = [es.enter_context(nc.sbuf_tensor(f"r4b{k}", [NT, ch], F32))
               for k in range(2)]
        Pb = [es.enter_context(nc.sbuf_tensor(f"Pb{k}", [NT, ch], F32))
              for k in range(2)]
        rdb = es.enter_context(nc.sbuf_tensor("rdb", [NT, 1], F32))
        maxacc = es.enter_context(nc.sbuf_tensor("maxacc", [NT, NMAX], F32))

        t_idf = p.dma("mis", lambda e: e.dma_start(
            out=idf_sb[:], in_=idf_c.ap()))
        t_mz = p.op("v", lambda e: e.memset(maxacc[:], 0.0))
        wv = [mxtok["m1all"], mxtok["m23all"]]
        tv = p.op("v", lambda e: e.tensor_scalar(
            s1c[:], mx1[:, 0:1], INVQ, EPS12, Alu.mult, Alu.max), waits=wv)
        tv = p.op("v", lambda e: e.tensor_tensor(
            tmpc[:], s1c[:], mx23[:, 1:2], Alu.mult), waits=[tv])
        t_s3 = p.op("v", lambda e: e.tensor_scalar(
            s3c[:], tmpc[:], INVQ, EPS12, Alu.mult, Alu.max), waits=[tv])
        tv = p.op("v", lambda e: e.reciprocal(c3c[:], mx23[:, 1:2]),
                  waits=[t_s3])
        t_c3 = p.op("v", lambda e: e.tensor_scalar(
            c3c[:], c3c[:], Q15, None, Alu.mult), waits=[tv])

        nchunk = (I3 // NT) * (T // ch)
        ncol = T // ch
        tok = {k: [None] * nchunk for k in
               ("ld", "t3", "c3", "d3", "su3", "r3", "sg", "t4", "c4", "d4",
                "su4", "min", "P", "red", "st")}

        def rc(c):
            ii, qq = divmod(c, ncol)
            return slice(ii * NT, (ii + 1) * NT), slice(qq * ch, (qq + 1) * ch)

        def tk(k, c):
            return tok[k][c] if c >= 0 else None

        def stage1(c):
            rows, cols = rc(c)
            b = c % 2
            tok["ld"][c] = p.dma(f"in{c % 2}", lambda e, c=c, rows=rows,
                                 cols=cols: e.dma_start(
                out=gb[c % 2][:], in_=gt_d.ap()[rows, cols]),
                waits=[tk("t3", c - 2)])
            tok["t3"][c] = p.op("a", lambda e, c=c, b=b: e.activation(
                t3b[b][:], gb[c % 2][:], Act.Identity, bias=0.0, scale=c3c[:]),
                waits=[tok["ld"][c], t_c3, tk("r3", c - 2)])
            tok["c3"][c] = p.op("a", lambda e, b=b: e.activation(
                r3i[b][:], t3b[b][:], Act.Identity),
                waits=[tok["t3"][c], tk("su3", c - 2)])
            tok["d3"][c] = p.op("gp", lambda e, b=b: e.tensor_tensor(
                d3b[b][:], t3b[b][:], r3i[b][:], Alu.subtract),
                waits=[tok["c3"][c], tk("r3", c - 2)])
            tok["su3"][c] = p.op("v", lambda e, b=b: e.scalar_tensor_tensor(
                t3b[b][:], d3b[b][:], 0.5, r3i[b][:], Alu.is_ge, Alu.add),
                waits=[tok["d3"][c]])
            tok["r3"][c] = p.op("v", lambda e, b=b: e.scalar_tensor_tensor(
                r3b[b][:], d3b[b][:], -0.5, t3b[b][:], Alu.is_le,
                Alu.subtract), waits=[tok["su3"][c], tk("P", c - 2)])

        def stage2(c):
            rows, cols = rc(c)
            b = c % 2
            tok["sg"][c] = p.op("a", lambda e, b=b: e.activation(
                sgb[b][:], r3b[b][:], Act.Sigmoid, bias=0.0, scale=s3c[:]),
                waits=[tok["r3"][c], t_s3, tk("t4", c - 2)])
            tok["t4"][c] = p.op("a", lambda e, b=b: e.activation(
                t4b[b][:], sgb[b][:], Act.Identity, bias=0.0, scale=65536.0),
                waits=[tok["sg"][c], tk("min", c - 2)])
            tok["c4"][c] = p.op("a", lambda e, b=b: e.activation(
                r4u[b][:], t4b[b][:], Act.Identity),
                waits=[tok["t4"][c], tk("su4", c - 2)])
            tok["d4"][c] = p.op("gp", lambda e, b=b: e.tensor_tensor(
                d4b[b][:], t4b[b][:], r4u[b][:], Alu.subtract),
                waits=[tok["c4"][c], tk("su4", c - 2)])
            tok["su4"][c] = p.op("v", lambda e, b=b: e.scalar_tensor_tensor(
                t4b[b][:], d4b[b][:], 0.5, r4u[b][:], Alu.is_ge, Alu.add),
                waits=[tok["d4"][c]])
            tok["min"][c] = p.op("v", lambda e, b=b: e.tensor_scalar(
                r4b[b][:], t4b[b][:], 65535.0, None, Alu.min),
                waits=[tok["su4"][c], tk("P", c - 2)])
            tok["P"][c] = p.op("gp", lambda e, c=c, b=b: e.tensor_tensor(
                Pb[c % 2][:], r3b[b][:], r4b[b][:], Alu.mult),
                waits=[tok["min"][c], tk("st", c - 2), tk("red", c - 2)])
            tv = p.op("v", lambda e, c=c: e.tensor_reduce(
                rdb[:], Pb[c % 2][:], AX.X, Alu.max,
                apply_absolute_value=True), waits=[tok["P"][c]])
            tok["red"][c] = p.op("v", lambda e: e.tensor_tensor(
                maxacc[:, 0:1], maxacc[:, 0:1], rdb[:], Alu.max),
                waits=[tv, t_mz])
            tok["st"][c] = p.dma(f"sP{c % 2}", lambda e, c=c, rows=rows,
                                 cols=cols: e.dma_start(
                out=P_d.ap()[rows, cols], in_=Pb[c % 2][:]),
                waits=[tok["P"][c]], eng="a")

        for c in range(nchunk):
            stage1(c)
            if c >= 1:
                stage2(c - 1)
        stage2(nchunk - 1)

        max_finalize(nc, es, p, maxacc, idf_sb, "mPloc", [t_idf])
        p.barrier_all()
        p.emit_block(nc, es)
    return nc



def build_L4(T):
    ch = min(CH, T)
    nc = bass.Bass("TRN2", target_bir_lowering=False)
    ut_d = nc.dram_tensor("ut", [I3, T], F32, kind="ExternalInput")
    P_d = nc.dram_tensor("P", [I3, T], F32, kind="ExternalInput")
    P2_d = nc.dram_tensor("P2", [I3, T], F32, kind="ExternalOutput")
    idf_c = nc.inline_tensor(np.eye(NT, dtype=np.float32), name="idf")

    p = Plan()
    es = contextlib.ExitStack()
    with es:
        mxt, mxtok = maxes_prologue(nc, es, p, ["m23all", "mPall"])
        mx23, mxP = mxt["m23all"], mxt["mPall"]
        idf_sb = es.enter_context(nc.sbuf_tensor("idf_sb", [NT, NT], F32))
        c2c = es.enter_context(nc.sbuf_tensor("c2c", [NT, 1], F32))
        c5c = es.enter_context(nc.sbuf_tensor("c5c", [NT, 1], F32))
        ub = [es.enter_context(nc.sbuf_tensor(f"ub{k}", [NT, ch], F32))
              for k in range(2)]
        Pb = [es.enter_context(nc.sbuf_tensor(f"Pb{k}", [NT, ch], F32))
              for k in range(2)]
        t2b = [es.enter_context(nc.sbuf_tensor(f"t2b{k}", [NT, ch], F32))
               for k in range(2)]
        d2b = [es.enter_context(nc.sbuf_tensor(f"d2b{k}", [NT, ch], F32))
               for k in range(2)]
        r2i = [es.enter_context(nc.sbuf_tensor(f"r2i{k}", [NT, ch], I16))
               for k in range(2)]
        r2b = [es.enter_context(nc.sbuf_tensor(f"r2b{k}", [NT, ch], F32))
               for k in range(2)]
        t5b = [es.enter_context(nc.sbuf_tensor(f"t5b{k}", [NT, ch], F32))
               for k in range(2)]
        d5b = [es.enter_context(nc.sbuf_tensor(f"d5b{k}", [NT, ch], F32))
               for k in range(2)]
        r5i = [es.enter_context(nc.sbuf_tensor(f"r5i{k}", [NT, ch], I16))
               for k in range(2)]
        r5b = [es.enter_context(nc.sbuf_tensor(f"r5b{k}", [NT, ch], F32))
               for k in range(2)]
        P2b = [es.enter_context(nc.sbuf_tensor(f"P2b{k}", [NT, ch], F32))
               for k in range(2)]
        rdb = es.enter_context(nc.sbuf_tensor("rdb", [NT, 1], F32))
        maxacc = es.enter_context(nc.sbuf_tensor("maxacc", [NT, NMAX], F32))

        t_idf = p.dma("mis", lambda e: e.dma_start(
            out=idf_sb[:], in_=idf_c.ap()))
        t_mz = p.op("v", lambda e: e.memset(maxacc[:], 0.0))
        wv = [mxtok["m23all"], mxtok["mPall"]]
        tv = p.op("v", lambda e: e.reciprocal(c2c[:], mx23[:, 0:1]), waits=wv)
        t_c2 = p.op("v", lambda e: e.tensor_scalar(
            c2c[:], c2c[:], Q15, None, Alu.mult), waits=[tv])
        tv = p.op("v", lambda e: e.reciprocal(c5c[:], mxP[:, 0:1]),
                  waits=[t_c2])
        t_c5 = p.op("v", lambda e: e.tensor_scalar(
            c5c[:], c5c[:], Q15, None, Alu.mult), waits=[tv])

        nchunk = (I3 // NT) * (T // ch)
        ncol = T // ch
        tok = {k: [None] * nchunk for k in
               ("ldu", "ldp", "t2", "c2", "d2", "su2", "r2", "t5", "c5",
                "d5", "su5", "r5", "P2", "red", "st")}

        def rc(c):
            ii, qq = divmod(c, ncol)
            return slice(ii * NT, (ii + 1) * NT), slice(qq * ch, (qq + 1) * ch)

        def tk(k, c):
            return tok[k][c] if c >= 0 else None

        def stage1(c):
            rows, cols = rc(c)
            tok["ldu"][c] = p.dma(f"inu{c % 2}", lambda e, c=c, rows=rows,
                                  cols=cols: e.dma_start(
                out=ub[c % 2][:], in_=ut_d.ap()[rows, cols]),
                waits=[tk("t2", c - 2)])
            tok["ldp"][c] = p.dma(f"inp{c % 2}", lambda e, c=c, rows=rows,
                                  cols=cols: e.dma_start(
                out=Pb[c % 2][:], in_=P_d.ap()[rows, cols]),
                waits=[tk("t5", c - 2)])
            b = c % 2
            tok["t2"][c] = p.op("a", lambda e, c=c, b=b: e.activation(
                t2b[b][:], ub[c % 2][:], Act.Identity, bias=0.0, scale=c2c[:]),
                waits=[tok["ldu"][c], t_c5, tk("r2", c - 2)])
            tok["c2"][c] = p.op("a", lambda e, b=b: e.activation(
                r2i[b][:], t2b[b][:], Act.Identity),
                waits=[tok["t2"][c], tk("su2", c - 2)])
            tok["d2"][c] = p.op("gp", lambda e, b=b: e.tensor_tensor(
                d2b[b][:], t2b[b][:], r2i[b][:], Alu.subtract),
                waits=[tok["c2"][c], tk("r2", c - 2)])
            tok["su2"][c] = p.op("v", lambda e, b=b: e.scalar_tensor_tensor(
                t2b[b][:], d2b[b][:], 0.5, r2i[b][:], Alu.is_ge, Alu.add),
                waits=[tok["d2"][c]])
            tok["r2"][c] = p.op("v", lambda e, b=b: e.scalar_tensor_tensor(
                r2b[b][:], d2b[b][:], -0.5, t2b[b][:], Alu.is_le,
                Alu.subtract), waits=[tok["su2"][c], tk("P2", c - 2)])

        def stage2(c):
            rows, cols = rc(c)
            b = c % 2
            tok["t5"][c] = p.op("a", lambda e, c=c, b=b: e.activation(
                t5b[b][:], Pb[c % 2][:], Act.Identity, bias=0.0, scale=c5c[:]),
                waits=[tok["ldp"][c], t_c5, tk("r5", c - 2)])
            tok["c5"][c] = p.op("a", lambda e, b=b: e.activation(
                r5i[b][:], t5b[b][:], Act.Identity),
                waits=[tok["t5"][c], tk("su5", c - 2)])
            tok["d5"][c] = p.op("gp", lambda e, b=b: e.tensor_tensor(
                d5b[b][:], t5b[b][:], r5i[b][:], Alu.subtract),
                waits=[tok["c5"][c], tk("r5", c - 2)])
            tok["su5"][c] = p.op("v", lambda e, b=b: e.scalar_tensor_tensor(
                t5b[b][:], d5b[b][:], 0.5, r5i[b][:], Alu.is_ge, Alu.add),
                waits=[tok["d5"][c]])
            tok["r5"][c] = p.op("v", lambda e, b=b: e.scalar_tensor_tensor(
                r5b[b][:], d5b[b][:], -0.5, t5b[b][:], Alu.is_le,
                Alu.subtract), waits=[tok["su5"][c], tk("P2", c - 2)])
            tok["P2"][c] = p.op("gp", lambda e, c=c, b=b: e.tensor_tensor(
                P2b[c % 2][:], r5b[b][:], r2b[b][:], Alu.mult),
                waits=[tok["r5"][c], tok["r2"][c], tk("st", c - 2),
                       tk("red", c - 2)])
            tv = p.op("v", lambda e, c=c: e.tensor_reduce(
                rdb[:], P2b[c % 2][:], AX.X, Alu.max,
                apply_absolute_value=True), waits=[tok["P2"][c]])
            tok["red"][c] = p.op("v", lambda e: e.tensor_tensor(
                maxacc[:, 0:1], maxacc[:, 0:1], rdb[:], Alu.max),
                waits=[tv, t_mz])
            tok["st"][c] = p.dma(f"sP{c % 2}", lambda e, c=c, rows=rows,
                                 cols=cols: e.dma_start(
                out=P2_d.ap()[rows, cols], in_=P2b[c % 2][:]),
                waits=[tok["P2"][c]], eng="a")

        for c in range(nchunk):
            stage1(c)
            if c >= 1:
                stage2(c - 1)
        stage2(nchunk - 1)

        max_finalize(nc, es, p, maxacc, idf_sb, "mP2loc", [t_idf])
        p.barrier_all()
        p.emit_block(nc, es)
    return nc



def build_L5(T):
    ngrp = T // TQ
    ICH = I3 // NT
    nc = bass.Bass("TRN2", target_bir_lowering=False)
    P2_d = nc.dram_tensor("P2", [I3, T], F32, kind="ExternalInput")
    wd_d = nc.dram_tensor("wd", [H, I3], F32, kind="ExternalInput")
    part_d = nc.dram_tensor("partial", [T, H], F32, kind="ExternalInput")
    out_d = nc.dram_tensor("out", [T, H], F32, kind="ExternalOutput")
    idb_c = nc.inline_tensor(np.eye(NT, dtype=BF), name="idb")
    idf_c = nc.inline_tensor(np.eye(NT, dtype=np.float32), name="idf")

    p = Plan()
    es = contextlib.ExitStack()
    with es:
        mxt, mxtok = maxes_prologue(
            nc, es, p, ["m1all", "m23all", "mPall", "mP2all"])
        mx1, mx23 = mxt["m1all"], mxt["m23all"]
        mxP, mxP2 = mxt["mPall"], mxt["mP2all"]
        idb_sb = es.enter_context(nc.sbuf_tensor("idb_sb", [NT, NT], BF16))
        idf_sb = es.enter_context(nc.sbuf_tensor("idf_sb", [NT, NT], F32))
        s1c = es.enter_context(nc.sbuf_tensor("s1c", [NT, 1], F32))
        s2c = es.enter_context(nc.sbuf_tensor("s2c", [NT, 1], F32))
        s3c = es.enter_context(nc.sbuf_tensor("s3c", [NT, 1], F32))
        s4c = es.enter_context(nc.sbuf_tensor("s4c", [NT, 1], F32))
        s6c = es.enter_context(nc.sbuf_tensor("s6c", [NT, 1], F32))
        c6c = es.enter_context(nc.sbuf_tensor("c6c", [NT, 1], F32))
        tmpc = es.enter_context(nc.sbuf_tensor("tmpc", [NT, 1], F32))
        wdt = es.enter_context(nc.sbuf_tensor("wdt", [NT, ICH * H], BF16))
        BW = 2 * TQ
        P2b = [es.enter_context(nc.sbuf_tensor(f"P2b{k}", [NT, BW], F32))
               for k in range(2)]
        t6b = [es.enter_context(nc.sbuf_tensor(f"t6b{k}", [NT, BW], F32))
               for k in range(2)]
        d6b = [es.enter_context(nc.sbuf_tensor(f"d6b{k}", [NT, BW], F32))
               for k in range(2)]
        r6i = [es.enter_context(nc.sbuf_tensor(f"r6i{k}", [NT, BW], I16))
               for k in range(2)]
        r6g = [es.enter_context(nc.sbuf_tensor("r6g0", [NT, ICH * TQ], BF16))]
        dch = [es.enter_context(nc.sbuf_tensor(f"dch{k}", [NT, 8 * TQ], F32))
               for k in range(2)]
        prt = [es.enter_context(nc.sbuf_tensor(f"prt{k}", [NT, H], F32))
               for k in range(2)]
        osb = [es.enter_context(nc.sbuf_tensor(f"osb{k}", [NT, H], F32))
               for k in range(2)]
        fsb = [es.enter_context(nc.sbuf_tensor(f"fsb{k}", [NT, H], F32))
               for k in range(2)]
        psd = [es.enter_context(nc.psum_tensor(f"pd{k}", [NT, TQ], F32))
               for k in range(2)]
        pst = [es.enter_context(nc.psum_tensor(f"pt{k}", [NT, NT], F32))
               for k in range(2)]

        t_id = p.dma("mis", lambda e: e.dma_start(
            out=idb_sb[:], in_=idb_c.ap()))
        t_idf = p.dma("mis", lambda e: e.dma_start(
            out=idf_sb[:], in_=idf_c.ap()))
        wv = [mxtok[n] for n in ("m1all", "m23all", "mPall", "mP2all")]
        tv = p.op("v", lambda e: e.tensor_scalar(
            s1c[:], mx1[:, 0:1], INVQ, EPS12, Alu.mult, Alu.max), waits=wv)
        tv = p.op("v", lambda e: e.tensor_tensor(
            tmpc[:], s1c[:], mx23[:, 0:1], Alu.mult), waits=[tv])
        tv = p.op("v", lambda e: e.tensor_scalar(
            s2c[:], tmpc[:], INVQ, EPS12, Alu.mult, Alu.max), waits=[tv])
        tv = p.op("v", lambda e: e.tensor_tensor(
            tmpc[:], s1c[:], mx23[:, 1:2], Alu.mult), waits=[tv])
        tv = p.op("v", lambda e: e.tensor_scalar(
            s3c[:], tmpc[:], INVQ, EPS12, Alu.mult, Alu.max), waits=[tv])
        tv = p.op("v", lambda e: e.tensor_tensor(
            tmpc[:], s3c[:], mxP[:, 0:1], Alu.mult), waits=[tv])
        tv = p.op("v", lambda e: e.tensor_scalar(
            tmpc[:], tmpc[:], INV65536, None, Alu.mult), waits=[tv])
        tv = p.op("v", lambda e: e.tensor_scalar(
            s4c[:], tmpc[:], INVQ, EPS12, Alu.mult, Alu.max), waits=[tv])
        tv = p.op("v", lambda e: e.tensor_tensor(
            tmpc[:], s4c[:], s2c[:], Alu.mult), waits=[tv])
        tv = p.op("v", lambda e: e.tensor_tensor(
            tmpc[:], tmpc[:], mxP2[:, 0:1], Alu.mult), waits=[tv])
        t_s6 = p.op("v", lambda e: e.tensor_scalar(
            s6c[:], tmpc[:], INVQ, EPS12, Alu.mult, Alu.max), waits=[tv])
        tv = p.op("v", lambda e: e.reciprocal(c6c[:], mxP2[:, 0:1]),
                  waits=[t_s6])
        t_c6 = p.op("v", lambda e: e.tensor_scalar(
            c6c[:], c6c[:], Q15, None, Alu.mult), waits=[tv])

        tok_wd = dequant_weights(nc, es, p, [(wd_d, H, I3, wdt)], idb_sb, "d")

        tok_mm = [[None] * 8 for _ in range(ngrp)]
        tok_dch = [[None] * 8 for _ in range(ngrp)]
        tok_out = [[None] * 4 for _ in range(ngrp)]
        tok_stf = [[None] * 4 for _ in range(ngrp)]
        tok_lasttr = [None] * ngrp
        tok_p2read = [None, None]
        tok_su6 = [None, None]
        tok_r6w = [None, None]

        for q in range(ngrp):
            lastv = None
            for blk in range(ICH // 2):
                b = blk % 2
                ii0 = blk * 2
                w = [t_c6, tok_p2read[b]]
                if q >= 1 and blk == 0:
                    w.append(tok_mm[q - 1][7])
                tl = p.dma(f"inp{b}", lambda e, q=q, ii0=ii0, b=b: e.dma_start(
                    out=P2b[b][:].rearrange("p (k t) -> p k t", t=TQ),
                    in_=P2_d.ap()[ii0 * NT:(ii0 + 2) * NT,
                                  q * TQ:(q + 1) * TQ].rearrange(
                        "(k p) t -> p k t", p=NT)), waits=w)
                ta = p.op("a", lambda e, b=b: e.activation(
                    t6b[b][:], P2b[b][:], Act.Identity, bias=0.0,
                    scale=c6c[:]),
                    waits=[tl, t_c6, tok_r6w[b]])
                tok_p2read[b] = ta
                ta2 = p.op("a", lambda e, b=b: e.activation(
                    r6i[b][:], t6b[b][:], Act.Identity),
                    waits=[ta, tok_su6[b]])
                tg = p.op("gp", lambda e, b=b: e.tensor_tensor(
                    d6b[b][:], t6b[b][:], r6i[b][:], Alu.subtract),
                    waits=[ta2, tok_r6w[b]])
                tsu = p.op("v", lambda e, b=b: e.scalar_tensor_tensor(
                    t6b[b][:], d6b[b][:], 0.5, r6i[b][:], Alu.is_ge, Alu.add),
                    waits=[tg])
                tok_su6[b] = tsu
                wq = [tsu]
                if q >= 1 and blk == 0:
                    wq.append(tok_mm[q - 1][7])
                lastv = p.op("v", lambda e, q=q, ii0=ii0, b=b:
                             e.scalar_tensor_tensor(
                    r6g[0][:, ii0 * TQ:(ii0 + 2) * TQ], d6b[b][:], -0.5,
                    t6b[b][:], Alu.is_le, Alu.subtract), waits=wq)
                tok_r6w[b] = lastv
            tok_r6 = lastv
            for hh in range(8):
                ps = psd[hh % 2]
                wm = [tok_r6, tok_wd]
                if hh >= 2:
                    wm.append(tok_dch[q][hh - 2])
                elif q >= 1:
                    wm.append(tok_dch[q - 1][6 + hh])
                last = None
                for ii in range(ICH):
                    last = p.op("t", lambda e, ps=ps, hh=hh, ii=ii, q=q:
                                e.matmul(
                                    ps[:],
                                    wdt[:, ii * H + hh * NT:
                                        ii * H + (hh + 1) * NT],
                                    r6g[0][:, ii * TQ:(ii + 1) * TQ],
                                    start=(ii == 0), stop=(ii == ICH - 1)),
                                waits=wm, quiet=(ii != ICH - 1))
                    wm = []
                tok_mm[q][hh] = last
                we = [last, t_s6]
                if q >= 2 and hh == 0:
                    we.append(tok_lasttr[q - 2])
                tok_dch[q][hh] = p.op(
                    "a", lambda e, ps=ps, q=q, hh=hh: e.activation(
                        dch[q % 2][:, hh * TQ:(hh + 1) * TQ], ps[:],
                        Act.Identity, bias=0.0, scale=s6c[:]), waits=we)
            prev_lasta = None
            for tt in range(4):
                wl = []
                if tt >= 2:
                    wl.append(tok_stf[q][tt - 2])
                elif q >= 1:
                    wl.append(tok_stf[q - 1][2 + tt])
                tok_ld = p.dma(f"inq{tt % 2}", lambda e, q=q, tt=tt:
                               e.dma_start(
                    out=prt[tt % 2][:],
                    in_=part_d.ap()[q * TQ + tt * NT:q * TQ + (tt + 1) * NT, :]),
                    waits=wl)
                lasta = None
                for hh in range(8):
                    pt_ = pst[hh % 2]
                    wt = [tok_dch[q][hh], t_idf, prev_lasta]
                    if hh >= 2:
                        wt.append(lasta)
                    if hh == 0:
                        if tt >= 2:
                            wt.append(tok_out[q][tt - 2])
                        elif q >= 1:
                            wt.append(tok_out[q - 1][2 + tt])
                    ttr = p.op("t", lambda e, pt_=pt_, q=q, hh=hh, tt=tt:
                               e.transpose(
                                   pt_[:],
                                   dch[q % 2][:, hh * TQ + tt * NT:
                                              hh * TQ + (tt + 1) * NT],
                                   idf_sb[:]), waits=wt)
                    lasta = p.op("a", lambda e, pt_=pt_, tt=tt, hh=hh:
                                 e.activation(
                                     osb[tt % 2][:, hh * NT:(hh + 1) * NT],
                                     pt_[:], Act.Identity), waits=[ttr])
                prev_lasta = lasta
                if tt == 3:
                    tok_lasttr[q] = lasta
                wf = [lasta, tok_ld]
                if tt >= 2:
                    wf.append(tok_stf[q][tt - 2])
                elif q >= 1:
                    wf.append(tok_stf[q - 1][2 + tt])
                tf = p.op("v", lambda e, tt=tt: e.tensor_tensor(
                    fsb[tt % 2][:], osb[tt % 2][:], prt[tt % 2][:], Alu.add),
                    waits=wf)
                tok_out[q][tt] = tf
                tok_stf[q][tt] = p.dma(f"sou{tt % 2}", lambda e, q=q, tt=tt:
                                       e.dma_start(
                    out=out_d.ap()[q * TQ + tt * NT:q * TQ + (tt + 1) * NT, :],
                    in_=fsb[tt % 2][:]), waits=[tf], eng="gp")

        p.barrier_all()
        p.emit_block(nc, es)
    return nc



def _gather_max(results, key):
    m = np.stack([np.asarray(r[key], np.float32).reshape(NMAX)
                  for r in results])
    return np.ascontiguousarray(m.reshape(1, 8 * NMAX))


def kernel(L0_attn_x, L0_attn_filt, L0_ffn_w_gate, L0_ffn_w_up, L0_ffn_w_down):
    x = np.ascontiguousarray(np.asarray(L0_attn_x, np.float32))
    filt = np.ascontiguousarray(np.asarray(L0_attn_filt, np.float32))
    wg = np.ascontiguousarray(np.asarray(L0_ffn_w_gate, np.float32))
    wu = np.ascontiguousarray(np.asarray(L0_ffn_w_up, np.float32))
    wd = np.ascontiguousarray(np.asarray(L0_ffn_w_down, np.float32))
    B, S, _ = x.shape
    T = S
    cores = list(range(NCORES))
    Fz = build_toeplitz(filt)

    r1 = run_bass_kernel_spmd(
        build_L1(T), [{"xs": x[c], "Fz": Fz} for c in range(NCORES)], cores)
    m1all = _gather_max(r1.results, "m1loc")

    r2 = run_bass_kernel_spmd(
        build_L2(T),
        [{"xh": np.asarray(r["xh"]), "xl": np.asarray(r["xl"]),
          "wu": wu, "wg": wg, "m1all": m1all} for r in r1.results], cores)
    m23all = _gather_max(r2.results, "m23loc")

    r3 = run_bass_kernel_spmd(
        build_L3(T),
        [{"gt": np.asarray(r["gt"]), "m1all": m1all, "m23all": m23all}
         for r in r2.results], cores)
    mPall = _gather_max(r3.results, "mPloc")

    r4 = run_bass_kernel_spmd(
        build_L4(T),
        [{"ut": np.asarray(r2.results[c]["ut"]),
          "P": np.asarray(r3.results[c]["P"]),
          "m23all": m23all, "mPall": mPall} for c in range(NCORES)], cores)
    mP2all = _gather_max(r4.results, "mP2loc")

    r5 = run_bass_kernel_spmd(
        build_L5(T),
        [{"P2": np.asarray(r4.results[c]["P2"]), "wd": wd,
          "partial": np.asarray(r1.results[c]["partial"]),
          "m1all": m1all, "m23all": m23all, "mPall": mPall,
          "mP2all": mP2all} for c in range(NCORES)], cores)

    out = np.stack([np.asarray(r["out"], np.float32) for r in r5.results])
    return np.ascontiguousarray(out)


# version 07
# speedup vs baseline: 1.6914x; 1.4084x over previous
import contextlib

import numpy as np
import ml_dtypes

import concourse.bass as bass
import concourse.mybir as mybir
from concourse.bass_utils import run_bass_kernel_spmd

F32 = mybir.dt.float32
BF16 = mybir.dt.bfloat16
I16 = mybir.dt.int16
U16 = mybir.dt.uint16
Alu = mybir.AluOpType
Act = mybir.ActivationFunctionType
AX = mybir.AxisListType

NCORES = 8
H = 1024
I3 = 3072
NT = 128
LF = 2048
ND = 17
RING = 18
NMAX = 16
TQ = 512
CH = 2048
INV7 = float(np.float32(1.0 / 7.0))
INVQ = float(np.float32(1.0 / 32767.0))
Q15 = 32767.0
Q16 = 65536.0
INV65536 = float(np.float32(1.0 / 65536.0))
INVH = float(np.float32(1.0 / H))
EPS12 = 1e-12
MAGIC = 12582912.0

BF = ml_dtypes.bfloat16


def build_toeplitz(filt):
    out = np.zeros((ND, NT, NT), np.float32)
    j, i = np.meshgrid(np.arange(NT), np.arange(NT), indexing="ij")
    for d in range(ND):
        idx = NT * d + j - i
        m = (idx >= 0) & (idx < LF)
        out[d][m] = filt[idx[m]]
    return np.ascontiguousarray(out.reshape(ND * NT, NT))


class Plan:

    ENGS = ("sy", "gp", "v", "a", "t")

    def __init__(self):
        self.ops = {k: [] for k in self.ENGS}
        self.count = {}
        for k in ("v", "a", "t"):
            self.count[k] = 0
        self.lastw = {k: {} for k in self.ENGS}
        self.chan_eng = {}
        self.chan_last = {}

    def _record(self, eng, sem, emit, waits, inc):
        seen = {}
        for tok in waits:
            if tok is None:
                continue
            s, v = tok
            if v <= 0 or self.lastw[eng].get(s, 0) >= v:
                continue
            seen[s] = max(seen.get(s, 0), v)
        w = []
        for s, v in seen.items():
            w.append((s, v))
            self.lastw[eng][s] = v
        self.count[sem] = self.count.get(sem, 0) + inc
        self.ops[eng].append((w, emit, sem, inc))
        return (sem, self.count[sem])

    def op(self, eng, emit, waits=(), quiet=False):
        if quiet:
            return self._record(eng, eng, emit, waits, 0)
        return self._record(eng, eng, emit, waits, 1)

    def dma(self, chan, emit, waits=(), eng="sy"):
        if chan not in self.chan_eng:
            self.chan_eng[chan] = eng
        eng = self.chan_eng[chan]
        w = list(waits) + [self.chan_last.get(chan)]
        tok = self._record(eng, chan, emit, w, 16)
        self.chan_last[chan] = tok
        return tok

    def barrier_all(self, eng="gp"):
        waits = [(s, c) for s, c in self.count.items() if c > 0]
        self._record(eng, eng if eng in self.count else "v",
                     lambda e: None, waits, 0)

    def emit_block(self, nc, es):
        sems = {s: es.enter_context(nc.semaphore("sem_" + s))
                for s in self.count}
        with nc.Block() as block:
            def runner(engname):
                def run(engobj):
                    for (waits, emit, sem, inc) in self.ops[engname]:
                        for (s, v) in waits:
                            engobj.wait_ge(sems[s], v)
                        ins = emit(engobj)
                        if ins is not None and inc > 0:
                            ins.then_inc(sems[sem], inc)
                return run
            if self.ops["sy"]:
                block.sync(runner("sy"))
            if self.ops["gp"]:
                block.gpsimd(runner("gp"))
            if self.ops["v"]:
                block.vector(runner("v"))
            if self.ops["a"]:
                block.scalar(runner("a"))
            if self.ops["t"]:
                block.tensor(runner("t"))


def maxes_prologue(nc, es, p, names):
    tiles, toks = {}, {}
    for n in names:
        inp = nc.dram_tensor(n, [1, 8 * NMAX], F32, kind="ExternalInput")
        scr = nc.dram_tensor("scr_" + n, [1, NMAX], F32, kind="Internal")
        mh = es.enter_context(nc.sbuf_tensor("mh_" + n, [1, 8 * NMAX], F32))
        m4 = es.enter_context(nc.sbuf_tensor("m4_" + n, [1, 4 * NMAX], F32))
        m2 = es.enter_context(nc.sbuf_tensor("m2_" + n, [1, 2 * NMAX], F32))
        m1 = es.enter_context(nc.sbuf_tensor("m1_" + n, [1, NMAX], F32))
        mx = es.enter_context(nc.sbuf_tensor("mx_" + n, [NT, NMAX], F32))
        t = p.dma("mis", lambda e, inp=inp, mh=mh: e.dma_start(
            out=mh[:], in_=inp.ap()))
        t = p.op("v", lambda e, mh=mh, m4=m4: e.tensor_tensor(
            m4[:], mh[:, :4 * NMAX], mh[:, 4 * NMAX:], Alu.max), waits=[t])
        t = p.op("v", lambda e, m4=m4, m2=m2: e.tensor_tensor(
            m2[:], m4[:, :2 * NMAX], m4[:, 2 * NMAX:], Alu.max), waits=[t])
        t = p.op("v", lambda e, m2=m2, m1=m1: e.tensor_tensor(
            m1[:], m2[:, :NMAX], m2[:, NMAX:], Alu.max), waits=[t])
        t = p.dma("mis", lambda e, scr=scr, m1=m1: e.dma_start(
            out=scr.ap(), in_=m1[:]), waits=[t])
        t = p.dma("mis", lambda e, mx=mx, scr=scr: e.dma_start(
            out=mx[:], in_=scr.ap().broadcast_to([NT, NMAX])), waits=[t])
        tiles[n] = mx
        toks[n] = t
    return tiles, toks


def max_finalize(nc, es, p, maxacc, idf_sb, out_name, extra_waits=()):
    mout = nc.dram_tensor(out_name, [NMAX, 1], F32, kind="ExternalOutput")
    psf = es.enter_context(nc.psum_tensor("ps_mfin", [NT, NT], F32))
    mfs = es.enter_context(nc.sbuf_tensor("mfin_s", [NMAX, NT], F32))
    mfr = es.enter_context(nc.sbuf_tensor("mfin_r", [NMAX, 1], F32))
    w = list(extra_waits) + [("v", p.count["v"])]
    t = p.op("t", lambda e: e.transpose(psf[:NMAX, :], maxacc[:], idf_sb[:]),
             waits=w)
    t = p.op("v", lambda e: e.tensor_copy(mfs[:], psf[:NMAX, :]), waits=[t])
    t = p.op("v", lambda e: e.tensor_reduce(mfr[:], mfs[:], AX.X, Alu.max),
             waits=[t])
    return p.dma("mis", lambda e: e.dma_start(out=mout.ap(), in_=mfr[:]),
                 waits=[t])



def plan_round_signed(p, t_tok, tbuf, ibuf, dbuf, out_emit):
    tk = p.op("v", lambda e: e.tensor_copy(ibuf, tbuf), waits=[t_tok])
    tk = p.op("v", lambda e: e.tensor_tensor(dbuf, tbuf, ibuf, Alu.subtract),
              waits=[tk])
    tk = p.op("v", lambda e: e.scalar_tensor_tensor(
        tbuf, dbuf, 0.5, ibuf, Alu.is_ge, Alu.add), waits=[tk])
    tk = p.op("v", lambda e: out_emit(e, dbuf, tbuf), waits=[tk])
    return tk


def plan_round_u16(p, t_tok, tbuf, ubuf, dbuf, out, clip_hi):
    tk = p.op("v", lambda e: e.tensor_copy(ubuf, tbuf), waits=[t_tok])
    tk = p.op("v", lambda e: e.tensor_tensor(dbuf, tbuf, ubuf, Alu.subtract),
              waits=[tk])
    tk = p.op("v", lambda e: e.scalar_tensor_tensor(
        tbuf, dbuf, 0.5, ubuf, Alu.is_ge, Alu.add), waits=[tk])
    tk = p.op("v", lambda e: e.tensor_scalar(
        out, tbuf, float(clip_hi), None, Alu.min), waits=[tk])
    return tk


def dequant_weights(nc, es, p, specs, idb_sb, tag, psw=None):
    CT = 1024
    wbuf = [es.enter_context(nc.sbuf_tensor(f"wb{tag}{k}", [NT, CT], F32))
            for k in range(2)]
    wqb = [es.enter_context(nc.sbuf_tensor(f"wq{tag}{k}", [NT, CT], BF16))
           for k in range(2)]
    absr = es.enter_context(nc.sbuf_tensor(f"ab{tag}", [NT, CT // 32], F32))
    sblk = es.enter_context(nc.sbuf_tensor(f"sb{tag}", [NT, CT // 32], F32))
    sinv = es.enter_context(nc.sbuf_tensor(f"si{tag}", [NT, CT // 32], F32))
    tqb = es.enter_context(nc.sbuf_tensor(f"tq{tag}", [NT, CT], F32))
    dqb = es.enter_context(nc.sbuf_tensor(f"dq{tag}", [NT, CT], F32))
    kqb = es.enter_context(nc.sbuf_tensor(f"kq{tag}", [NT, CT], I16))
    bCw = es.enter_context(nc.sbuf_tensor(f"bCw{tag}", [NT, 1], F32))
    bNCw = es.enter_context(nc.sbuf_tensor(f"bNCw{tag}", [NT, 1], F32))
    t_bw = p.op("v", lambda e: e.memset(bCw[:], MAGIC))
    t_bw = p.op("v", lambda e: e.memset(bNCw[:], -MAGIC), waits=[t_bw])
    if psw is None:
        psw = [es.enter_context(nc.psum_tensor(f"pw{tag}{k}", [NT, NT], BF16))
               for k in range(2)]
    last_evac = None
    tok_consumed = [None, None]
    tok_wq_read = [None, None]
    nblk = CT // 32
    n = 0
    for (w_dram, nrows, ncols, wt_sb) in specs:
        for ii in range(nrows // NT):
            for cc in range(ncols // CT):
                par = n % 2
                n += 1
                wb, wq = wbuf[par], wqb[par]
                tl = p.dma("wld", lambda e, wb=wb, w_dram=w_dram, ii=ii, cc=cc:
                           e.dma_start(
                               out=wb[:],
                               in_=w_dram.ap()[ii * NT:(ii + 1) * NT,
                                               cc * CT:(cc + 1) * CT]),
                           waits=[tok_consumed[par]])
                tv = p.op("v", lambda e, wb=wb: e.tensor_reduce(
                    absr[:].rearrange("p (b o) -> p b o", o=1),
                    wb[:].rearrange("p (b e) -> p b e", e=32),
                    AX.X, Alu.max, apply_absolute_value=True), waits=[tl])
                tv = p.op("v", lambda e: e.tensor_scalar(
                    sblk[:], absr[:], INV7, EPS12, Alu.mult, Alu.max),
                    waits=[tv])
                tv = p.op("v", lambda e: e.reciprocal(sinv[:], sblk[:]),
                          waits=[tv])
                tv = p.op("v", lambda e: e.tensor_scalar(
                    sinv[:], sinv[:], -1.0, None, Alu.mult), waits=[tv])
                tv = p.op("v", lambda e, wb=wb: e.tensor_tensor(
                    tqb[:].rearrange("p (b e) -> p b e", e=32),
                    wb[:].rearrange("p (b e) -> p b e", e=32),
                    sinv[:].rearrange("p (b o) -> p b o", o=1)
                    .broadcast_to([NT, nblk, 32]), Alu.mult), waits=[tv])
                tok_consumed[par] = tv
                tv = p.op("a", lambda e: e.activation(
                    dqb[:], tqb[:], Act.Identity, bias=bCw[:], scale=1.0),
                    waits=[tv, t_bw])
                tv = p.op("a", lambda e: e.activation(
                    kqb[:], dqb[:], Act.Identity, bias=bNCw[:], scale=1.0),
                    waits=[tv])
                tv = p.op("v", lambda e, wq=wq: e.tensor_tensor(
                    wq[:].rearrange("p (b e) -> p b e", e=32),
                    kqb[:].rearrange("p (b e) -> p b e", e=32),
                    sblk[:].rearrange("p (b o) -> p b o", o=1)
                    .broadcast_to([NT, nblk, 32]), Alu.mult),
                    waits=[tv, tok_wq_read[par]])
                for ci in range(CT // NT):
                    cig = cc * (CT // NT) + ci
                    ps = psw[ci % len(psw)]
                    ttr = p.op("t", lambda e, ps=ps, wq=wq, ci=ci: e.transpose(
                        ps[:], wq[:, ci * NT:(ci + 1) * NT], idb_sb[:]),
                        waits=[tv, last_evac])
                    last_evac = p.op(
                        "a", lambda e, ps=ps, wt_sb=wt_sb, cig=cig, ii=ii,
                        nrows=nrows: e.activation(
                            wt_sb[:, cig * nrows + ii * NT:
                                  cig * nrows + (ii + 1) * NT],
                            ps[:], Act.Identity), waits=[ttr])
                tok_wq_read[par] = last_evac
    return last_evac



def build_L1(T):
    ntiles = T // NT
    nc = bass.Bass("TRN2", target_bir_lowering=False)
    xs = nc.dram_tensor("xs", [T, H], F32, kind="ExternalInput")
    Fz = nc.dram_tensor("Fz", [ND * NT, NT], F32, kind="ExternalInput")
    xh_d = nc.dram_tensor("xh", [T, H], BF16, kind="ExternalOutput")
    xl_d = nc.dram_tensor("xl", [T, H], BF16, kind="ExternalOutput")
    part_d = nc.dram_tensor("partial", [T, H], F32, kind="ExternalOutput")
    idf_c = nc.inline_tensor(np.eye(NT, dtype=np.float32), name="idf")

    p = Plan()
    es = contextlib.ExitStack()
    with es:
        xr = es.enter_context(nc.sbuf_tensor("xr", [NT, RING * H], F32))
        ff = es.enter_context(nc.sbuf_tensor("ff", [NT, ND * NT], F32))
        fh = es.enter_context(nc.sbuf_tensor("fh", [NT, ND * NT], BF16))
        fl = es.enter_context(nc.sbuf_tensor("fl", [NT, ND * NT], BF16))
        xhr = es.enter_context(nc.sbuf_tensor("xhr", [NT, RING * H], BF16))
        xlr = es.enter_context(nc.sbuf_tensor("xlr", [NT, RING * H], BF16))
        xn = es.enter_context(nc.sbuf_tensor("xn", [NT, H], F32))
        dump = es.enter_context(nc.sbuf_tensor("dump", [NT, H], F32))
        smv = es.enter_context(nc.sbuf_tensor("smv", [NT, 1], F32))
        sqv = es.enter_context(nc.sbuf_tensor("sqv", [NT, 1], F32))
        muv = es.enter_context(nc.sbuf_tensor("muv", [NT, 1], F32))
        ex2 = es.enter_context(nc.sbuf_tensor("ex2", [NT, 1], F32))
        mu2 = es.enter_context(nc.sbuf_tensor("mu2", [NT, 1], F32))
        varv = es.enter_context(nc.sbuf_tensor("varv", [NT, 1], F32))
        rrv = es.enter_context(nc.sbuf_tensor("rrv", [NT, 1], F32))
        rsv = es.enter_context(nc.sbuf_tensor("rsv", [NT, 1], F32))
        mtv = es.enter_context(nc.sbuf_tensor("mtv", [NT, 1], F32))
        maxacc = es.enter_context(nc.sbuf_tensor("maxacc", [NT, NMAX], F32))
        idf_sb = es.enter_context(nc.sbuf_tensor("idf_sb", [NT, NT], F32))
        parts = [es.enter_context(nc.sbuf_tensor(f"part{k}", [NT, H], F32))
                 for k in range(2)]
        psc = [es.enter_context(nc.psum_tensor(f"psc{k}", [NT, H], F32))
               for k in range(3)]

        t_id = p.dma("mis", lambda e: e.dma_start(
            out=idf_sb[:], in_=idf_c.ap()))
        t_f = p.dma("mis", lambda e: e.dma_start(
            out=ff[:].rearrange("p (d i) -> p d i", i=NT),
            in_=Fz.ap().rearrange("(d j) i -> j d i", j=NT)))
        t_fh = p.op("v", lambda e: e.tensor_copy(fh[:], ff[:]), waits=[t_f])
        t_fl = p.op("v", lambda e: e.tensor_tensor(
            fl[:], ff[:], fh[:], Alu.subtract), waits=[t_fh])
        t_mz = p.op("v", lambda e: e.memset(maxacc[:], 0.0))

        tok_load = [None] * ntiles
        tok_xl = [None] * ntiles
        tok_store_hl = [None] * ntiles
        tok_evac = [None] * ntiles
        tok_store = [None] * ntiles
        tok_statsfree = [None] * ntiles

        def emit_stats(i):
            sl = (i % RING) * H
            xb = xr[:, sl:sl + H]
            w = [tok_load[i]]
            if i >= 1:
                w.append(tok_statsfree[i - 1])
            ta = p.op("a", lambda e, xb=xb: e.activation(
                dump[:], xb, Act.Square, accum_out=sqv[:]), waits=w)
            ta = p.op("a", lambda e, xb=xb: e.activation(
                dump[:], xb, Act.Identity, accum_out=smv[:]), waits=[ta])
            tv = p.op("v", lambda e: e.tensor_scalar(
                muv[:], smv[:], INVH, None, Alu.mult), waits=[ta])
            tv = p.op("v", lambda e: e.tensor_scalar(
                ex2[:], sqv[:], INVH, None, Alu.mult), waits=[tv])
            tok_statsfree[i] = tv
            tv = p.op("v", lambda e: e.tensor_tensor(
                mu2[:], muv[:], muv[:], Alu.mult), waits=[tv])
            tv = p.op("v", lambda e: e.scalar_tensor_tensor(
                varv[:], ex2[:], 1e-5, mu2[:], Alu.add, Alu.subtract),
                waits=[tv])
            ta2 = p.op("a", lambda e: e.activation(
                rrv[:], varv[:], Act.Sqrt), waits=[tv])
            tv = p.op("v", lambda e: e.reciprocal(rsv[:], rrv[:]), waits=[ta2])
            tv = p.op("v", lambda e, xb=xb: e.scalar_tensor_tensor(
                xn[:], xb, muv[:], rsv[:].broadcast_to([NT, H]),
                Alu.subtract, Alu.mult), waits=[tv])
            tv = p.op("v", lambda e: e.tensor_reduce(
                mtv[:], xn[:], AX.X, Alu.max, apply_absolute_value=True),
                waits=[tv])
            tv = p.op("v", lambda e: e.tensor_tensor(
                maxacc[:, 0:1], maxacc[:, 0:1], mtv[:], Alu.max),
                waits=[tv, t_mz])
            wh = [tv, t_fl]
            if i >= RING:
                wh.extend(tok_store_hl[i - RING])
            th = p.op("v", lambda e, sl=sl: e.tensor_copy(
                xhr[:, sl:sl + H], xn[:]), waits=wh)
            tl = p.op("v", lambda e, sl=sl: e.tensor_tensor(
                xlr[:, sl:sl + H], xn[:], xhr[:, sl:sl + H], Alu.subtract),
                waits=[th])
            tok_xl[i] = tl
            t1 = p.dma("sxh", lambda e, sl=sl, i=i: e.dma_start(
                out=xh_d.ap()[i * NT:(i + 1) * NT, :], in_=xhr[:, sl:sl + H]),
                waits=[tl], eng="gp")
            t2 = p.dma("sxl", lambda e, sl=sl, i=i: e.dma_start(
                out=xl_d.ap()[i * NT:(i + 1) * NT, :], in_=xlr[:, sl:sl + H]),
                waits=[tl], eng="gp")
            tok_store_hl[i] = (t1, t2)

        def emit_conv(j):
            ps = psc[j % 3]
            dmax = min(ND - 1, ntiles - 1 - j)
            w = [tok_xl[j + dmax], t_fl]
            if j >= 3:
                w.append(tok_evac[j - 3])
            n_terms = 3 * (dmax + 1)
            cnt = 0
            first = [True, True]
            last = None
            for d in range(dmax + 1):
                sl = ((j + d) % RING) * H
                for (fm, xm) in ((fh, xhr), (fl, xhr), (fh, xlr)):
                    cnt += 1
                    stop = (cnt == n_terms)
                    for half in range(2):
                        cs = slice(half * 512, half * 512 + 512)
                        st = first[half]
                        first[half] = False
                        is_last = (cnt == n_terms and half == 1)
                        last = p.op("t", lambda e, ps=ps, fm=fm, xm=xm, d=d,
                                    sl=sl, cs=cs, st=st, stop=stop: e.matmul(
                                        ps[:, cs], fm[:, d * NT:(d + 1) * NT],
                                        xm[:, sl + cs.start:sl + cs.stop],
                                        start=st, stop=stop), waits=w,
                                    quiet=not is_last)
                        w = []
            sl = (j % RING) * H
            pb = parts[j % 2]
            we = [last, tok_load[j]]
            if j >= 2:
                we.append(tok_store[j - 2])
            te = p.op("v", lambda e, ps=ps, sl=sl, pb=pb: e.tensor_tensor(
                pb[:], ps[:], xr[:, sl:sl + H], Alu.add), waits=we)
            tok_evac[j] = te
            tok_store[j] = p.dma("spt", lambda e, pb=pb, j=j: e.dma_start(
                out=part_d.ap()[j * NT:(j + 1) * NT, :], in_=pb[:]),
                waits=[te], eng="gp")

        for i in range(ntiles):
            w = []
            if i >= RING:
                w.append(tok_evac[i - RING])
            tok_load[i] = p.dma(f"xi{i % 2}", lambda e, i=i: e.dma_start(
                out=xr[:, (i % RING) * H:(i % RING) * H + H],
                in_=xs.ap()[i * NT:(i + 1) * NT, :]), waits=w)
            emit_stats(i)
            if i >= ND - 1:
                emit_conv(i - (ND - 1))
        for j in range(max(0, ntiles - (ND - 1)), ntiles):
            emit_conv(j)

        max_finalize(nc, es, p, maxacc, idf_sb, "m1loc", [t_id])
        p.barrier_all()
        p.emit_block(nc, es)
    return nc



def build_L2(T):
    ngrp = T // TQ
    ICH = I3 // NT
    nc = bass.Bass("TRN2", target_bir_lowering=False)
    xh_d = nc.dram_tensor("xh", [T, H], BF16, kind="ExternalInput")
    xl_d = nc.dram_tensor("xl", [T, H], BF16, kind="ExternalInput")
    wu_d = nc.dram_tensor("wu", [I3, H], F32, kind="ExternalInput")
    wg_d = nc.dram_tensor("wg", [I3, H], F32, kind="ExternalInput")
    ut_d = nc.dram_tensor("ut", [I3, T], F32, kind="ExternalOutput")
    gt_d = nc.dram_tensor("gt", [I3, T], F32, kind="ExternalOutput")
    idb_c = nc.inline_tensor(np.eye(NT, dtype=BF), name="idb")
    idf_c = nc.inline_tensor(np.eye(NT, dtype=np.float32), name="idf")

    p = Plan()
    es = contextlib.ExitStack()
    with es:
        mxt, mxtok = maxes_prologue(nc, es, p, ["m1all"])
        mx1 = mxt["m1all"]
        idb_sb = es.enter_context(nc.sbuf_tensor("idb_sb", [NT, NT], BF16))
        idf_sb = es.enter_context(nc.sbuf_tensor("idf_sb", [NT, NT], F32))
        s1c = es.enter_context(nc.sbuf_tensor("s1c", [NT, 1], F32))
        c1c = es.enter_context(nc.sbuf_tensor("c1c", [NT, 1], F32))
        wut = es.enter_context(nc.sbuf_tensor("wut", [NT, 8 * I3], BF16))
        wgt = es.enter_context(nc.sbuf_tensor("wgt", [NT, 8 * I3], BF16))
        xhg = [es.enter_context(nc.sbuf_tensor(f"xhg{k}", [NT, 4 * H], BF16))
               for k in range(2)]
        xlg = [es.enter_context(nc.sbuf_tensor(f"xlg{k}", [NT, 4 * H], BF16))
               for k in range(2)]
        xqt = [es.enter_context(nc.sbuf_tensor(f"xqt{k}", [NT, 8 * TQ], BF16))
               for k in range(2)]
        xnb = es.enter_context(nc.sbuf_tensor("xnb", [NT, H], F32))
        t1b = es.enter_context(nc.sbuf_tensor("t1b", [NT, H], F32))
        r1i = es.enter_context(nc.sbuf_tensor("r1i", [NT, H], I16))
        bC = es.enter_context(nc.sbuf_tensor("bC", [NT, 1], F32))
        bNC = es.enter_context(nc.sbuf_tensor("bNC", [NT, 1], F32))
        r1f = [es.enter_context(nc.sbuf_tensor(f"r1f{k}", [NT, H], BF16))
               for k in range(2)]
        ue = [es.enter_context(nc.sbuf_tensor(f"ue{k}", [NT, TQ], F32))
              for k in range(2)]
        ge = [es.enter_context(nc.sbuf_tensor(f"ge{k}", [NT, TQ], F32))
              for k in range(2)]
        nch_mm = (T // TQ) * (I3 // NT)
        urd = es.enter_context(nc.sbuf_tensor("urd", [NT, nch_mm], F32))
        grd = es.enter_context(nc.sbuf_tensor("grd", [NT, nch_mm], F32))
        maxacc = es.enter_context(nc.sbuf_tensor("maxacc", [NT, NMAX], F32))
        psu = [es.enter_context(nc.psum_tensor(f"pu{k}", [NT, TQ], F32))
               for k in range(3)]
        psg = [es.enter_context(nc.psum_tensor(f"pg{k}", [NT, TQ], F32))
               for k in range(3)]
        psx = [es.enter_context(nc.psum_tensor(f"px{k}", [NT, NT], BF16))
               for k in range(1)]

        t_id = p.dma("mis", lambda e: e.dma_start(
            out=idb_sb[:], in_=idb_c.ap()))
        t_idf = p.dma("mis", lambda e: e.dma_start(
            out=idf_sb[:], in_=idf_c.ap()))
        t_mz = p.op("v", lambda e: e.memset(maxacc[:], 0.0))
        t_mz = p.op("v", lambda e: e.memset(bC[:], MAGIC), waits=[t_mz])
        t_mz = p.op("v", lambda e: e.memset(bNC[:], -MAGIC), waits=[t_mz])
        tv = p.op("v", lambda e: e.tensor_scalar(
            s1c[:], mx1[:, 0:1], INVQ, EPS12, Alu.mult, Alu.max),
            waits=[mxtok["m1all"]])
        t_c1 = p.op("v", lambda e: e.reciprocal(c1c[:], s1c[:]),
                    waits=[tv, t_mz])

        tok_w = dequant_weights(
            nc, es, p, [(wu_d, I3, H, wut), (wg_d, I3, H, wgt)], idb_sb, "w",
            psw=psx)

        tok_mm = {"u": [[None] * ICH for _ in range(ngrp)],
                  "g": [[None] * ICH for _ in range(ngrp)]}
        tok_ev = {"u": [[None] * ICH for _ in range(ngrp)],
                  "g": [[None] * ICH for _ in range(ngrp)]}
        tok_st = {"u": [[None] * ICH for _ in range(ngrp)],
                  "g": [[None] * ICH for _ in range(ngrp)]}
        tok_rd = {"u": [[None] * ICH for _ in range(ngrp)],
                  "g": [[None] * ICH for _ in range(ngrp)]}
        tok_rf = [None, None]
        tok_t1 = None
        tok_xqt = [None] * ngrp
        last_xev = tok_w

        for q in range(ngrp):
            w = []
            if q >= 2:
                w += [tok_mm["u"][q - 2][ICH - 1], tok_mm["g"][q - 2][ICH - 1]]
            tl = p.dma(f"xi{q % 2}", lambda e, q=q: e.dma_start(
                out=xhg[q % 2][:].rearrange("p (k h) -> p k h", h=H),
                in_=xh_d.ap()[q * TQ:(q + 1) * TQ, :].rearrange(
                    "(k p) h -> p k h", p=NT)), waits=w)
            tl = p.dma(f"xi{q % 2}", lambda e, q=q: e.dma_start(
                out=xlg[q % 2][:].rearrange("p (k h) -> p k h", h=H),
                in_=xl_d.ap()[q * TQ:(q + 1) * TQ, :].rearrange(
                    "(k p) h -> p k h", p=NT)), waits=[tl])
            lastev = None
            for k in range(4):
                cs = slice(k * H, (k + 1) * H)
                tv = p.op("v", lambda e, q=q, cs=cs: e.tensor_tensor(
                    xnb[:], xhg[q % 2][:, cs], xlg[q % 2][:, cs], Alu.add),
                    waits=[tl, t_c1, tok_t1])
                rf = r1f[k % 2]
                ta1 = p.op("a", lambda e: e.activation(
                    t1b[:], xnb[:], Act.Identity, bias=bC[:], scale=c1c[:]),
                    waits=[tv, tok_rf[k % 2]])
                tok_t1 = ta1
                ta1 = p.op("a", lambda e: e.activation(
                    r1i[:], t1b[:], Act.Identity, bias=bNC[:], scale=1.0),
                    waits=[ta1])
                tv = p.op("a", lambda e, rf=rf: e.activation(
                    rf[:], r1i[:], Act.Identity), waits=[ta1])
                for ci in range(8):
                    ps = psx[0]
                    wt = [tv, t_id, last_xev]
                    if q >= 2 and k == 0 and ci == 0:
                        wt += [tok_mm["u"][q - 2][ICH - 1],
                               tok_mm["g"][q - 2][ICH - 1]]
                    ttr = p.op("t", lambda e, ps=ps, rf=rf, ci=ci: e.transpose(
                        ps[:], rf[:, ci * NT:(ci + 1) * NT], idb_sb[:]),
                        waits=wt)
                    lastev = p.op("a", lambda e, ps=ps, q=q, k=k, ci=ci:
                                  e.activation(
                                      xqt[q % 2][:, ci * TQ + k * NT:
                                                 ci * TQ + (k + 1) * NT],
                                      ps[:], Act.Identity), waits=[ttr])
                    last_xev = lastev
                tok_rf[k % 2] = lastev
            tok_xqt[q] = lastev
            for ii in range(ICH):
                for key, wt_sb, psv, ev, rd, dst, col in (
                    ("u", wut, psu, ue, urd, ut_d, 0),
                    ("g", wgt, psg, ge, grd, gt_d, 1),
                ):
                    ps = psv[ii % 3]
                    wm = [tok_xqt[q], tok_w]
                    if ii >= 3:
                        wm.append(tok_ev[key][q][ii - 3])
                    elif q >= 1:
                        wm.append(tok_ev[key][q - 1][ICH - 3 + ii])
                    last = None
                    for ci in range(8):
                        last = p.op("t", lambda e, ps=ps, wt_sb=wt_sb, ci=ci,
                                    ii=ii, q=q: e.matmul(
                                        ps[:],
                                        wt_sb[:, ci * I3 + ii * NT:
                                              ci * I3 + (ii + 1) * NT],
                                        xqt[q % 2][:, ci * TQ:(ci + 1) * TQ],
                                        start=(ci == 0), stop=(ci == 7)),
                                    waits=wm, quiet=(ci != 7))
                        wm = []
                    tok_mm[key][q][ii] = last
                    eb = ev[ii % 2]
                    we = [last]
                    if ii >= 2:
                        we.append(tok_st[key][q][ii - 2])
                        we.extend(tok_rd[key][q][ii - 2])
                    elif q >= 1:
                        we.append(tok_st[key][q - 1][ICH - 2 + ii])
                        we.extend(tok_rd[key][q - 1][ICH - 2 + ii])
                    tev = p.op("a", lambda e, ps=ps, eb=eb: e.activation(
                        eb[:], ps[:], Act.Identity), waits=we)
                    tok_ev[key][q][ii] = tev
                    idx = q * ICH + ii
                    tn = p.op("v", lambda e, eb=eb, rd=rd, idx=idx:
                              e.tensor_reduce(
                                  rd[:, idx:idx + 1], eb[:], AX.X, Alu.max,
                                  apply_absolute_value=True), waits=[tev])
                    tok_rd[key][q][ii] = (tn,)
                    tok_st[key][q][ii] = p.dma(
                        "s" + key, lambda e, eb=eb, ii=ii, q=q, dst=dst:
                        e.dma_start(out=dst.ap()[ii * NT:(ii + 1) * NT,
                                                 q * TQ:(q + 1) * TQ],
                                    in_=eb[:]), waits=[tev], eng="a")

        wlast = list(tok_rd["u"][ngrp - 1][ICH - 1]) + \
            list(tok_rd["g"][ngrp - 1][ICH - 1])
        tfr = p.op("v", lambda e: e.tensor_reduce(
            maxacc[:, 0:1], urd[:], AX.X, Alu.max, apply_absolute_value=True),
            waits=wlast)
        p.op("v", lambda e: e.tensor_reduce(
            maxacc[:, 1:2], grd[:], AX.X, Alu.max, apply_absolute_value=True),
            waits=[tfr])
        max_finalize(nc, es, p, maxacc, idf_sb, "m23loc", [t_idf])
        p.barrier_all()
        p.emit_block(nc, es)
    return nc



def build_L3(T):
    ch = min(1024, T)
    nc = bass.Bass("TRN2", target_bir_lowering=False)
    gt_d = nc.dram_tensor("gt", [I3, T], F32, kind="ExternalInput")
    P_d = nc.dram_tensor("P", [I3, T], F32, kind="ExternalOutput")
    idf_c = nc.inline_tensor(np.eye(NT, dtype=np.float32), name="idf")

    p = Plan()
    es = contextlib.ExitStack()
    with es:
        mxt, mxtok = maxes_prologue(nc, es, p, ["m1all", "m23all"])
        mx1, mx23 = mxt["m1all"], mxt["m23all"]
        idf_sb = es.enter_context(nc.sbuf_tensor("idf_sb", [NT, NT], F32))
        s1c = es.enter_context(nc.sbuf_tensor("s1c", [NT, 1], F32))
        s3c = es.enter_context(nc.sbuf_tensor("s3c", [NT, 1], F32))
        c3n = es.enter_context(nc.sbuf_tensor("c3n", [NT, 1], F32))
        tmpc = es.enter_context(nc.sbuf_tensor("tmpc", [NT, 1], F32))
        bC = es.enter_context(nc.sbuf_tensor("bC", [NT, 1], F32))
        bNC = es.enter_context(nc.sbuf_tensor("bNC", [NT, 1], F32))
        nchunk = (I3 // NT) * (T // ch)
        ncol = T // ch

        def fam(name, par, dt=F32):
            return [es.enter_context(nc.sbuf_tensor(f"{name}{k}", [NT, ch],
                                                    dt)) for k in range(par)]

        gb = fam("gb", 3)
        t3b = fam("t3b", 3)
        r3b = fam("r3b", 3)
        r3ib = fam("r3ib", 5, I16)
        sgb = fam("sgb", 2)
        t4b = fam("t4b", 3)
        r4b = fam("r4b", 3, U16)
        Pb = fam("Pb", 3)
        rdw = es.enter_context(nc.sbuf_tensor("rdw", [NT, nchunk], F32))
        maxacc = es.enter_context(nc.sbuf_tensor("maxacc", [NT, NMAX], F32))

        t_idf = p.dma("mis", lambda e: e.dma_start(
            out=idf_sb[:], in_=idf_c.ap()))
        t_mz = p.op("v", lambda e: e.memset(maxacc[:], 0.0))
        t_mz = p.op("v", lambda e: e.memset(bC[:], MAGIC), waits=[t_mz])
        t_mz = p.op("v", lambda e: e.memset(bNC[:], -MAGIC), waits=[t_mz])
        wv = [mxtok["m1all"], mxtok["m23all"]]
        tv = p.op("v", lambda e: e.tensor_scalar(
            s1c[:], mx1[:, 0:1], INVQ, EPS12, Alu.mult, Alu.max), waits=wv)
        tv = p.op("v", lambda e: e.tensor_tensor(
            tmpc[:], s1c[:], mx23[:, 1:2], Alu.mult), waits=[tv])
        t_s3 = p.op("v", lambda e: e.tensor_scalar(
            s3c[:], tmpc[:], INVQ, EPS12, Alu.mult, Alu.max), waits=[tv])
        tv = p.op("v", lambda e: e.reciprocal(c3n[:], mx23[:, 1:2]),
                  waits=[t_s3])
        t_c3 = p.op("v", lambda e: e.tensor_scalar(
            c3n[:], c3n[:], -Q15, None, Alu.mult), waits=[tv, t_mz])

        tok = {k: [None] * nchunk for k in
               ("ld", "t3", "r3f", "r3i", "sg", "t4", "rm4", "r4", "P",
                "red", "st")}

        def rc(c):
            ii, qq = divmod(c, ncol)
            return slice(ii * NT, (ii + 1) * NT), slice(qq * ch, (qq + 1) * ch)

        def tk(k, c):
            return tok[k][c] if c >= 0 else None

        def ph_ld(c):
            rows, cols = rc(c)
            tok["ld"][c] = p.dma(f"in{c % 3}", lambda e, c=c, rows=rows,
                                 cols=cols: e.dma_start(
                out=gb[c % 3][:], in_=gt_d.ap()[rows, cols]),
                waits=[tk("t3", c - 3)])

        def ph_A1(c):
            tok["t3"][c] = p.op("a", lambda e, c=c: e.activation(
                t3b[c % 3][:], gb[c % 3][:], Act.Identity, bias=bC[:],
                scale=c3n[:]),
                waits=[tok["ld"][c], t_c3, tk("r3f", c - 3)])

        def ph_V1(c):
            tok["r3f"][c] = p.op("v", lambda e, c=c: e.tensor_scalar(
                r3b[c % 3][:], t3b[c % 3][:], -MAGIC, None, Alu.add),
                waits=[tok["t3"][c], tk("sg", c - 3)])
            tok["r3i"][c] = p.op("v", lambda e, c=c: e.tensor_copy(
                r3ib[c % 5][:], r3b[c % 3][:]),
                waits=[tok["r3f"][c], tk("P", c - 5)])

        def ph_A2(c):
            tok["sg"][c] = p.op("a", lambda e, c=c: e.activation(
                sgb[c % 2][:], r3b[c % 3][:], Act.Sigmoid, bias=0.0,
                scale=s3c[:]),
                waits=[tok["r3f"][c], t_s3, tk("t4", c - 2)])

        def ph_V2(c):
            tok["t4"][c] = p.op("v", lambda e, c=c: e.tensor_scalar(
                t4b[c % 3][:], sgb[c % 2][:], 65536.0, MAGIC, Alu.mult,
                Alu.add), waits=[tok["sg"][c], tk("rm4", c - 3)])
            tok["rm4"][c] = p.op("v", lambda e, c=c: e.tensor_scalar(
                t4b[c % 3][:], t4b[c % 3][:], MAGIC + 65535.0, None, Alu.min),
                waits=[tok["t4"][c], tk("r4", c - 3)])

        def ph_A3(c):
            tok["r4"][c] = p.op("a", lambda e, c=c: e.activation(
                r4b[c % 3][:], t4b[c % 3][:], Act.Identity, bias=bNC[:],
                scale=1.0), waits=[tok["rm4"][c], tk("P", c - 3)])

        def ph_G1(c):
            tok["P"][c] = p.op("gp", lambda e, c=c: e.tensor_tensor(
                Pb[c % 3][:], r3ib[c % 5][:], r4b[c % 3][:], Alu.mult),
                waits=[tok["r4"][c], tok["r3i"][c], tk("st", c - 3),
                       tk("red", c - 3)])

        def ph_V3(c):
            rows, cols = rc(c)
            tok["red"][c] = p.op("v", lambda e, c=c: e.tensor_reduce(
                rdw[:, c:c + 1], Pb[c % 3][:], AX.X, Alu.max,
                apply_absolute_value=True), waits=[tok["P"][c]])
            tok["st"][c] = p.dma(f"sP{c % 3}", lambda e, c=c, rows=rows,
                                 cols=cols: e.dma_start(
                out=P_d.ap()[rows, cols], in_=Pb[c % 3][:]),
                waits=[tok["P"][c]], eng="a")

        phases = [(0, ph_ld), (2, ph_A1), (3, ph_V1), (4, ph_A2),
                  (5, ph_V2), (6, ph_A3), (7, ph_G1), (8, ph_V3)]
        for i in range(nchunk + 8):
            for off, fn in phases:
                c = i - off
                if 0 <= c < nchunk:
                    fn(c)

        p.op("v", lambda e: e.tensor_reduce(
            maxacc[:, 0:1], rdw[:], AX.X, Alu.max, apply_absolute_value=True),
            waits=[t_mz] + [tok["red"][c] for c in range(nchunk)])

        max_finalize(nc, es, p, maxacc, idf_sb, "mPloc", [t_idf])
        p.barrier_all()
        p.emit_block(nc, es)
    return nc



def build_L4(T):
    ch = min(1024, T)
    nc = bass.Bass("TRN2", target_bir_lowering=False)
    ut_d = nc.dram_tensor("ut", [I3, T], F32, kind="ExternalInput")
    P_d = nc.dram_tensor("P", [I3, T], F32, kind="ExternalInput")
    P2_d = nc.dram_tensor("P2", [I3, T], F32, kind="ExternalOutput")
    idf_c = nc.inline_tensor(np.eye(NT, dtype=np.float32), name="idf")

    p = Plan()
    es = contextlib.ExitStack()
    with es:
        mxt, mxtok = maxes_prologue(nc, es, p, ["m23all", "mPall"])
        mx23, mxP = mxt["m23all"], mxt["mPall"]
        idf_sb = es.enter_context(nc.sbuf_tensor("idf_sb", [NT, NT], F32))
        c2c = es.enter_context(nc.sbuf_tensor("c2c", [NT, 1], F32))
        c5c = es.enter_context(nc.sbuf_tensor("c5c", [NT, 1], F32))
        bC = es.enter_context(nc.sbuf_tensor("bC", [NT, 1], F32))
        bNC = es.enter_context(nc.sbuf_tensor("bNC", [NT, 1], F32))
        nchunk = (I3 // NT) * (T // ch)
        ncol = T // ch

        def fam(name, par, dt=F32):
            return [es.enter_context(nc.sbuf_tensor(f"{name}{k}", [NT, ch],
                                                    dt)) for k in range(par)]

        ub = fam("ub", 3)
        Pb = fam("Pb", 3)
        t2b = fam("t2b", 3)
        r2ib = fam("r2ib", 4, I16)
        t5b = fam("t5b", 3)
        r5b = fam("r5b", 3)
        r5ib = fam("r5ib", 3, I16)
        P2b = fam("P2b", 3)
        rdw = es.enter_context(nc.sbuf_tensor("rdw", [NT, nchunk], F32))
        maxacc = es.enter_context(nc.sbuf_tensor("maxacc", [NT, NMAX], F32))

        t_idf = p.dma("mis", lambda e: e.dma_start(
            out=idf_sb[:], in_=idf_c.ap()))
        t_mz = p.op("v", lambda e: e.memset(maxacc[:], 0.0))
        t_mz = p.op("v", lambda e: e.memset(bC[:], MAGIC), waits=[t_mz])
        t_mz = p.op("v", lambda e: e.memset(bNC[:], -MAGIC), waits=[t_mz])
        wv = [mxtok["m23all"], mxtok["mPall"]]
        tv = p.op("v", lambda e: e.reciprocal(c2c[:], mx23[:, 0:1]), waits=wv)
        t_c2 = p.op("v", lambda e: e.tensor_scalar(
            c2c[:], c2c[:], Q15, None, Alu.mult), waits=[tv, t_mz])
        tv = p.op("v", lambda e: e.reciprocal(c5c[:], mxP[:, 0:1]),
                  waits=[t_c2])
        t_c5 = p.op("v", lambda e: e.tensor_scalar(
            c5c[:], c5c[:], Q15, None, Alu.mult), waits=[tv])

        tok = {k: [None] * nchunk for k in
               ("ldu", "ldp", "t2", "r2i", "t5", "r5f", "r5i", "P2",
                "red", "st")}

        def rc(c):
            ii, qq = divmod(c, ncol)
            return slice(ii * NT, (ii + 1) * NT), slice(qq * ch, (qq + 1) * ch)

        def tk(k, c):
            return tok[k][c] if c >= 0 else None

        def ph_ldu(c):
            rows, cols = rc(c)
            tok["ldu"][c] = p.dma(f"inu{c % 3}", lambda e, c=c, rows=rows,
                                  cols=cols: e.dma_start(
                out=ub[c % 3][:], in_=ut_d.ap()[rows, cols]),
                waits=[tk("t2", c - 3)])

        def ph_ldp(c):
            rows, cols = rc(c)
            tok["ldp"][c] = p.dma(f"inp{c % 3}", lambda e, c=c, rows=rows,
                                  cols=cols: e.dma_start(
                out=Pb[c % 3][:], in_=P_d.ap()[rows, cols]),
                waits=[tk("t5", c - 3)])

        def ph_A1(c):
            tok["t2"][c] = p.op("a", lambda e, c=c: e.activation(
                t2b[c % 3][:], ub[c % 3][:], Act.Identity, bias=bC[:],
                scale=c2c[:]),
                waits=[tok["ldu"][c], t_c5, tk("r2i", c - 3)])

        def ph_A2(c):
            tok["r2i"][c] = p.op("a", lambda e, c=c: e.activation(
                r2ib[c % 4][:], t2b[c % 3][:], Act.Identity, bias=bNC[:],
                scale=1.0), waits=[tok["t2"][c], tk("P2", c - 4)])

        def ph_A3(c):
            tok["t5"][c] = p.op("a", lambda e, c=c: e.activation(
                t5b[c % 3][:], Pb[c % 3][:], Act.Identity, bias=bC[:],
                scale=c5c[:]),
                waits=[tok["ldp"][c], t_c5, tk("r5f", c - 3)])

        def ph_V1(c):
            tok["r5f"][c] = p.op("v", lambda e, c=c: e.tensor_scalar(
                r5b[c % 3][:], t5b[c % 3][:], -MAGIC, None, Alu.add),
                waits=[tok["t5"][c], tk("r5i", c - 3)])
            tok["r5i"][c] = p.op("v", lambda e, c=c: e.tensor_copy(
                r5ib[c % 3][:], r5b[c % 3][:]),
                waits=[tok["r5f"][c], tk("P2", c - 3)])

        def ph_G1(c):
            tok["P2"][c] = p.op("gp", lambda e, c=c: e.tensor_tensor(
                P2b[c % 3][:], r5ib[c % 3][:], r2ib[c % 4][:], Alu.mult),
                waits=[tok["r5i"][c], tok["r2i"][c], tk("st", c - 3),
                       tk("red", c - 3)])

        def ph_V3(c):
            rows, cols = rc(c)
            tok["red"][c] = p.op("v", lambda e, c=c: e.tensor_reduce(
                rdw[:, c:c + 1], P2b[c % 3][:], AX.X, Alu.max,
                apply_absolute_value=True), waits=[tok["P2"][c]])
            tok["st"][c] = p.dma(f"sP{c % 3}", lambda e, c=c, rows=rows,
                                 cols=cols: e.dma_start(
                out=P2_d.ap()[rows, cols], in_=P2b[c % 3][:]),
                waits=[tok["P2"][c]], eng="a")

        phases = [(0, ph_ldu), (1, ph_ldp), (2, ph_A1), (3, ph_A2),
                  (3, ph_A3), (4, ph_V1), (5, ph_G1), (6, ph_V3)]
        for i in range(nchunk + 6):
            for off, fn in phases:
                c = i - off
                if 0 <= c < nchunk:
                    fn(c)

        p.op("v", lambda e: e.tensor_reduce(
            maxacc[:, 0:1], rdw[:], AX.X, Alu.max, apply_absolute_value=True),
            waits=[t_mz] + [tok["red"][c] for c in range(nchunk)])

        max_finalize(nc, es, p, maxacc, idf_sb, "mP2loc", [t_idf])
        p.barrier_all()
        p.emit_block(nc, es)
    return nc



def build_L5(T):
    ngrp = T // TQ
    ICH = I3 // NT
    nc = bass.Bass("TRN2", target_bir_lowering=False)
    P2_d = nc.dram_tensor("P2", [I3, T], F32, kind="ExternalInput")
    wd_d = nc.dram_tensor("wd", [H, I3], F32, kind="ExternalInput")
    part_d = nc.dram_tensor("partial", [T, H], F32, kind="ExternalInput")
    out_d = nc.dram_tensor("out", [T, H], F32, kind="ExternalOutput")
    idb_c = nc.inline_tensor(np.eye(NT, dtype=BF), name="idb")
    idf_c = nc.inline_tensor(np.eye(NT, dtype=np.float32), name="idf")

    p = Plan()
    es = contextlib.ExitStack()
    with es:
        mxt, mxtok = maxes_prologue(
            nc, es, p, ["m1all", "m23all", "mPall", "mP2all"])
        mx1, mx23 = mxt["m1all"], mxt["m23all"]
        mxP, mxP2 = mxt["mPall"], mxt["mP2all"]
        idb_sb = es.enter_context(nc.sbuf_tensor("idb_sb", [NT, NT], BF16))
        idf_sb = es.enter_context(nc.sbuf_tensor("idf_sb", [NT, NT], F32))
        s1c = es.enter_context(nc.sbuf_tensor("s1c", [NT, 1], F32))
        s2c = es.enter_context(nc.sbuf_tensor("s2c", [NT, 1], F32))
        s3c = es.enter_context(nc.sbuf_tensor("s3c", [NT, 1], F32))
        s4c = es.enter_context(nc.sbuf_tensor("s4c", [NT, 1], F32))
        s6c = es.enter_context(nc.sbuf_tensor("s6c", [NT, 1], F32))
        c6c = es.enter_context(nc.sbuf_tensor("c6c", [NT, 1], F32))
        tmpc = es.enter_context(nc.sbuf_tensor("tmpc", [NT, 1], F32))
        wdt = es.enter_context(nc.sbuf_tensor("wdt", [NT, ICH * H], BF16))
        BW = 2 * TQ
        P2b = [es.enter_context(nc.sbuf_tensor(f"P2b{k}", [NT, BW], F32))
               for k in range(2)]
        t6b = [es.enter_context(nc.sbuf_tensor(f"t6b{k}", [NT, BW], F32))
               for k in range(2)]
        r6i = [es.enter_context(nc.sbuf_tensor(f"r6i{k}", [NT, BW], I16))
               for k in range(2)]
        bC = es.enter_context(nc.sbuf_tensor("bC", [NT, 1], F32))
        bNC = es.enter_context(nc.sbuf_tensor("bNC", [NT, 1], F32))
        r6g = [es.enter_context(nc.sbuf_tensor("r6g0", [NT, ICH * TQ], BF16))]
        dch = [es.enter_context(nc.sbuf_tensor(f"dch{k}", [NT, 8 * TQ], F32))
               for k in range(2)]
        prt = [es.enter_context(nc.sbuf_tensor(f"prt{k}", [NT, H], F32))
               for k in range(2)]
        osb = [es.enter_context(nc.sbuf_tensor(f"osb{k}", [NT, H], F32))
               for k in range(2)]
        fsb = [es.enter_context(nc.sbuf_tensor(f"fsb{k}", [NT, H], F32))
               for k in range(2)]
        psd = [es.enter_context(nc.psum_tensor(f"pd{k}", [NT, TQ], F32))
               for k in range(2)]
        pst = [es.enter_context(nc.psum_tensor(f"pt{k}", [NT, NT], F32))
               for k in range(2)]

        t_id = p.dma("mis", lambda e: e.dma_start(
            out=idb_sb[:], in_=idb_c.ap()))
        t_idf = p.dma("mis", lambda e: e.dma_start(
            out=idf_sb[:], in_=idf_c.ap()))
        wv = [mxtok[n] for n in ("m1all", "m23all", "mPall", "mP2all")]
        tv = p.op("v", lambda e: e.tensor_scalar(
            s1c[:], mx1[:, 0:1], INVQ, EPS12, Alu.mult, Alu.max), waits=wv)
        tv = p.op("v", lambda e: e.tensor_tensor(
            tmpc[:], s1c[:], mx23[:, 0:1], Alu.mult), waits=[tv])
        tv = p.op("v", lambda e: e.tensor_scalar(
            s2c[:], tmpc[:], INVQ, EPS12, Alu.mult, Alu.max), waits=[tv])
        tv = p.op("v", lambda e: e.tensor_tensor(
            tmpc[:], s1c[:], mx23[:, 1:2], Alu.mult), waits=[tv])
        tv = p.op("v", lambda e: e.tensor_scalar(
            s3c[:], tmpc[:], INVQ, EPS12, Alu.mult, Alu.max), waits=[tv])
        tv = p.op("v", lambda e: e.tensor_tensor(
            tmpc[:], s3c[:], mxP[:, 0:1], Alu.mult), waits=[tv])
        tv = p.op("v", lambda e: e.tensor_scalar(
            tmpc[:], tmpc[:], INV65536, None, Alu.mult), waits=[tv])
        tv = p.op("v", lambda e: e.tensor_scalar(
            s4c[:], tmpc[:], INVQ, EPS12, Alu.mult, Alu.max), waits=[tv])
        tv = p.op("v", lambda e: e.tensor_tensor(
            tmpc[:], s4c[:], s2c[:], Alu.mult), waits=[tv])
        tv = p.op("v", lambda e: e.tensor_tensor(
            tmpc[:], tmpc[:], mxP2[:, 0:1], Alu.mult), waits=[tv])
        t_s6 = p.op("v", lambda e: e.tensor_scalar(
            s6c[:], tmpc[:], INVQ, EPS12, Alu.mult, Alu.max), waits=[tv])
        tv = p.op("v", lambda e: e.reciprocal(c6c[:], mxP2[:, 0:1]),
                  waits=[t_s6])
        t_c6 = p.op("v", lambda e: e.tensor_scalar(
            c6c[:], c6c[:], Q15, None, Alu.mult), waits=[tv])
        t_c6 = p.op("v", lambda e: e.memset(bC[:], MAGIC), waits=[t_c6])
        t_c6 = p.op("v", lambda e: e.memset(bNC[:], -MAGIC), waits=[t_c6])

        tok_wd = dequant_weights(nc, es, p, [(wd_d, H, I3, wdt)], idb_sb, "d")

        tok_mm = [[None] * 8 for _ in range(ngrp)]
        tok_dch = [[None] * 8 for _ in range(ngrp)]
        tok_out = [[None] * 4 for _ in range(ngrp)]
        tok_stf = [[None] * 4 for _ in range(ngrp)]
        tok_lasttr = [None] * ngrp
        tok_p2read = [None, None]
        tok_su6 = [None, None]
        tok_r6w = [None, None]
        tok_r6pair = [None, None]

        for q in range(ngrp):
            lastv = None
            for blk in range(ICH // 2):
                b = blk % 2
                ii0 = blk * 2
                w = [t_c6, tok_p2read[b]]
                if q >= 1 and blk == 0:
                    w.append(tok_mm[q - 1][7])
                tl = p.dma(f"inp{b}", lambda e, q=q, ii0=ii0, b=b: e.dma_start(
                    out=P2b[b][:].rearrange("p (k t) -> p k t", t=TQ),
                    in_=P2_d.ap()[ii0 * NT:(ii0 + 2) * NT,
                                  q * TQ:(q + 1) * TQ].rearrange(
                        "(k p) t -> p k t", p=NT)), waits=w)
                ta = p.op("a", lambda e, b=b: e.activation(
                    t6b[b][:], P2b[b][:], Act.Identity, bias=bC[:],
                    scale=c6c[:]),
                    waits=[tl, t_c6, tok_r6w[b]])
                tok_p2read[b] = ta
                ti = p.op("a", lambda e, b=b: e.activation(
                    r6i[b][:], t6b[b][:], Act.Identity, bias=bNC[:],
                    scale=1.0), waits=[ta, tok_su6[b]])
                tok_r6w[b] = ti
                wq = [ti]
                if q >= 1 and blk == 0:
                    wq.append(tok_mm[q - 1][7])
                lastv = p.op("a", lambda e, q=q, ii0=ii0, b=b: e.activation(
                    r6g[0][:, ii0 * TQ:(ii0 + 2) * TQ], r6i[b][:],
                    Act.Identity), waits=wq)
                tok_su6[b] = lastv
                tok_r6pair[b] = lastv
            tok_r6 = lastv
            for hh in range(8):
                ps = psd[hh % 2]
                wm = [tok_r6pair[0], tok_r6pair[1], tok_wd]
                if hh >= 2:
                    wm.append(tok_dch[q][hh - 2])
                elif q >= 1:
                    wm.append(tok_dch[q - 1][6 + hh])
                last = None
                for ii in range(ICH):
                    last = p.op("t", lambda e, ps=ps, hh=hh, ii=ii, q=q:
                                e.matmul(
                                    ps[:],
                                    wdt[:, ii * H + hh * NT:
                                        ii * H + (hh + 1) * NT],
                                    r6g[0][:, ii * TQ:(ii + 1) * TQ],
                                    start=(ii == 0), stop=(ii == ICH - 1)),
                                waits=wm, quiet=(ii != ICH - 1))
                    wm = []
                tok_mm[q][hh] = last
                we = [last, t_s6]
                if q >= 2 and hh == 0:
                    we.append(tok_lasttr[q - 2])
                tok_dch[q][hh] = p.op(
                    "a", lambda e, ps=ps, q=q, hh=hh: e.activation(
                        dch[q % 2][:, hh * TQ:(hh + 1) * TQ], ps[:],
                        Act.Identity, bias=0.0, scale=s6c[:]), waits=we)
            prev_lasta = None
            for tt in range(4):
                wl = []
                if tt >= 2:
                    wl.append(tok_stf[q][tt - 2])
                elif q >= 1:
                    wl.append(tok_stf[q - 1][2 + tt])
                tok_ld = p.dma(f"inq{tt % 2}", lambda e, q=q, tt=tt:
                               e.dma_start(
                    out=prt[tt % 2][:],
                    in_=part_d.ap()[q * TQ + tt * NT:q * TQ + (tt + 1) * NT, :]),
                    waits=wl)
                lasta = None
                for hh in range(8):
                    pt_ = pst[hh % 2]
                    wt = [tok_dch[q][hh], t_idf, prev_lasta]
                    if hh >= 2:
                        wt.append(lasta)
                    if hh == 0:
                        if tt >= 2:
                            wt.append(tok_out[q][tt - 2])
                        elif q >= 1:
                            wt.append(tok_out[q - 1][2 + tt])
                    ttr = p.op("t", lambda e, pt_=pt_, q=q, hh=hh, tt=tt:
                               e.transpose(
                                   pt_[:],
                                   dch[q % 2][:, hh * TQ + tt * NT:
                                              hh * TQ + (tt + 1) * NT],
                                   idf_sb[:]), waits=wt)
                    lasta = p.op("a", lambda e, pt_=pt_, tt=tt, hh=hh:
                                 e.activation(
                                     osb[tt % 2][:, hh * NT:(hh + 1) * NT],
                                     pt_[:], Act.Identity), waits=[ttr])
                prev_lasta = lasta
                if tt == 3:
                    tok_lasttr[q] = lasta
                wf = [lasta, tok_ld]
                if tt >= 2:
                    wf.append(tok_stf[q][tt - 2])
                elif q >= 1:
                    wf.append(tok_stf[q - 1][2 + tt])
                tf = p.op("v", lambda e, tt=tt: e.tensor_tensor(
                    fsb[tt % 2][:], osb[tt % 2][:], prt[tt % 2][:], Alu.add),
                    waits=wf)
                tok_out[q][tt] = tf
                tok_stf[q][tt] = p.dma(f"sou{tt % 2}", lambda e, q=q, tt=tt:
                                       e.dma_start(
                    out=out_d.ap()[q * TQ + tt * NT:q * TQ + (tt + 1) * NT, :],
                    in_=fsb[tt % 2][:]), waits=[tf], eng="gp")

        p.barrier_all()
        p.emit_block(nc, es)
    return nc



def _gather_max(results, key):
    m = np.stack([np.asarray(r[key], np.float32).reshape(NMAX)
                  for r in results])
    return np.ascontiguousarray(m.reshape(1, 8 * NMAX))


def kernel(L0_attn_x, L0_attn_filt, L0_ffn_w_gate, L0_ffn_w_up, L0_ffn_w_down):
    x = np.ascontiguousarray(np.asarray(L0_attn_x, np.float32))
    filt = np.ascontiguousarray(np.asarray(L0_attn_filt, np.float32))
    wg = np.ascontiguousarray(np.asarray(L0_ffn_w_gate, np.float32))
    wu = np.ascontiguousarray(np.asarray(L0_ffn_w_up, np.float32))
    wd = np.ascontiguousarray(np.asarray(L0_ffn_w_down, np.float32))
    B, S, _ = x.shape
    T = S
    cores = list(range(NCORES))
    Fz = build_toeplitz(filt)

    r1 = run_bass_kernel_spmd(
        build_L1(T), [{"xs": x[c], "Fz": Fz} for c in range(NCORES)], cores)
    m1all = _gather_max(r1.results, "m1loc")

    r2 = run_bass_kernel_spmd(
        build_L2(T),
        [{"xh": np.asarray(r["xh"]), "xl": np.asarray(r["xl"]),
          "wu": wu, "wg": wg, "m1all": m1all} for r in r1.results], cores)
    m23all = _gather_max(r2.results, "m23loc")

    r3 = run_bass_kernel_spmd(
        build_L3(T),
        [{"gt": np.asarray(r["gt"]), "m1all": m1all, "m23all": m23all}
         for r in r2.results], cores)
    mPall = _gather_max(r3.results, "mPloc")

    r4 = run_bass_kernel_spmd(
        build_L4(T),
        [{"ut": np.asarray(r2.results[c]["ut"]),
          "P": np.asarray(r3.results[c]["P"]),
          "m23all": m23all, "mPall": mPall} for c in range(NCORES)], cores)
    mP2all = _gather_max(r4.results, "mP2loc")

    r5 = run_bass_kernel_spmd(
        build_L5(T),
        [{"P2": np.asarray(r4.results[c]["P2"]), "wd": wd,
          "partial": np.asarray(r1.results[c]["partial"]),
          "m1all": m1all, "m23all": m23all, "mPall": mPall,
          "mP2all": mP2all} for c in range(NCORES)], cores)

    out = np.stack([np.asarray(r["out"], np.float32) for r in r5.results])
    return np.ascontiguousarray(out)


# version 08
# speedup vs baseline: 1.9761x; 1.1683x over previous
import contextlib

import numpy as np
import ml_dtypes

import concourse.bass as bass
import concourse.mybir as mybir
from concourse.bass_utils import run_bass_kernel_spmd

F32 = mybir.dt.float32
BF16 = mybir.dt.bfloat16
I16 = mybir.dt.int16
U16 = mybir.dt.uint16
Alu = mybir.AluOpType
Act = mybir.ActivationFunctionType
AX = mybir.AxisListType

NCORES = 8
H = 1024
I3 = 3072
NT = 128
LF = 2048
ND = 17
RING = 18
NMAX = 16
TQ = 512
CH = 2048
INV7 = float(np.float32(1.0 / 7.0))
INVQ = float(np.float32(1.0 / 32767.0))
Q15 = 32767.0
Q16 = 65536.0
INV65536 = float(np.float32(1.0 / 65536.0))
INVH = float(np.float32(1.0 / H))
EPS12 = 1e-12
MAGIC = 12582912.0

BF = ml_dtypes.bfloat16


def build_toeplitz(filt):
    out = np.zeros((ND, NT, NT), np.float32)
    j, i = np.meshgrid(np.arange(NT), np.arange(NT), indexing="ij")
    for d in range(ND):
        idx = NT * d + j - i
        m = (idx >= 0) & (idx < LF)
        out[d][m] = filt[idx[m]]
    return np.ascontiguousarray(out.reshape(ND * NT, NT))


class Plan:

    ENGS = ("sy", "gp", "v", "a", "t")

    def __init__(self):
        self.ops = {k: [] for k in self.ENGS}
        self.count = {}
        for k in ("v", "a", "t"):
            self.count[k] = 0
        self.lastw = {k: {} for k in self.ENGS}
        self.chan_eng = {}
        self.chan_last = {}

    def _record(self, eng, sem, emit, waits, inc):
        seen = {}
        for tok in waits:
            if tok is None:
                continue
            s, v = tok
            if v <= 0 or self.lastw[eng].get(s, 0) >= v:
                continue
            seen[s] = max(seen.get(s, 0), v)
        w = []
        for s, v in seen.items():
            w.append((s, v))
            self.lastw[eng][s] = v
        self.count[sem] = self.count.get(sem, 0) + inc
        self.ops[eng].append((w, emit, sem, inc))
        return (sem, self.count[sem])

    def op(self, eng, emit, waits=(), quiet=False):
        if quiet:
            return self._record(eng, eng, emit, waits, 0)
        return self._record(eng, eng, emit, waits, 1)

    def dma(self, chan, emit, waits=(), eng="sy"):
        if chan not in self.chan_eng:
            self.chan_eng[chan] = eng
        eng = self.chan_eng[chan]
        w = list(waits) + [self.chan_last.get(chan)]
        tok = self._record(eng, chan, emit, w, 16)
        self.chan_last[chan] = tok
        return tok

    def barrier_all(self, eng="gp"):
        waits = [(s, c) for s, c in self.count.items() if c > 0]
        self._record(eng, eng if eng in self.count else "v",
                     lambda e: None, waits, 0)

    def emit_block(self, nc, es):
        sems = {s: es.enter_context(nc.semaphore("sem_" + s))
                for s in self.count}
        with nc.Block() as block:
            def runner(engname):
                def run(engobj):
                    for (waits, emit, sem, inc) in self.ops[engname]:
                        for (s, v) in waits:
                            engobj.wait_ge(sems[s], v)
                        ins = emit(engobj)
                        if ins is not None and inc > 0:
                            ins.then_inc(sems[sem], inc)
                return run
            if self.ops["sy"]:
                block.sync(runner("sy"))
            if self.ops["gp"]:
                block.gpsimd(runner("gp"))
            if self.ops["v"]:
                block.vector(runner("v"))
            if self.ops["a"]:
                block.scalar(runner("a"))
            if self.ops["t"]:
                block.tensor(runner("t"))


def maxes_prologue(nc, es, p, names):
    tiles, toks = {}, {}
    for n in names:
        inp = nc.dram_tensor(n, [1, 8 * NMAX], F32, kind="ExternalInput")
        scr = nc.dram_tensor("scr_" + n, [1, NMAX], F32, kind="Internal")
        mh = es.enter_context(nc.sbuf_tensor("mh_" + n, [1, 8 * NMAX], F32))
        m4 = es.enter_context(nc.sbuf_tensor("m4_" + n, [1, 4 * NMAX], F32))
        m2 = es.enter_context(nc.sbuf_tensor("m2_" + n, [1, 2 * NMAX], F32))
        m1 = es.enter_context(nc.sbuf_tensor("m1_" + n, [1, NMAX], F32))
        mx = es.enter_context(nc.sbuf_tensor("mx_" + n, [NT, NMAX], F32))
        t = p.dma("mis", lambda e, inp=inp, mh=mh: e.dma_start(
            out=mh[:], in_=inp.ap()))
        t = p.op("v", lambda e, mh=mh, m4=m4: e.tensor_tensor(
            m4[:], mh[:, :4 * NMAX], mh[:, 4 * NMAX:], Alu.max), waits=[t])
        t = p.op("v", lambda e, m4=m4, m2=m2: e.tensor_tensor(
            m2[:], m4[:, :2 * NMAX], m4[:, 2 * NMAX:], Alu.max), waits=[t])
        t = p.op("v", lambda e, m2=m2, m1=m1: e.tensor_tensor(
            m1[:], m2[:, :NMAX], m2[:, NMAX:], Alu.max), waits=[t])
        t = p.dma("mis", lambda e, scr=scr, m1=m1: e.dma_start(
            out=scr.ap(), in_=m1[:]), waits=[t])
        t = p.dma("mis", lambda e, mx=mx, scr=scr: e.dma_start(
            out=mx[:], in_=scr.ap().broadcast_to([NT, NMAX])), waits=[t])
        tiles[n] = mx
        toks[n] = t
    return tiles, toks


def max_finalize(nc, es, p, maxacc, idf_sb, out_name, extra_waits=()):
    mout = nc.dram_tensor(out_name, [NMAX, 1], F32, kind="ExternalOutput")
    psf = es.enter_context(nc.psum_tensor("ps_mfin", [NT, NT], F32))
    mfs = es.enter_context(nc.sbuf_tensor("mfin_s", [NMAX, NT], F32))
    mfr = es.enter_context(nc.sbuf_tensor("mfin_r", [NMAX, 1], F32))
    w = list(extra_waits) + [("v", p.count["v"])]
    t = p.op("t", lambda e: e.transpose(psf[:NMAX, :], maxacc[:], idf_sb[:]),
             waits=w)
    t = p.op("v", lambda e: e.tensor_copy(mfs[:], psf[:NMAX, :]), waits=[t])
    t = p.op("v", lambda e: e.tensor_reduce(mfr[:], mfs[:], AX.X, Alu.max),
             waits=[t])
    return p.dma("mis", lambda e: e.dma_start(out=mout.ap(), in_=mfr[:]),
                 waits=[t])



def plan_round_signed(p, t_tok, tbuf, ibuf, dbuf, out_emit):
    tk = p.op("v", lambda e: e.tensor_copy(ibuf, tbuf), waits=[t_tok])
    tk = p.op("v", lambda e: e.tensor_tensor(dbuf, tbuf, ibuf, Alu.subtract),
              waits=[tk])
    tk = p.op("v", lambda e: e.scalar_tensor_tensor(
        tbuf, dbuf, 0.5, ibuf, Alu.is_ge, Alu.add), waits=[tk])
    tk = p.op("v", lambda e: out_emit(e, dbuf, tbuf), waits=[tk])
    return tk


def plan_round_u16(p, t_tok, tbuf, ubuf, dbuf, out, clip_hi):
    tk = p.op("v", lambda e: e.tensor_copy(ubuf, tbuf), waits=[t_tok])
    tk = p.op("v", lambda e: e.tensor_tensor(dbuf, tbuf, ubuf, Alu.subtract),
              waits=[tk])
    tk = p.op("v", lambda e: e.scalar_tensor_tensor(
        tbuf, dbuf, 0.5, ubuf, Alu.is_ge, Alu.add), waits=[tk])
    tk = p.op("v", lambda e: e.tensor_scalar(
        out, tbuf, float(clip_hi), None, Alu.min), waits=[tk])
    return tk


def dequant_weights(nc, es, p, specs, idb_sb, tag, psw=None, tok_id=None):
    CT = 1024
    wbuf = [es.enter_context(nc.sbuf_tensor(f"wb{tag}{k}", [NT, CT], F32))
            for k in range(2)]
    wqb = [es.enter_context(nc.sbuf_tensor(f"wq{tag}{k}", [NT, CT], BF16))
           for k in range(2)]
    absr = es.enter_context(nc.sbuf_tensor(f"ab{tag}", [NT, CT // 32], F32))
    sblk = es.enter_context(nc.sbuf_tensor(f"sb{tag}", [NT, CT // 32], F32))
    sinv = es.enter_context(nc.sbuf_tensor(f"si{tag}", [NT, CT // 32], F32))
    tqb = es.enter_context(nc.sbuf_tensor(f"tq{tag}", [NT, CT], F32))
    dqb = es.enter_context(nc.sbuf_tensor(f"dq{tag}", [NT, CT], F32))
    kqb = es.enter_context(nc.sbuf_tensor(f"kq{tag}", [NT, CT], I16))
    bCw = es.enter_context(nc.sbuf_tensor(f"bCw{tag}", [NT, 1], F32))
    bNCw = es.enter_context(nc.sbuf_tensor(f"bNCw{tag}", [NT, 1], F32))
    t_bw = p.op("v", lambda e: e.memset(bCw[:], MAGIC))
    t_bw = p.op("v", lambda e: e.memset(bNCw[:], -MAGIC), waits=[t_bw])
    if psw is None:
        psw = [es.enter_context(nc.psum_tensor(f"pw{tag}{k}", [NT, 8 * NT],
                                               BF16)) for k in range(2)]
    last_evac = None
    tok_consumed = [None, None]
    tok_wq_read = [None, None]
    nblk = CT // 32
    n = 0
    for (w_dram, nrows, ncols, wt_sb) in specs:
        for ii in range(nrows // NT):
            for cc in range(ncols // CT):
                par = n % 2
                n += 1
                wb, wq = wbuf[par], wqb[par]
                tl = p.dma("wld", lambda e, wb=wb, w_dram=w_dram, ii=ii, cc=cc:
                           e.dma_start(
                               out=wb[:],
                               in_=w_dram.ap()[ii * NT:(ii + 1) * NT,
                                               cc * CT:(cc + 1) * CT]),
                           waits=[tok_consumed[par]])
                tv = p.op("v", lambda e, wb=wb: e.tensor_reduce(
                    absr[:].rearrange("p (b o) -> p b o", o=1),
                    wb[:].rearrange("p (b e) -> p b e", e=32),
                    AX.X, Alu.max, apply_absolute_value=True), waits=[tl])
                tv = p.op("v", lambda e: e.tensor_scalar(
                    sblk[:], absr[:], INV7, EPS12, Alu.mult, Alu.max),
                    waits=[tv])
                tv = p.op("v", lambda e: e.reciprocal(sinv[:], sblk[:]),
                          waits=[tv])
                tv = p.op("v", lambda e: e.tensor_scalar(
                    sinv[:], sinv[:], -1.0, None, Alu.mult), waits=[tv])
                tv = p.op("v", lambda e, wb=wb: e.tensor_tensor(
                    tqb[:].rearrange("p (b e) -> p b e", e=32),
                    wb[:].rearrange("p (b e) -> p b e", e=32),
                    sinv[:].rearrange("p (b o) -> p b o", o=1)
                    .broadcast_to([NT, nblk, 32]), Alu.mult), waits=[tv])
                tok_consumed[par] = tv
                tv = p.op("a", lambda e: e.activation(
                    dqb[:], tqb[:], Act.Identity, bias=bCw[:], scale=1.0),
                    waits=[tv, t_bw])
                tv = p.op("a", lambda e: e.activation(
                    kqb[:], dqb[:], Act.Identity, bias=bNCw[:], scale=1.0),
                    waits=[tv])
                tv = p.op("v", lambda e, wq=wq: e.tensor_tensor(
                    wq[:].rearrange("p (b e) -> p b e", e=32),
                    kqb[:].rearrange("p (b e) -> p b e", e=32),
                    sblk[:].rearrange("p (b o) -> p b o", o=1)
                    .broadcast_to([NT, nblk, 32]), Alu.mult),
                    waits=[tv, tok_wq_read[par]])
                ps = psw[n % len(psw)]
                ttr = None
                for ci in range(CT // NT):
                    ttr = p.op("t", lambda e, ps=ps, wq=wq, ci=ci:
                               e.transpose(
                        ps[:, ci * NT:(ci + 1) * NT],
                        wq[:, ci * NT:(ci + 1) * NT], idb_sb[:]),
                        waits=[tv, last_evac, tok_id], quiet=(ci != 7))
                last_evac = p.op(
                    "a", lambda e, ps=ps, wt_sb=wt_sb, cc=cc, ii=ii,
                    nrows=nrows: e.activation(
                        wt_sb[:].rearrange("p (cig m) -> p cig m", m=nrows)
                        [:, cc * 8:(cc + 1) * 8, ii * NT:(ii + 1) * NT],
                        ps[:], Act.Identity), waits=[ttr])
                tok_wq_read[par] = last_evac
    return last_evac



def build_L1(T):
    ntiles = T // NT
    nc = bass.Bass("TRN2", target_bir_lowering=False)
    xs = nc.dram_tensor("xs", [T, H], F32, kind="ExternalInput")
    Fz = nc.dram_tensor("Fz", [ND * NT, NT], F32, kind="ExternalInput")
    xh_d = nc.dram_tensor("xh", [T, H], BF16, kind="ExternalOutput")
    xl_d = nc.dram_tensor("xl", [T, H], BF16, kind="ExternalOutput")
    part_d = nc.dram_tensor("partial", [T, H], F32, kind="ExternalOutput")
    idf_c = nc.inline_tensor(np.eye(NT, dtype=np.float32), name="idf")

    p = Plan()
    es = contextlib.ExitStack()
    with es:
        xr = es.enter_context(nc.sbuf_tensor("xr", [NT, RING * H], F32))
        ff = es.enter_context(nc.sbuf_tensor("ff", [NT, ND * NT], F32))
        fh = es.enter_context(nc.sbuf_tensor("fh", [NT, ND * NT], BF16))
        fl = es.enter_context(nc.sbuf_tensor("fl", [NT, ND * NT], BF16))
        xhr = es.enter_context(nc.sbuf_tensor("xhr", [NT, RING * H], BF16))
        xlr = es.enter_context(nc.sbuf_tensor("xlr", [NT, RING * H], BF16))
        xn = es.enter_context(nc.sbuf_tensor("xn", [NT, H], F32))
        dump = es.enter_context(nc.sbuf_tensor("dump", [NT, H], F32))
        smv = es.enter_context(nc.sbuf_tensor("smv", [NT, 1], F32))
        sqv = es.enter_context(nc.sbuf_tensor("sqv", [NT, 1], F32))
        muv = es.enter_context(nc.sbuf_tensor("muv", [NT, 1], F32))
        ex2 = es.enter_context(nc.sbuf_tensor("ex2", [NT, 1], F32))
        mu2 = es.enter_context(nc.sbuf_tensor("mu2", [NT, 1], F32))
        varv = es.enter_context(nc.sbuf_tensor("varv", [NT, 1], F32))
        rrv = es.enter_context(nc.sbuf_tensor("rrv", [NT, 1], F32))
        rsv = es.enter_context(nc.sbuf_tensor("rsv", [NT, 1], F32))
        mtv = es.enter_context(nc.sbuf_tensor("mtv", [NT, 1], F32))
        maxacc = es.enter_context(nc.sbuf_tensor("maxacc", [NT, NMAX], F32))
        idf_sb = es.enter_context(nc.sbuf_tensor("idf_sb", [NT, NT], F32))
        parts = [es.enter_context(nc.sbuf_tensor(f"part{k}", [NT, H], F32))
                 for k in range(2)]
        psc = [es.enter_context(nc.psum_tensor(f"psc{k}", [NT, H], F32))
               for k in range(3)]

        t_id = p.dma("mis", lambda e: e.dma_start(
            out=idf_sb[:], in_=idf_c.ap()))
        t_f = p.dma("mis", lambda e: e.dma_start(
            out=ff[:].rearrange("p (d i) -> p d i", i=NT),
            in_=Fz.ap().rearrange("(d j) i -> j d i", j=NT)))
        t_fh = p.op("v", lambda e: e.tensor_copy(fh[:], ff[:]), waits=[t_f])
        t_fl = p.op("v", lambda e: e.tensor_tensor(
            fl[:], ff[:], fh[:], Alu.subtract), waits=[t_fh])
        t_mz = p.op("v", lambda e: e.memset(maxacc[:], 0.0))

        tok_load = [None] * ntiles
        tok_xl = [None] * ntiles
        tok_store_hl = [None] * ntiles
        tok_evac = [None] * ntiles
        tok_store = [None] * ntiles
        tok_statsfree = [None] * ntiles

        def emit_stats(i):
            sl = (i % RING) * H
            xb = xr[:, sl:sl + H]
            w = [tok_load[i]]
            if i >= 1:
                w.append(tok_statsfree[i - 1])
            ta = p.op("a", lambda e, xb=xb: e.activation(
                dump[:], xb, Act.Square, accum_out=sqv[:]), waits=w)
            ta = p.op("a", lambda e, xb=xb: e.activation(
                dump[:], xb, Act.Identity, accum_out=smv[:]), waits=[ta])
            tv = p.op("v", lambda e: e.tensor_scalar(
                muv[:], smv[:], INVH, None, Alu.mult), waits=[ta])
            tv = p.op("v", lambda e: e.tensor_scalar(
                ex2[:], sqv[:], INVH, None, Alu.mult), waits=[tv])
            tok_statsfree[i] = tv
            tv = p.op("v", lambda e: e.tensor_tensor(
                mu2[:], muv[:], muv[:], Alu.mult), waits=[tv])
            tv = p.op("v", lambda e: e.scalar_tensor_tensor(
                varv[:], ex2[:], 1e-5, mu2[:], Alu.add, Alu.subtract),
                waits=[tv])
            ta2 = p.op("a", lambda e: e.activation(
                rrv[:], varv[:], Act.Sqrt), waits=[tv])
            tv = p.op("v", lambda e: e.reciprocal(rsv[:], rrv[:]), waits=[ta2])
            tv = p.op("v", lambda e, xb=xb: e.scalar_tensor_tensor(
                xn[:], xb, muv[:], rsv[:].broadcast_to([NT, H]),
                Alu.subtract, Alu.mult), waits=[tv])
            tv = p.op("v", lambda e: e.tensor_reduce(
                mtv[:], xn[:], AX.X, Alu.max, apply_absolute_value=True),
                waits=[tv])
            tv = p.op("v", lambda e: e.tensor_tensor(
                maxacc[:, 0:1], maxacc[:, 0:1], mtv[:], Alu.max),
                waits=[tv, t_mz])
            wh = [tv, t_fl]
            if i >= RING:
                wh.extend(tok_store_hl[i - RING])
            th = p.op("v", lambda e, sl=sl: e.tensor_copy(
                xhr[:, sl:sl + H], xn[:]), waits=wh)
            tl = p.op("v", lambda e, sl=sl: e.tensor_tensor(
                xlr[:, sl:sl + H], xn[:], xhr[:, sl:sl + H], Alu.subtract),
                waits=[th])
            tok_xl[i] = tl
            t1 = p.dma("sxh", lambda e, sl=sl, i=i: e.dma_start(
                out=xh_d.ap()[i * NT:(i + 1) * NT, :], in_=xhr[:, sl:sl + H]),
                waits=[tl], eng="gp")
            t2 = p.dma("sxl", lambda e, sl=sl, i=i: e.dma_start(
                out=xl_d.ap()[i * NT:(i + 1) * NT, :], in_=xlr[:, sl:sl + H]),
                waits=[tl], eng="gp")
            tok_store_hl[i] = (t1, t2)

        def emit_conv(j):
            ps = psc[j % 3]
            dmax = min(ND - 1, ntiles - 1 - j)
            w = [tok_xl[j + dmax], t_fl]
            if j >= 3:
                w.append(tok_evac[j - 3])
            n_terms = 3 * (dmax + 1)
            cnt = 0
            first = [True, True]
            last = None
            for d in range(dmax + 1):
                sl = ((j + d) % RING) * H
                for (fm, xm) in ((fh, xhr), (fl, xhr), (fh, xlr)):
                    cnt += 1
                    stop = (cnt == n_terms)
                    for half in range(2):
                        cs = slice(half * 512, half * 512 + 512)
                        st = first[half]
                        first[half] = False
                        is_last = (cnt == n_terms and half == 1)
                        last = p.op("t", lambda e, ps=ps, fm=fm, xm=xm, d=d,
                                    sl=sl, cs=cs, st=st, stop=stop: e.matmul(
                                        ps[:, cs], fm[:, d * NT:(d + 1) * NT],
                                        xm[:, sl + cs.start:sl + cs.stop],
                                        start=st, stop=stop), waits=w,
                                    quiet=not is_last)
                        w = []
            sl = (j % RING) * H
            pb = parts[j % 2]
            we = [last, tok_load[j]]
            if j >= 2:
                we.append(tok_store[j - 2])
            te = p.op("v", lambda e, ps=ps, sl=sl, pb=pb: e.tensor_tensor(
                pb[:], ps[:], xr[:, sl:sl + H], Alu.add), waits=we)
            tok_evac[j] = te
            tok_store[j] = p.dma("spt", lambda e, pb=pb, j=j: e.dma_start(
                out=part_d.ap()[j * NT:(j + 1) * NT, :], in_=pb[:]),
                waits=[te], eng="gp")

        for i in range(ntiles):
            w = []
            if i >= RING:
                w.append(tok_evac[i - RING])
            tok_load[i] = p.dma(f"xi{i % 2}", lambda e, i=i: e.dma_start(
                out=xr[:, (i % RING) * H:(i % RING) * H + H],
                in_=xs.ap()[i * NT:(i + 1) * NT, :]), waits=w)
            emit_stats(i)
            if i >= ND - 1:
                emit_conv(i - (ND - 1))
        for j in range(max(0, ntiles - (ND - 1)), ntiles):
            emit_conv(j)

        max_finalize(nc, es, p, maxacc, idf_sb, "m1loc", [t_id])
        p.barrier_all()
        p.emit_block(nc, es)
    return nc



def build_L2(T):
    ngrp = T // TQ
    ICH = I3 // NT
    nc = bass.Bass("TRN2", target_bir_lowering=False)
    xh_d = nc.dram_tensor("xh", [T, H], BF16, kind="ExternalInput")
    xl_d = nc.dram_tensor("xl", [T, H], BF16, kind="ExternalInput")
    wu_d = nc.dram_tensor("wu", [I3, H], F32, kind="ExternalInput")
    wg_d = nc.dram_tensor("wg", [I3, H], F32, kind="ExternalInput")
    ut_d = nc.dram_tensor("ut", [I3, T], F32, kind="ExternalOutput")
    gt_d = nc.dram_tensor("gt", [I3, T], F32, kind="ExternalOutput")
    idb_c = nc.inline_tensor(np.eye(NT, dtype=BF), name="idb")
    idf_c = nc.inline_tensor(np.eye(NT, dtype=np.float32), name="idf")

    p = Plan()
    es = contextlib.ExitStack()
    with es:
        mxt, mxtok = maxes_prologue(nc, es, p, ["m1all"])
        mx1 = mxt["m1all"]
        idb_sb = es.enter_context(nc.sbuf_tensor("idb_sb", [NT, NT], BF16))
        idf_sb = es.enter_context(nc.sbuf_tensor("idf_sb", [NT, NT], F32))
        s1c = es.enter_context(nc.sbuf_tensor("s1c", [NT, 1], F32))
        c1c = es.enter_context(nc.sbuf_tensor("c1c", [NT, 1], F32))
        wut = es.enter_context(nc.sbuf_tensor("wut", [NT, 8 * I3], BF16))
        wgt = es.enter_context(nc.sbuf_tensor("wgt", [NT, 8 * I3], BF16))
        xhg = [es.enter_context(nc.sbuf_tensor(f"xhg{k}", [NT, 4 * H], BF16))
               for k in range(2)]
        xlg = [es.enter_context(nc.sbuf_tensor(f"xlg{k}", [NT, 4 * H], BF16))
               for k in range(2)]
        xqt = [es.enter_context(nc.sbuf_tensor(f"xqt{k}", [NT, 8 * TQ], BF16))
               for k in range(2)]
        xnb = es.enter_context(nc.sbuf_tensor("xnb", [NT, H], F32))
        t1b = es.enter_context(nc.sbuf_tensor("t1b", [NT, H], F32))
        r1i = es.enter_context(nc.sbuf_tensor("r1i", [NT, H], I16))
        bC = es.enter_context(nc.sbuf_tensor("bC", [NT, 1], F32))
        bNC = es.enter_context(nc.sbuf_tensor("bNC", [NT, 1], F32))
        r1f = [es.enter_context(nc.sbuf_tensor(f"r1f{k}", [NT, H], BF16))
               for k in range(2)]
        ue = [es.enter_context(nc.sbuf_tensor(f"ue{k}", [NT, TQ], F32))
              for k in range(2)]
        ge = [es.enter_context(nc.sbuf_tensor(f"ge{k}", [NT, TQ], F32))
              for k in range(2)]
        nch_mm = (T // TQ) * (I3 // NT)
        urd = es.enter_context(nc.sbuf_tensor("urd", [NT, nch_mm], F32))
        grd = es.enter_context(nc.sbuf_tensor("grd", [NT, nch_mm], F32))
        maxacc = es.enter_context(nc.sbuf_tensor("maxacc", [NT, NMAX], F32))
        psu = [es.enter_context(nc.psum_tensor(f"pu{k}", [NT, TQ], F32))
               for k in range(2)]
        psg = [es.enter_context(nc.psum_tensor(f"pg{k}", [NT, TQ], F32))
               for k in range(2)]
        psx = [es.enter_context(nc.psum_tensor(f"px{k}", [NT, 8 * NT], BF16))
               for k in range(2)]
        psxq = es.enter_context(nc.psum_tensor("pxq", [NT, 8 * NT], BF16))

        t_id = p.dma("mis", lambda e: e.dma_start(
            out=idb_sb[:], in_=idb_c.ap()))
        t_idf = p.dma("mis", lambda e: e.dma_start(
            out=idf_sb[:], in_=idf_c.ap()))
        t_mz = p.op("v", lambda e: e.memset(maxacc[:], 0.0))
        t_mz = p.op("v", lambda e: e.memset(bC[:], MAGIC), waits=[t_mz])
        t_mz = p.op("v", lambda e: e.memset(bNC[:], -MAGIC), waits=[t_mz])
        tv = p.op("v", lambda e: e.tensor_scalar(
            s1c[:], mx1[:, 0:1], INVQ, EPS12, Alu.mult, Alu.max),
            waits=[mxtok["m1all"]])
        t_c1 = p.op("v", lambda e: e.reciprocal(c1c[:], s1c[:]),
                    waits=[tv, t_mz])

        tok_w = dequant_weights(
            nc, es, p, [(wu_d, I3, H, wut), (wg_d, I3, H, wgt)], idb_sb, "w",
            psw=psx, tok_id=t_id)

        tok_mm = {"u": [[None] * ICH for _ in range(ngrp)],
                  "g": [[None] * ICH for _ in range(ngrp)]}
        tok_ev = {"u": [[None] * ICH for _ in range(ngrp)],
                  "g": [[None] * ICH for _ in range(ngrp)]}
        tok_st = {"u": [[None] * ICH for _ in range(ngrp)],
                  "g": [[None] * ICH for _ in range(ngrp)]}
        tok_rd = {"u": [[None] * ICH for _ in range(ngrp)],
                  "g": [[None] * ICH for _ in range(ngrp)]}
        tok_ld = [None] * ngrp
        tok_chainTT = [None] * ngrp
        tok_xqt = [None] * ngrp
        st = {"t1": None, "r1i": None, "xev": tok_w, "rf": [None, None]}

        def emit_loads(q):
            w = [tok_chainTT[q - 2]] if q >= 2 else []
            tl = p.dma(f"xi{q % 2}", lambda e, q=q: e.dma_start(
                out=xhg[q % 2][:].rearrange("p (k h) -> p k h", h=H),
                in_=xh_d.ap()[q * TQ:(q + 1) * TQ, :].rearrange(
                    "(k p) h -> p k h", p=NT)), waits=w)
            tok_ld[q] = p.dma(f"xi{q % 2}", lambda e, q=q: e.dma_start(
                out=xlg[q % 2][:].rearrange("p (k h) -> p k h", h=H),
                in_=xl_d.ap()[q * TQ:(q + 1) * TQ, :].rearrange(
                    "(k p) h -> p k h", p=NT)), waits=[tl])

        def emit_chain_k(q, k):
            cs = slice(k * H, (k + 1) * H)
            tv = p.op("v", lambda e, q=q, cs=cs: e.tensor_tensor(
                xnb[:], xhg[q % 2][:, cs], xlg[q % 2][:, cs], Alu.add),
                waits=[tok_ld[q], t_c1, st["t1"]])
            if k == 3:
                tok_chainTT[q] = tv
            ta1 = p.op("a", lambda e: e.activation(
                t1b[:], xnb[:], Act.Identity, bias=bC[:], scale=c1c[:]),
                waits=[tv, st["r1i"]])
            st["t1"] = ta1
            ta2 = p.op("a", lambda e: e.activation(
                r1i[:], t1b[:], Act.Identity, bias=bNC[:], scale=1.0),
                waits=[ta1, st["rf"][k % 2]])
            st["r1i"] = ta2
            rf = r1f[k % 2]
            ta3 = p.op("a", lambda e, rf=rf: e.activation(
                rf[:], r1i[:], Act.Identity), waits=[ta2])
            wt = [ta3, t_id, st["xev"]]
            if k == 0 and q >= 2:
                wt += [tok_mm["u"][q - 2][ICH - 1], tok_mm["g"][q - 2][ICH - 1]]
            ttr = None
            for ci in range(8):
                ttr = p.op("t", lambda e, rf=rf, ci=ci: e.transpose(
                    psxq[:, ci * NT:(ci + 1) * NT],
                    rf[:, ci * NT:(ci + 1) * NT], idb_sb[:]),
                    waits=wt, quiet=(ci != 7))
                wt = []
            st["rf"][k % 2] = ttr
            xev = p.op("a", lambda e, q=q, k=k: e.activation(
                xqt[q % 2][:].rearrange("p (ci tq) -> p ci tq", tq=TQ)
                [:, :, k * NT:(k + 1) * NT], psxq[:], Act.Identity),
                waits=[ttr])
            st["xev"] = xev
            if k == 3:
                tok_xqt[q] = xev

        emit_loads(0)
        for k in range(4):
            emit_chain_k(0, k)

        for q in range(ngrp):
            for ii in range(ICH):
                if q + 1 < ngrp:
                    if ii == 0:
                        emit_loads(q + 1)
                    elif ii in (4, 8, 12, 16) and q + 1 < ngrp:
                        emit_chain_k(q + 1, (ii - 4) // 4)
                for key, wt_sb, psv, ev, rd, dst in (
                    ("u", wut, psu, ue, urd, ut_d),
                    ("g", wgt, psg, ge, grd, gt_d),
                ):
                    ps = psv[ii % 2]
                    wm = [tok_xqt[q], tok_w]
                    if ii >= 2:
                        wm.append(tok_ev[key][q][ii - 2])
                    elif q >= 1:
                        wm.append(tok_ev[key][q - 1][ICH - 2 + ii])
                    last = None
                    for ci in range(8):
                        last = p.op("t", lambda e, ps=ps, wt_sb=wt_sb, ci=ci,
                                    ii=ii, q=q: e.matmul(
                                        ps[:],
                                        wt_sb[:, ci * I3 + ii * NT:
                                              ci * I3 + (ii + 1) * NT],
                                        xqt[q % 2][:, ci * TQ:(ci + 1) * TQ],
                                        start=(ci == 0), stop=(ci == 7)),
                                    waits=wm, quiet=(ci != 7))
                        wm = []
                    tok_mm[key][q][ii] = last
                    eb = ev[ii % 2]
                    we = [last]
                    if ii >= 2:
                        we.append(tok_st[key][q][ii - 2])
                        we.extend(tok_rd[key][q][ii - 2])
                    elif q >= 1:
                        we.append(tok_st[key][q - 1][ICH - 2 + ii])
                        we.extend(tok_rd[key][q - 1][ICH - 2 + ii])
                    if key == "u":
                        tev = p.op("a", lambda e, ps=ps, eb=eb: e.activation(
                            eb[:], ps[:], Act.Identity), waits=we)
                    else:
                        tev = p.op("v", lambda e, ps=ps, eb=eb: e.tensor_copy(
                            eb[:], ps[:]), waits=we)
                    tok_ev[key][q][ii] = tev
                    idx = q * ICH + ii
                    tn = p.op("v", lambda e, eb=eb, rd=rd, idx=idx:
                              e.tensor_reduce(
                                  rd[:, idx:idx + 1], eb[:], AX.X, Alu.max,
                                  apply_absolute_value=True), waits=[tev])
                    tok_rd[key][q][ii] = (tn,)
                    tok_st[key][q][ii] = p.dma(
                        "s" + key, lambda e, eb=eb, ii=ii, q=q, dst=dst:
                        e.dma_start(out=dst.ap()[ii * NT:(ii + 1) * NT,
                                                 q * TQ:(q + 1) * TQ],
                                    in_=eb[:]), waits=[tev], eng="a")

        wlast = list(tok_rd["u"][ngrp - 1][ICH - 1]) + \
            list(tok_rd["g"][ngrp - 1][ICH - 1])
        tfr = p.op("v", lambda e: e.tensor_reduce(
            maxacc[:, 0:1], urd[:], AX.X, Alu.max, apply_absolute_value=True),
            waits=wlast)
        p.op("v", lambda e: e.tensor_reduce(
            maxacc[:, 1:2], grd[:], AX.X, Alu.max, apply_absolute_value=True),
            waits=[tfr])
        max_finalize(nc, es, p, maxacc, idf_sb, "m23loc", [t_idf])
        p.barrier_all()
        p.emit_block(nc, es)
    return nc



def build_L3(T):
    ch = min(1024, T)
    nc = bass.Bass("TRN2", target_bir_lowering=False)
    gt_d = nc.dram_tensor("gt", [I3, T], F32, kind="ExternalInput")
    P_d = nc.dram_tensor("P", [I3, T], F32, kind="ExternalOutput")
    idf_c = nc.inline_tensor(np.eye(NT, dtype=np.float32), name="idf")

    p = Plan()
    es = contextlib.ExitStack()
    with es:
        mxt, mxtok = maxes_prologue(nc, es, p, ["m1all", "m23all"])
        mx1, mx23 = mxt["m1all"], mxt["m23all"]
        idf_sb = es.enter_context(nc.sbuf_tensor("idf_sb", [NT, NT], F32))
        s1c = es.enter_context(nc.sbuf_tensor("s1c", [NT, 1], F32))
        s3c = es.enter_context(nc.sbuf_tensor("s3c", [NT, 1], F32))
        c3n = es.enter_context(nc.sbuf_tensor("c3n", [NT, 1], F32))
        tmpc = es.enter_context(nc.sbuf_tensor("tmpc", [NT, 1], F32))
        bC = es.enter_context(nc.sbuf_tensor("bC", [NT, 1], F32))
        bNC = es.enter_context(nc.sbuf_tensor("bNC", [NT, 1], F32))
        nchunk = (I3 // NT) * (T // ch)
        ncol = T // ch

        def fam(name, par, dt=F32):
            return [es.enter_context(nc.sbuf_tensor(f"{name}{k}", [NT, ch],
                                                    dt)) for k in range(par)]

        gb = fam("gb", 3)
        t3b = fam("t3b", 3)
        r3b = fam("r3b", 3)
        r3ib = fam("r3ib", 5, I16)
        sgb = fam("sgb", 2)
        t4b = fam("t4b", 3)
        r4b = fam("r4b", 3, U16)
        Pb = fam("Pb", 3)
        rdw = es.enter_context(nc.sbuf_tensor("rdw", [NT, nchunk], F32))
        maxacc = es.enter_context(nc.sbuf_tensor("maxacc", [NT, NMAX], F32))

        t_idf = p.dma("mis", lambda e: e.dma_start(
            out=idf_sb[:], in_=idf_c.ap()))
        t_mz = p.op("v", lambda e: e.memset(maxacc[:], 0.0))
        t_mz = p.op("v", lambda e: e.memset(bC[:], MAGIC), waits=[t_mz])
        t_mz = p.op("v", lambda e: e.memset(bNC[:], -MAGIC), waits=[t_mz])
        wv = [mxtok["m1all"], mxtok["m23all"]]
        tv = p.op("v", lambda e: e.tensor_scalar(
            s1c[:], mx1[:, 0:1], INVQ, EPS12, Alu.mult, Alu.max), waits=wv)
        tv = p.op("v", lambda e: e.tensor_tensor(
            tmpc[:], s1c[:], mx23[:, 1:2], Alu.mult), waits=[tv])
        t_s3 = p.op("v", lambda e: e.tensor_scalar(
            s3c[:], tmpc[:], INVQ, EPS12, Alu.mult, Alu.max), waits=[tv])
        tv = p.op("v", lambda e: e.reciprocal(c3n[:], mx23[:, 1:2]),
                  waits=[t_s3])
        t_c3 = p.op("v", lambda e: e.tensor_scalar(
            c3n[:], c3n[:], -Q15, None, Alu.mult), waits=[tv, t_mz])

        tok = {k: [None] * nchunk for k in
               ("ld", "t3", "r3f", "r3i", "sg", "t4", "rm4", "r4", "P",
                "red", "st")}

        def rc(c):
            ii, qq = divmod(c, ncol)
            return slice(ii * NT, (ii + 1) * NT), slice(qq * ch, (qq + 1) * ch)

        def tk(k, c):
            return tok[k][c] if c >= 0 else None

        def ph_ld(c):
            rows, cols = rc(c)
            tok["ld"][c] = p.dma(f"in{c % 3}", lambda e, c=c, rows=rows,
                                 cols=cols: e.dma_start(
                out=gb[c % 3][:], in_=gt_d.ap()[rows, cols]),
                waits=[tk("t3", c - 3)])

        def ph_A1(c):
            tok["t3"][c] = p.op("a", lambda e, c=c: e.activation(
                t3b[c % 3][:], gb[c % 3][:], Act.Identity, bias=bC[:],
                scale=c3n[:]),
                waits=[tok["ld"][c], t_c3, tk("r3f", c - 3)])

        def ph_V1(c):
            tok["r3f"][c] = p.op("v", lambda e, c=c: e.tensor_scalar(
                r3b[c % 3][:], t3b[c % 3][:], -MAGIC, None, Alu.add),
                waits=[tok["t3"][c], tk("sg", c - 3)])
            tok["r3i"][c] = p.op("v", lambda e, c=c: e.tensor_copy(
                r3ib[c % 5][:], r3b[c % 3][:]),
                waits=[tok["r3f"][c], tk("P", c - 5)])

        def ph_A2(c):
            tok["sg"][c] = p.op("a", lambda e, c=c: e.activation(
                sgb[c % 2][:], r3b[c % 3][:], Act.Sigmoid, bias=0.0,
                scale=s3c[:]),
                waits=[tok["r3f"][c], t_s3, tk("t4", c - 2)])

        def ph_V2(c):
            tok["t4"][c] = p.op("v", lambda e, c=c: e.tensor_scalar(
                t4b[c % 3][:], sgb[c % 2][:], 65536.0, MAGIC, Alu.mult,
                Alu.add), waits=[tok["sg"][c], tk("rm4", c - 3)])
            tok["rm4"][c] = p.op("v", lambda e, c=c: e.tensor_scalar(
                t4b[c % 3][:], t4b[c % 3][:], MAGIC + 65535.0, None, Alu.min),
                waits=[tok["t4"][c], tk("r4", c - 3)])

        def ph_A3(c):
            tok["r4"][c] = p.op("a", lambda e, c=c: e.activation(
                r4b[c % 3][:], t4b[c % 3][:], Act.Identity, bias=bNC[:],
                scale=1.0), waits=[tok["rm4"][c], tk("P", c - 3)])

        def ph_G1(c):
            tok["P"][c] = p.op("gp", lambda e, c=c: e.tensor_tensor(
                Pb[c % 3][:], r3ib[c % 5][:], r4b[c % 3][:], Alu.mult),
                waits=[tok["r4"][c], tok["r3i"][c], tk("st", c - 3),
                       tk("red", c - 3)])

        def ph_V3(c):
            rows, cols = rc(c)
            tok["red"][c] = p.op("v", lambda e, c=c: e.tensor_reduce(
                rdw[:, c:c + 1], Pb[c % 3][:], AX.X, Alu.max,
                apply_absolute_value=True), waits=[tok["P"][c]])
            tok["st"][c] = p.dma(f"sP{c % 3}", lambda e, c=c, rows=rows,
                                 cols=cols: e.dma_start(
                out=P_d.ap()[rows, cols], in_=Pb[c % 3][:]),
                waits=[tok["P"][c]], eng="a")

        phases = [(0, ph_ld), (2, ph_A1), (3, ph_V1), (4, ph_A2),
                  (5, ph_V2), (6, ph_A3), (7, ph_G1), (8, ph_V3)]
        for i in range(nchunk + 8):
            for off, fn in phases:
                c = i - off
                if 0 <= c < nchunk:
                    fn(c)

        p.op("v", lambda e: e.tensor_reduce(
            maxacc[:, 0:1], rdw[:], AX.X, Alu.max, apply_absolute_value=True),
            waits=[t_mz] + [tok["red"][c] for c in range(nchunk)])

        max_finalize(nc, es, p, maxacc, idf_sb, "mPloc", [t_idf])
        p.barrier_all()
        p.emit_block(nc, es)
    return nc



def build_L4(T):
    ch = min(1024, T)
    nc = bass.Bass("TRN2", target_bir_lowering=False)
    ut_d = nc.dram_tensor("ut", [I3, T], F32, kind="ExternalInput")
    P_d = nc.dram_tensor("P", [I3, T], F32, kind="ExternalInput")
    P2_d = nc.dram_tensor("P2", [I3, T], F32, kind="ExternalOutput")
    idf_c = nc.inline_tensor(np.eye(NT, dtype=np.float32), name="idf")

    p = Plan()
    es = contextlib.ExitStack()
    with es:
        mxt, mxtok = maxes_prologue(nc, es, p, ["m23all", "mPall"])
        mx23, mxP = mxt["m23all"], mxt["mPall"]
        idf_sb = es.enter_context(nc.sbuf_tensor("idf_sb", [NT, NT], F32))
        c2c = es.enter_context(nc.sbuf_tensor("c2c", [NT, 1], F32))
        c5c = es.enter_context(nc.sbuf_tensor("c5c", [NT, 1], F32))
        bC = es.enter_context(nc.sbuf_tensor("bC", [NT, 1], F32))
        bNC = es.enter_context(nc.sbuf_tensor("bNC", [NT, 1], F32))
        nchunk = (I3 // NT) * (T // ch)
        ncol = T // ch

        def fam(name, par, dt=F32):
            return [es.enter_context(nc.sbuf_tensor(f"{name}{k}", [NT, ch],
                                                    dt)) for k in range(par)]

        ub = fam("ub", 3)
        Pb = fam("Pb", 3)
        t2b = fam("t2b", 3)
        r2ib = fam("r2ib", 4, I16)
        t5b = fam("t5b", 3)
        r5b = fam("r5b", 3)
        r5ib = fam("r5ib", 3, I16)
        P2b = fam("P2b", 3)
        rdw = es.enter_context(nc.sbuf_tensor("rdw", [NT, nchunk], F32))
        maxacc = es.enter_context(nc.sbuf_tensor("maxacc", [NT, NMAX], F32))

        t_idf = p.dma("mis", lambda e: e.dma_start(
            out=idf_sb[:], in_=idf_c.ap()))
        t_mz = p.op("v", lambda e: e.memset(maxacc[:], 0.0))
        t_mz = p.op("v", lambda e: e.memset(bC[:], MAGIC), waits=[t_mz])
        t_mz = p.op("v", lambda e: e.memset(bNC[:], -MAGIC), waits=[t_mz])
        wv = [mxtok["m23all"], mxtok["mPall"]]
        tv = p.op("v", lambda e: e.reciprocal(c2c[:], mx23[:, 0:1]), waits=wv)
        t_c2 = p.op("v", lambda e: e.tensor_scalar(
            c2c[:], c2c[:], Q15, None, Alu.mult), waits=[tv, t_mz])
        tv = p.op("v", lambda e: e.reciprocal(c5c[:], mxP[:, 0:1]),
                  waits=[t_c2])
        t_c5 = p.op("v", lambda e: e.tensor_scalar(
            c5c[:], c5c[:], Q15, None, Alu.mult), waits=[tv])

        tok = {k: [None] * nchunk for k in
               ("ldu", "ldp", "t2", "r2i", "t5", "r5f", "r5i", "P2",
                "red", "st")}

        def rc(c):
            ii, qq = divmod(c, ncol)
            return slice(ii * NT, (ii + 1) * NT), slice(qq * ch, (qq + 1) * ch)

        def tk(k, c):
            return tok[k][c] if c >= 0 else None

        def ph_ldu(c):
            rows, cols = rc(c)
            tok["ldu"][c] = p.dma(f"inu{c % 3}", lambda e, c=c, rows=rows,
                                  cols=cols: e.dma_start(
                out=ub[c % 3][:], in_=ut_d.ap()[rows, cols]),
                waits=[tk("t2", c - 3)])

        def ph_ldp(c):
            rows, cols = rc(c)
            tok["ldp"][c] = p.dma(f"inp{c % 3}", lambda e, c=c, rows=rows,
                                  cols=cols: e.dma_start(
                out=Pb[c % 3][:], in_=P_d.ap()[rows, cols]),
                waits=[tk("t5", c - 3)])

        def ph_A1(c):
            tok["t2"][c] = p.op("a", lambda e, c=c: e.activation(
                t2b[c % 3][:], ub[c % 3][:], Act.Identity, bias=bC[:],
                scale=c2c[:]),
                waits=[tok["ldu"][c], t_c5, tk("r2i", c - 3)])

        def ph_A2(c):
            tok["r2i"][c] = p.op("a", lambda e, c=c: e.activation(
                r2ib[c % 4][:], t2b[c % 3][:], Act.Identity, bias=bNC[:],
                scale=1.0), waits=[tok["t2"][c], tk("P2", c - 4)])

        def ph_A3(c):
            tok["t5"][c] = p.op("a", lambda e, c=c: e.activation(
                t5b[c % 3][:], Pb[c % 3][:], Act.Identity, bias=bC[:],
                scale=c5c[:]),
                waits=[tok["ldp"][c], t_c5, tk("r5f", c - 3)])

        def ph_V1(c):
            tok["r5f"][c] = p.op("v", lambda e, c=c: e.tensor_scalar(
                r5b[c % 3][:], t5b[c % 3][:], -MAGIC, None, Alu.add),
                waits=[tok["t5"][c], tk("r5i", c - 3)])
            tok["r5i"][c] = p.op("v", lambda e, c=c: e.tensor_copy(
                r5ib[c % 3][:], r5b[c % 3][:]),
                waits=[tok["r5f"][c], tk("P2", c - 3)])

        def ph_G1(c):
            tok["P2"][c] = p.op("gp", lambda e, c=c: e.tensor_tensor(
                P2b[c % 3][:], r5ib[c % 3][:], r2ib[c % 4][:], Alu.mult),
                waits=[tok["r5i"][c], tok["r2i"][c], tk("st", c - 3),
                       tk("red", c - 3)])

        def ph_V3(c):
            rows, cols = rc(c)
            tok["red"][c] = p.op("v", lambda e, c=c: e.tensor_reduce(
                rdw[:, c:c + 1], P2b[c % 3][:], AX.X, Alu.max,
                apply_absolute_value=True), waits=[tok["P2"][c]])
            tok["st"][c] = p.dma(f"sP{c % 3}", lambda e, c=c, rows=rows,
                                 cols=cols: e.dma_start(
                out=P2_d.ap()[rows, cols], in_=P2b[c % 3][:]),
                waits=[tok["P2"][c]], eng="a")

        phases = [(0, ph_ldu), (1, ph_ldp), (2, ph_A1), (3, ph_A2),
                  (3, ph_A3), (4, ph_V1), (5, ph_G1), (6, ph_V3)]
        for i in range(nchunk + 6):
            for off, fn in phases:
                c = i - off
                if 0 <= c < nchunk:
                    fn(c)

        p.op("v", lambda e: e.tensor_reduce(
            maxacc[:, 0:1], rdw[:], AX.X, Alu.max, apply_absolute_value=True),
            waits=[t_mz] + [tok["red"][c] for c in range(nchunk)])

        max_finalize(nc, es, p, maxacc, idf_sb, "mP2loc", [t_idf])
        p.barrier_all()
        p.emit_block(nc, es)
    return nc



def build_L5(T):
    ngrp = T // TQ
    ICH = I3 // NT
    nc = bass.Bass("TRN2", target_bir_lowering=False)
    P2_d = nc.dram_tensor("P2", [I3, T], F32, kind="ExternalInput")
    wd_d = nc.dram_tensor("wd", [H, I3], F32, kind="ExternalInput")
    part_d = nc.dram_tensor("partial", [T, H], F32, kind="ExternalInput")
    out_d = nc.dram_tensor("out", [T, H], F32, kind="ExternalOutput")
    idb_c = nc.inline_tensor(np.eye(NT, dtype=BF), name="idb")

    p = Plan()
    es = contextlib.ExitStack()
    with es:
        mxt, mxtok = maxes_prologue(
            nc, es, p, ["m1all", "m23all", "mPall", "mP2all"])
        mx1, mx23 = mxt["m1all"], mxt["m23all"]
        mxP, mxP2 = mxt["mPall"], mxt["mP2all"]
        idb_sb = es.enter_context(nc.sbuf_tensor("idb_sb", [NT, NT], BF16))
        s1c = es.enter_context(nc.sbuf_tensor("s1c", [NT, 1], F32))
        s2c = es.enter_context(nc.sbuf_tensor("s2c", [NT, 1], F32))
        s3c = es.enter_context(nc.sbuf_tensor("s3c", [NT, 1], F32))
        s4c = es.enter_context(nc.sbuf_tensor("s4c", [NT, 1], F32))
        s6c = es.enter_context(nc.sbuf_tensor("s6c", [NT, 1], F32))
        c6c = es.enter_context(nc.sbuf_tensor("c6c", [NT, 1], F32))
        tmpc = es.enter_context(nc.sbuf_tensor("tmpc", [NT, 1], F32))
        bC = es.enter_context(nc.sbuf_tensor("bC", [NT, 1], F32))
        bNC = es.enter_context(nc.sbuf_tensor("bNC", [NT, 1], F32))
        wdt = es.enter_context(nc.sbuf_tensor("wdt", [NT, ICH * H], BF16))
        BW = 2 * TQ
        P2b = [es.enter_context(nc.sbuf_tensor(f"P2b{k}", [NT, BW], F32))
               for k in range(2)]
        t6b = [es.enter_context(nc.sbuf_tensor(f"t6b{k}", [NT, BW], F32))
               for k in range(2)]
        r6i = [es.enter_context(nc.sbuf_tensor(f"r6i{k}", [NT, BW], I16))
               for k in range(2)]
        r6g = [es.enter_context(nc.sbuf_tensor(f"r6g{k}", [NT, ICH * TQ],
                                                BF16)) for k in range(2)]
        prt = [es.enter_context(nc.sbuf_tensor(f"prt{k}", [NT, H], F32))
               for k in range(2)]
        osb = [es.enter_context(nc.sbuf_tensor(f"osb{k}", [NT, H], F32))
               for k in range(2)]
        fsb = [es.enter_context(nc.sbuf_tensor(f"fsb{k}", [NT, H], F32))
               for k in range(2)]
        psd = [es.enter_context(nc.psum_tensor(f"pd{k}", [NT, TQ], F32))
               for k in range(3)]

        t_id = p.dma("mis", lambda e: e.dma_start(
            out=idb_sb[:], in_=idb_c.ap()))
        wv = [mxtok[n] for n in ("m1all", "m23all", "mPall", "mP2all")]
        tv = p.op("v", lambda e: e.tensor_scalar(
            s1c[:], mx1[:, 0:1], INVQ, EPS12, Alu.mult, Alu.max), waits=wv)
        tv = p.op("v", lambda e: e.tensor_tensor(
            tmpc[:], s1c[:], mx23[:, 0:1], Alu.mult), waits=[tv])
        tv = p.op("v", lambda e: e.tensor_scalar(
            s2c[:], tmpc[:], INVQ, EPS12, Alu.mult, Alu.max), waits=[tv])
        tv = p.op("v", lambda e: e.tensor_tensor(
            tmpc[:], s1c[:], mx23[:, 1:2], Alu.mult), waits=[tv])
        tv = p.op("v", lambda e: e.tensor_scalar(
            s3c[:], tmpc[:], INVQ, EPS12, Alu.mult, Alu.max), waits=[tv])
        tv = p.op("v", lambda e: e.tensor_tensor(
            tmpc[:], s3c[:], mxP[:, 0:1], Alu.mult), waits=[tv])
        tv = p.op("v", lambda e: e.tensor_scalar(
            tmpc[:], tmpc[:], INV65536, None, Alu.mult), waits=[tv])
        tv = p.op("v", lambda e: e.tensor_scalar(
            s4c[:], tmpc[:], INVQ, EPS12, Alu.mult, Alu.max), waits=[tv])
        tv = p.op("v", lambda e: e.tensor_tensor(
            tmpc[:], s4c[:], s2c[:], Alu.mult), waits=[tv])
        tv = p.op("v", lambda e: e.tensor_tensor(
            tmpc[:], tmpc[:], mxP2[:, 0:1], Alu.mult), waits=[tv])
        t_s6 = p.op("v", lambda e: e.tensor_scalar(
            s6c[:], tmpc[:], INVQ, EPS12, Alu.mult, Alu.max), waits=[tv])
        tv = p.op("v", lambda e: e.reciprocal(c6c[:], mxP2[:, 0:1]),
                  waits=[t_s6])
        t_c6 = p.op("v", lambda e: e.tensor_scalar(
            c6c[:], c6c[:], Q15, None, Alu.mult), waits=[tv])
        t_c6 = p.op("v", lambda e: e.memset(bC[:], MAGIC), waits=[t_c6])
        t_c6 = p.op("v", lambda e: e.memset(bNC[:], -MAGIC), waits=[t_c6])

        tok_wd = dequant_weights(nc, es, p, [(wd_d, H, I3, wdt)], idb_sb,
                                 "d", tok_id=t_id)

        NGR = 8
        tok_mm = [[None] * NGR for _ in range(ngrp)]
        tok_ev = [[None] * NGR for _ in range(ngrp)]
        tok_out = [[None] * 4 for _ in range(ngrp)]
        tok_stf = [[None] * 4 for _ in range(ngrp)]
        tok_p2read = [None, None]
        tok_r6w = [None, None]
        tok_su6 = [None, None]
        tok_r6pair = [None, None]
        tok_r6last = [None] * ngrp

        def emit_chain(q):
            for blk in range(ICH // 2):
                b = blk % 2
                ii0 = blk * 2
                w = [t_c6, tok_p2read[b]]
                tl = p.dma(f"inp{b}", lambda e, q=q, ii0=ii0, b=b: e.dma_start(
                    out=P2b[b][:].rearrange("p (k t) -> p k t", t=TQ),
                    in_=P2_d.ap()[ii0 * NT:(ii0 + 2) * NT,
                                  q * TQ:(q + 1) * TQ].rearrange(
                        "(k p) t -> p k t", p=NT)), waits=w)
                ta = p.op("a", lambda e, b=b: e.activation(
                    t6b[b][:], P2b[b][:], Act.Identity, bias=bC[:],
                    scale=c6c[:]),
                    waits=[tl, t_c6, tok_r6w[b]])
                tok_p2read[b] = ta
                tsu = p.op("v", lambda e, b=b: e.tensor_scalar(
                    t6b[b][:], t6b[b][:], -MAGIC, None, Alu.add), waits=[ta])
                ti = p.op("v", lambda e, b=b: e.tensor_copy(
                    r6i[b][:], t6b[b][:]), waits=[tsu, tok_su6[b]])
                tok_r6w[b] = ti
                wq = [ti]
                if q >= 2 and blk <= 1:
                    wq.append(tok_mm[q - 2][NGR - 1])
                lastv = p.op("v", lambda e, q=q, ii0=ii0, b=b: e.tensor_copy(
                    r6g[q % 2][:, ii0 * TQ:(ii0 + 2) * TQ], r6i[b][:]),
                    waits=wq)
                tok_su6[b] = lastv
                tok_r6pair[b] = lastv
            tok_r6last[q] = (tok_r6pair[0], tok_r6pair[1])

        def emit_mm(q):
            for tt in range(4):
                wl = []
                if tt >= 2:
                    wl.append(tok_out[q][tt - 2])
                elif q >= 1:
                    wl.append(tok_out[q - 1][2 + tt])
                tok_ld = p.dma(f"inq{tt % 2}", lambda e, q=q, tt=tt:
                               e.dma_start(
                    out=prt[tt % 2][:],
                    in_=part_d.ap()[q * TQ + tt * NT:
                                    q * TQ + (tt + 1) * NT, :]), waits=wl)
                for half in range(2):
                    g = tt * 2 + half
                    ps = psd[g % 3]
                    wm = [tok_r6last[q][0], tok_r6last[q][1], tok_wd]
                    if g >= 3:
                        wm.append(tok_ev[q][g - 3])
                    elif q >= 1:
                        wm.append(tok_ev[q - 1][6 + g if g < 2 else 5])
                    last = None
                    for ii in range(ICH):
                        last = p.op("t", lambda e, ps=ps, ii=ii, tt=tt,
                                    half=half, q=q: e.matmul(
                                        ps[:],
                                        r6g[q % 2][:, ii * TQ + tt * NT:
                                                   ii * TQ + (tt + 1) * NT],
                                        wdt[:, ii * H + half * TQ:
                                            ii * H + (half + 1) * TQ],
                                        start=(ii == 0), stop=(ii == ICH - 1)),
                                    waits=wm, quiet=(ii != ICH - 1))
                        wm = []
                    tok_mm[q][g] = last
                    we = [last, t_s6]
                    if tt >= 2 and half == 0:
                        we.append(tok_out[q][tt - 2])
                    elif q >= 1 and half == 0:
                        we.append(tok_out[q - 1][(tt + 2) % 4])
                    tok_ev[q][g] = p.op(
                        "a", lambda e, ps=ps, tt=tt, half=half: e.activation(
                            osb[tt % 2][:, half * TQ:(half + 1) * TQ], ps[:],
                            Act.Identity, bias=0.0, scale=s6c[:]), waits=we)
                wf = [tok_ev[q][tt * 2], tok_ev[q][tt * 2 + 1], tok_ld]
                if tt >= 2:
                    wf.append(tok_stf[q][tt - 2])
                elif q >= 1:
                    wf.append(tok_stf[q - 1][2 + tt])
                tf = p.op("v", lambda e, tt=tt: e.tensor_tensor(
                    fsb[tt % 2][:], osb[tt % 2][:], prt[tt % 2][:], Alu.add),
                    waits=wf)
                tok_out[q][tt] = tf
                tok_stf[q][tt] = p.dma(f"sou{tt % 2}", lambda e, q=q, tt=tt:
                                       e.dma_start(
                    out=out_d.ap()[q * TQ + tt * NT:q * TQ + (tt + 1) * NT, :],
                    in_=fsb[tt % 2][:]), waits=[tf], eng="gp")

        for q in range(ngrp + 1):
            if q < ngrp:
                emit_chain(q)
            if q >= 1:
                emit_mm(q - 1)

        p.barrier_all()
        p.emit_block(nc, es)
    return nc



def _gather_max(results, key):
    m = np.stack([np.asarray(r[key], np.float32).reshape(NMAX)
                  for r in results])
    return np.ascontiguousarray(m.reshape(1, 8 * NMAX))


def kernel(L0_attn_x, L0_attn_filt, L0_ffn_w_gate, L0_ffn_w_up, L0_ffn_w_down):
    x = np.ascontiguousarray(np.asarray(L0_attn_x, np.float32))
    filt = np.ascontiguousarray(np.asarray(L0_attn_filt, np.float32))
    wg = np.ascontiguousarray(np.asarray(L0_ffn_w_gate, np.float32))
    wu = np.ascontiguousarray(np.asarray(L0_ffn_w_up, np.float32))
    wd = np.ascontiguousarray(np.asarray(L0_ffn_w_down, np.float32))
    B, S, _ = x.shape
    T = S
    cores = list(range(NCORES))
    Fz = build_toeplitz(filt)

    r1 = run_bass_kernel_spmd(
        build_L1(T), [{"xs": x[c], "Fz": Fz} for c in range(NCORES)], cores)
    m1all = _gather_max(r1.results, "m1loc")

    r2 = run_bass_kernel_spmd(
        build_L2(T),
        [{"xh": np.asarray(r["xh"]), "xl": np.asarray(r["xl"]),
          "wu": wu, "wg": wg, "m1all": m1all} for r in r1.results], cores)
    m23all = _gather_max(r2.results, "m23loc")

    r3 = run_bass_kernel_spmd(
        build_L3(T),
        [{"gt": np.asarray(r["gt"]), "m1all": m1all, "m23all": m23all}
         for r in r2.results], cores)
    mPall = _gather_max(r3.results, "mPloc")

    r4 = run_bass_kernel_spmd(
        build_L4(T),
        [{"ut": np.asarray(r2.results[c]["ut"]),
          "P": np.asarray(r3.results[c]["P"]),
          "m23all": m23all, "mPall": mPall} for c in range(NCORES)], cores)
    mP2all = _gather_max(r4.results, "mP2loc")

    r5 = run_bass_kernel_spmd(
        build_L5(T),
        [{"P2": np.asarray(r4.results[c]["P2"]), "wd": wd,
          "partial": np.asarray(r1.results[c]["partial"]),
          "m1all": m1all, "m23all": m23all, "mPall": mPall,
          "mP2all": mP2all} for c in range(NCORES)], cores)

    out = np.stack([np.asarray(r["out"], np.float32) for r in r5.results])
    return np.ascontiguousarray(out)


# version 09
# speedup vs baseline: 2.0771x; 1.0511x over previous
import contextlib

import numpy as np
import ml_dtypes

import concourse.bass as bass
import concourse.mybir as mybir
from concourse.bass_utils import run_bass_kernel_spmd

F32 = mybir.dt.float32
BF16 = mybir.dt.bfloat16
I16 = mybir.dt.int16
U16 = mybir.dt.uint16
Alu = mybir.AluOpType
Act = mybir.ActivationFunctionType
AX = mybir.AxisListType

NCORES = 8
H = 1024
I3 = 3072
NT = 128
LF = 2048
ND = 17
RING = 18
NMAX = 16
TQ = 512
CH = 2048
INV7 = float(np.float32(1.0 / 7.0))
INVQ = float(np.float32(1.0 / 32767.0))
Q15 = 32767.0
Q16 = 65536.0
INV65536 = float(np.float32(1.0 / 65536.0))
INVH = float(np.float32(1.0 / H))
EPS12 = 1e-12
MAGIC = 12582912.0

BF = ml_dtypes.bfloat16


def build_toeplitz(filt):
    out = np.zeros((ND, NT, NT), np.float32)
    j, i = np.meshgrid(np.arange(NT), np.arange(NT), indexing="ij")
    for d in range(ND):
        idx = NT * d + j - i
        m = (idx >= 0) & (idx < LF)
        out[d][m] = filt[idx[m]]
    return np.ascontiguousarray(out.reshape(ND * NT, NT))


class Plan:

    ENGS = ("sy", "gp", "v", "a", "t")

    def __init__(self):
        self.ops = {k: [] for k in self.ENGS}
        self.count = {}
        for k in ("v", "a", "t"):
            self.count[k] = 0
        self.lastw = {k: {} for k in self.ENGS}
        self.chan_eng = {}
        self.chan_last = {}

    def _record(self, eng, sem, emit, waits, inc):
        seen = {}
        for tok in waits:
            if tok is None:
                continue
            s, v = tok
            if v <= 0 or self.lastw[eng].get(s, 0) >= v:
                continue
            seen[s] = max(seen.get(s, 0), v)
        w = []
        for s, v in seen.items():
            w.append((s, v))
            self.lastw[eng][s] = v
        self.count[sem] = self.count.get(sem, 0) + inc
        self.ops[eng].append((w, emit, sem, inc))
        return (sem, self.count[sem])

    def op(self, eng, emit, waits=(), quiet=False):
        if quiet:
            return self._record(eng, eng, emit, waits, 0)
        return self._record(eng, eng, emit, waits, 1)

    def dma(self, chan, emit, waits=(), eng="sy"):
        if chan not in self.chan_eng:
            self.chan_eng[chan] = eng
        eng = self.chan_eng[chan]
        w = list(waits) + [self.chan_last.get(chan)]
        tok = self._record(eng, chan, emit, w, 16)
        self.chan_last[chan] = tok
        return tok

    def barrier_all(self, eng="gp"):
        waits = [(s, c) for s, c in self.count.items() if c > 0]
        self._record(eng, eng if eng in self.count else "v",
                     lambda e: None, waits, 0)

    def emit_block(self, nc, es):
        sems = {s: es.enter_context(nc.semaphore("sem_" + s))
                for s in self.count}
        with nc.Block() as block:
            def runner(engname):
                def run(engobj):
                    for (waits, emit, sem, inc) in self.ops[engname]:
                        for (s, v) in waits:
                            engobj.wait_ge(sems[s], v)
                        ins = emit(engobj)
                        if ins is not None and inc > 0:
                            ins.then_inc(sems[sem], inc)
                return run
            if self.ops["sy"]:
                block.sync(runner("sy"))
            if self.ops["gp"]:
                block.gpsimd(runner("gp"))
            if self.ops["v"]:
                block.vector(runner("v"))
            if self.ops["a"]:
                block.scalar(runner("a"))
            if self.ops["t"]:
                block.tensor(runner("t"))


def maxes_prologue(nc, es, p, names):
    tiles, toks = {}, {}
    for n in names:
        inp = nc.dram_tensor(n, [1, 8 * NMAX], F32, kind="ExternalInput")
        scr = nc.dram_tensor("scr_" + n, [1, NMAX], F32, kind="Internal")
        mh = es.enter_context(nc.sbuf_tensor("mh_" + n, [1, 8 * NMAX], F32))
        m4 = es.enter_context(nc.sbuf_tensor("m4_" + n, [1, 4 * NMAX], F32))
        m2 = es.enter_context(nc.sbuf_tensor("m2_" + n, [1, 2 * NMAX], F32))
        m1 = es.enter_context(nc.sbuf_tensor("m1_" + n, [1, NMAX], F32))
        mx = es.enter_context(nc.sbuf_tensor("mx_" + n, [NT, NMAX], F32))
        t = p.dma("mis", lambda e, inp=inp, mh=mh: e.dma_start(
            out=mh[:], in_=inp.ap()))
        t = p.op("v", lambda e, mh=mh, m4=m4: e.tensor_tensor(
            m4[:], mh[:, :4 * NMAX], mh[:, 4 * NMAX:], Alu.max), waits=[t])
        t = p.op("v", lambda e, m4=m4, m2=m2: e.tensor_tensor(
            m2[:], m4[:, :2 * NMAX], m4[:, 2 * NMAX:], Alu.max), waits=[t])
        t = p.op("v", lambda e, m2=m2, m1=m1: e.tensor_tensor(
            m1[:], m2[:, :NMAX], m2[:, NMAX:], Alu.max), waits=[t])
        t = p.dma("mis", lambda e, scr=scr, m1=m1: e.dma_start(
            out=scr.ap(), in_=m1[:]), waits=[t])
        t = p.dma("mis", lambda e, mx=mx, scr=scr: e.dma_start(
            out=mx[:], in_=scr.ap().broadcast_to([NT, NMAX])), waits=[t])
        tiles[n] = mx
        toks[n] = t
    return tiles, toks


def max_finalize(nc, es, p, maxacc, idf_sb, out_name, extra_waits=(),
                 psf=None):
    mout = nc.dram_tensor(out_name, [NMAX, 1], F32, kind="ExternalOutput")
    if psf is None:
        psf = es.enter_context(nc.psum_tensor("ps_mfin", [NT, NT], F32))
    mfs = es.enter_context(nc.sbuf_tensor("mfin_s", [NMAX, NT], F32))
    mfr = es.enter_context(nc.sbuf_tensor("mfin_r", [NMAX, 1], F32))
    w = list(extra_waits) + [("v", p.count["v"])]
    t = p.op("t", lambda e: e.transpose(psf[:NMAX, :NT], maxacc[:],
                                        idf_sb[:]), waits=w)
    t = p.op("v", lambda e: e.tensor_copy(mfs[:], psf[:NMAX, :NT]), waits=[t])
    t = p.op("v", lambda e: e.tensor_reduce(mfr[:], mfs[:], AX.X, Alu.max),
             waits=[t])
    return p.dma("mis", lambda e: e.dma_start(out=mout.ap(), in_=mfr[:]),
                 waits=[t])



def plan_round_signed(p, t_tok, tbuf, ibuf, dbuf, out_emit):
    tk = p.op("v", lambda e: e.tensor_copy(ibuf, tbuf), waits=[t_tok])
    tk = p.op("v", lambda e: e.tensor_tensor(dbuf, tbuf, ibuf, Alu.subtract),
              waits=[tk])
    tk = p.op("v", lambda e: e.scalar_tensor_tensor(
        tbuf, dbuf, 0.5, ibuf, Alu.is_ge, Alu.add), waits=[tk])
    tk = p.op("v", lambda e: out_emit(e, dbuf, tbuf), waits=[tk])
    return tk


def plan_round_u16(p, t_tok, tbuf, ubuf, dbuf, out, clip_hi):
    tk = p.op("v", lambda e: e.tensor_copy(ubuf, tbuf), waits=[t_tok])
    tk = p.op("v", lambda e: e.tensor_tensor(dbuf, tbuf, ubuf, Alu.subtract),
              waits=[tk])
    tk = p.op("v", lambda e: e.scalar_tensor_tensor(
        tbuf, dbuf, 0.5, ubuf, Alu.is_ge, Alu.add), waits=[tk])
    tk = p.op("v", lambda e: e.tensor_scalar(
        out, tbuf, float(clip_hi), None, Alu.min), waits=[tk])
    return tk


def dequant_weights(nc, es, p, specs, idb_sb, tag, psw=None, tok_id=None):
    CT = 1024
    wbuf = [es.enter_context(nc.sbuf_tensor(f"wb{tag}{k}", [NT, CT], F32))
            for k in range(2)]
    wqb = [es.enter_context(nc.sbuf_tensor(f"wq{tag}{k}", [NT, CT], BF16))
           for k in range(2)]
    absr = es.enter_context(nc.sbuf_tensor(f"ab{tag}", [NT, CT // 32], F32))
    sblk = es.enter_context(nc.sbuf_tensor(f"sb{tag}", [NT, CT // 32], F32))
    sinv = es.enter_context(nc.sbuf_tensor(f"si{tag}", [NT, CT // 32], F32))
    tqb = es.enter_context(nc.sbuf_tensor(f"tq{tag}", [NT, CT], F32))
    dqb = es.enter_context(nc.sbuf_tensor(f"dq{tag}", [NT, CT], F32))
    kqb = es.enter_context(nc.sbuf_tensor(f"kq{tag}", [NT, CT], I16))
    bCw = es.enter_context(nc.sbuf_tensor(f"bCw{tag}", [NT, 1], F32))
    bNCw = es.enter_context(nc.sbuf_tensor(f"bNCw{tag}", [NT, 1], F32))
    t_bw = p.op("v", lambda e: e.memset(bCw[:], MAGIC))
    t_bw = p.op("v", lambda e: e.memset(bNCw[:], -MAGIC), waits=[t_bw])
    if psw is None:
        psw = [es.enter_context(nc.psum_tensor(f"pw{tag}{k}", [NT, 8 * NT],
                                               BF16)) for k in range(2)]
    last_evac = None
    tok_consumed = [None, None]
    tok_wq_read = [None, None]
    tokmap = {}
    nblk = CT // 32
    n = 0
    order = []
    for ii in range(max(nr // NT for (_, nr, _, _) in specs)):
        for si, (w_dram, nrows, ncols, wt_sb) in enumerate(specs):
            if ii < nrows // NT:
                order.append((si, ii, w_dram, nrows, ncols, wt_sb))
    for (si, ii, w_dram, nrows, ncols, wt_sb) in order:
        if True:
            for cc in range(ncols // CT):
                par = n % 2
                n += 1
                wb, wq = wbuf[par], wqb[par]
                tl = p.dma("wld", lambda e, wb=wb, w_dram=w_dram, ii=ii, cc=cc:
                           e.dma_start(
                               out=wb[:],
                               in_=w_dram.ap()[ii * NT:(ii + 1) * NT,
                                               cc * CT:(cc + 1) * CT]),
                           waits=[tok_consumed[par]])
                tv = p.op("v", lambda e, wb=wb: e.tensor_reduce(
                    absr[:].rearrange("p (b o) -> p b o", o=1),
                    wb[:].rearrange("p (b e) -> p b e", e=32),
                    AX.X, Alu.max, apply_absolute_value=True), waits=[tl])
                tv = p.op("v", lambda e: e.tensor_scalar(
                    sblk[:], absr[:], INV7, EPS12, Alu.mult, Alu.max),
                    waits=[tv])
                tv = p.op("v", lambda e: e.reciprocal(sinv[:], sblk[:]),
                          waits=[tv])
                tv = p.op("v", lambda e: e.tensor_scalar(
                    sinv[:], sinv[:], -1.0, None, Alu.mult), waits=[tv])
                tv = p.op("v", lambda e, wb=wb: e.tensor_tensor(
                    tqb[:].rearrange("p (b e) -> p b e", e=32),
                    wb[:].rearrange("p (b e) -> p b e", e=32),
                    sinv[:].rearrange("p (b o) -> p b o", o=1)
                    .broadcast_to([NT, nblk, 32]), Alu.mult), waits=[tv])
                tok_consumed[par] = tv
                tv = p.op("a", lambda e: e.activation(
                    dqb[:], tqb[:], Act.Identity, bias=bCw[:], scale=1.0),
                    waits=[tv, t_bw])
                tv = p.op("a", lambda e: e.activation(
                    kqb[:], dqb[:], Act.Identity, bias=bNCw[:], scale=1.0),
                    waits=[tv])
                tv = p.op("v", lambda e, wq=wq: e.tensor_tensor(
                    wq[:].rearrange("p (b e) -> p b e", e=32),
                    kqb[:].rearrange("p (b e) -> p b e", e=32),
                    sblk[:].rearrange("p (b o) -> p b o", o=1)
                    .broadcast_to([NT, nblk, 32]), Alu.mult),
                    waits=[tv, tok_wq_read[par]])
                ps = psw[n % len(psw)]
                ttr = None
                for ci in range(CT // NT):
                    ttr = p.op("t", lambda e, ps=ps, wq=wq, ci=ci:
                               e.transpose(
                        ps[:, ci * NT:(ci + 1) * NT],
                        wq[:, ci * NT:(ci + 1) * NT], idb_sb[:]),
                        waits=[tv, last_evac, tok_id], quiet=(ci != 7))
                last_evac = p.op(
                    "a", lambda e, ps=ps, wt_sb=wt_sb, cc=cc, ii=ii,
                    nrows=nrows: e.activation(
                        wt_sb[:].rearrange("p (cig m) -> p cig m", m=nrows)
                        [:, cc * 8:(cc + 1) * 8, ii * NT:(ii + 1) * NT],
                        ps[:], Act.Identity), waits=[ttr])
                tok_wq_read[par] = last_evac
                tokmap[(si, ii)] = last_evac
    return last_evac, tokmap



def build_L1(T):
    ntiles = T // NT
    nc = bass.Bass("TRN2", target_bir_lowering=False)
    xs = nc.dram_tensor("xs", [T, H], F32, kind="ExternalInput")
    Fz = nc.dram_tensor("Fz", [ND * NT, NT], F32, kind="ExternalInput")
    xh_d = nc.dram_tensor("xh", [T, H], BF16, kind="ExternalOutput")
    xl_d = nc.dram_tensor("xl", [T, H], BF16, kind="ExternalOutput")
    part_d = nc.dram_tensor("partial", [T, H], F32, kind="ExternalOutput")
    idf_c = nc.inline_tensor(np.eye(NT, dtype=np.float32), name="idf")

    p = Plan()
    es = contextlib.ExitStack()
    with es:
        xr = es.enter_context(nc.sbuf_tensor("xr", [NT, RING * H], F32))
        ff = es.enter_context(nc.sbuf_tensor("ff", [NT, ND * NT], F32))
        fh = es.enter_context(nc.sbuf_tensor("fh", [NT, ND * NT], BF16))
        fl = es.enter_context(nc.sbuf_tensor("fl", [NT, ND * NT], BF16))
        xhr = es.enter_context(nc.sbuf_tensor("xhr", [NT, RING * H], BF16))
        xlr = es.enter_context(nc.sbuf_tensor("xlr", [NT, RING * H], BF16))
        xn = es.enter_context(nc.sbuf_tensor("xn", [NT, H], F32))
        dump = es.enter_context(nc.sbuf_tensor("dump", [NT, H], F32))
        smv = es.enter_context(nc.sbuf_tensor("smv", [NT, 1], F32))
        sqv = es.enter_context(nc.sbuf_tensor("sqv", [NT, 1], F32))
        muv = es.enter_context(nc.sbuf_tensor("muv", [NT, 1], F32))
        ex2 = es.enter_context(nc.sbuf_tensor("ex2", [NT, 1], F32))
        mu2 = es.enter_context(nc.sbuf_tensor("mu2", [NT, 1], F32))
        varv = es.enter_context(nc.sbuf_tensor("varv", [NT, 1], F32))
        rrv = es.enter_context(nc.sbuf_tensor("rrv", [NT, 1], F32))
        rsv = es.enter_context(nc.sbuf_tensor("rsv", [NT, 1], F32))
        mtv = es.enter_context(nc.sbuf_tensor("mtv", [NT, 1], F32))
        maxacc = es.enter_context(nc.sbuf_tensor("maxacc", [NT, NMAX], F32))
        idf_sb = es.enter_context(nc.sbuf_tensor("idf_sb", [NT, NT], F32))
        parts = [es.enter_context(nc.sbuf_tensor(f"part{k}", [NT, H], F32))
                 for k in range(2)]
        psc = [es.enter_context(nc.psum_tensor(f"psc{k}", [NT, H], F32))
               for k in range(3)]

        t_id = p.dma("mis", lambda e: e.dma_start(
            out=idf_sb[:], in_=idf_c.ap()))
        t_f = p.dma("mis", lambda e: e.dma_start(
            out=ff[:].rearrange("p (d i) -> p d i", i=NT),
            in_=Fz.ap().rearrange("(d j) i -> j d i", j=NT)))
        t_fh = p.op("v", lambda e: e.tensor_copy(fh[:], ff[:]), waits=[t_f])
        t_fl = p.op("v", lambda e: e.tensor_tensor(
            fl[:], ff[:], fh[:], Alu.subtract), waits=[t_fh])
        t_mz = p.op("v", lambda e: e.memset(maxacc[:], 0.0))

        tok_load = [None] * ntiles
        tok_xl = [None] * ntiles
        tok_store_hl = [None] * ntiles
        tok_evac = [None] * ntiles
        tok_store = [None] * ntiles
        tok_statsfree = [None] * ntiles

        def emit_stats(i):
            sl = (i % RING) * H
            xb = xr[:, sl:sl + H]
            w = [tok_load[i]]
            if i >= 1:
                w.append(tok_statsfree[i - 1])
            ta = p.op("a", lambda e, xb=xb: e.activation(
                dump[:], xb, Act.Square, accum_out=sqv[:]), waits=w)
            ta = p.op("a", lambda e, xb=xb: e.activation(
                dump[:], xb, Act.Identity, accum_out=smv[:]), waits=[ta])
            tv = p.op("v", lambda e: e.tensor_scalar(
                muv[:], smv[:], INVH, None, Alu.mult), waits=[ta])
            tv = p.op("v", lambda e: e.tensor_scalar(
                ex2[:], sqv[:], INVH, None, Alu.mult), waits=[tv])
            tok_statsfree[i] = tv
            tv = p.op("v", lambda e: e.tensor_tensor(
                mu2[:], muv[:], muv[:], Alu.mult), waits=[tv])
            tv = p.op("v", lambda e: e.scalar_tensor_tensor(
                varv[:], ex2[:], 1e-5, mu2[:], Alu.add, Alu.subtract),
                waits=[tv])
            ta2 = p.op("a", lambda e: e.activation(
                rrv[:], varv[:], Act.Sqrt), waits=[tv])
            tv = p.op("v", lambda e: e.reciprocal(rsv[:], rrv[:]), waits=[ta2])
            tv = p.op("v", lambda e, xb=xb: e.scalar_tensor_tensor(
                xn[:], xb, muv[:], rsv[:].broadcast_to([NT, H]),
                Alu.subtract, Alu.mult), waits=[tv])
            tv = p.op("v", lambda e: e.tensor_reduce(
                mtv[:], xn[:], AX.X, Alu.max, apply_absolute_value=True),
                waits=[tv])
            tv = p.op("v", lambda e: e.tensor_tensor(
                maxacc[:, 0:1], maxacc[:, 0:1], mtv[:], Alu.max),
                waits=[tv, t_mz])
            wh = [tv, t_fl]
            if i >= RING:
                wh.extend(tok_store_hl[i - RING])
            th = p.op("v", lambda e, sl=sl: e.tensor_copy(
                xhr[:, sl:sl + H], xn[:]), waits=wh)
            tl = p.op("v", lambda e, sl=sl: e.tensor_tensor(
                xlr[:, sl:sl + H], xn[:], xhr[:, sl:sl + H], Alu.subtract),
                waits=[th])
            tok_xl[i] = tl
            t1 = p.dma("sxh", lambda e, sl=sl, i=i: e.dma_start(
                out=xh_d.ap()[i * NT:(i + 1) * NT, :], in_=xhr[:, sl:sl + H]),
                waits=[tl], eng="gp")
            t2 = p.dma("sxl", lambda e, sl=sl, i=i: e.dma_start(
                out=xl_d.ap()[i * NT:(i + 1) * NT, :], in_=xlr[:, sl:sl + H]),
                waits=[tl], eng="gp")
            tok_store_hl[i] = (t1, t2)

        def emit_conv(j):
            ps = psc[j % 3]
            dmax = min(ND - 1, ntiles - 1 - j)
            w = [tok_xl[j + dmax], t_fl]
            if j >= 3:
                w.append(tok_evac[j - 3])
            n_terms = 3 * (dmax + 1)
            cnt = 0
            first = [True, True]
            last = None
            for d in range(dmax + 1):
                sl = ((j + d) % RING) * H
                for (fm, xm) in ((fh, xhr), (fl, xhr), (fh, xlr)):
                    cnt += 1
                    stop = (cnt == n_terms)
                    for half in range(2):
                        cs = slice(half * 512, half * 512 + 512)
                        st = first[half]
                        first[half] = False
                        is_last = (cnt == n_terms and half == 1)
                        last = p.op("t", lambda e, ps=ps, fm=fm, xm=xm, d=d,
                                    sl=sl, cs=cs, st=st, stop=stop: e.matmul(
                                        ps[:, cs], fm[:, d * NT:(d + 1) * NT],
                                        xm[:, sl + cs.start:sl + cs.stop],
                                        start=st, stop=stop), waits=w,
                                    quiet=not is_last)
                        w = []
            sl = (j % RING) * H
            pb = parts[j % 2]
            we = [last, tok_load[j]]
            if j >= 2:
                we.append(tok_store[j - 2])
            te = p.op("v", lambda e, ps=ps, sl=sl, pb=pb: e.tensor_tensor(
                pb[:], ps[:], xr[:, sl:sl + H], Alu.add), waits=we)
            tok_evac[j] = te
            tok_store[j] = p.dma("spt", lambda e, pb=pb, j=j: e.dma_start(
                out=part_d.ap()[j * NT:(j + 1) * NT, :], in_=pb[:]),
                waits=[te], eng="gp")

        for i in range(ntiles):
            w = []
            if i >= RING:
                w.append(tok_evac[i - RING])
            tok_load[i] = p.dma(f"xi{i % 2}", lambda e, i=i: e.dma_start(
                out=xr[:, (i % RING) * H:(i % RING) * H + H],
                in_=xs.ap()[i * NT:(i + 1) * NT, :]), waits=w)
            emit_stats(i)
            if i >= ND - 1:
                emit_conv(i - (ND - 1))
        for j in range(max(0, ntiles - (ND - 1)), ntiles):
            emit_conv(j)

        max_finalize(nc, es, p, maxacc, idf_sb, "m1loc", [t_id])
        p.barrier_all()
        p.emit_block(nc, es)
    return nc



def build_L2(T):
    ngrp = T // TQ
    ICH = I3 // NT
    nc = bass.Bass("TRN2", target_bir_lowering=False)
    xh_d = nc.dram_tensor("xh", [T, H], BF16, kind="ExternalInput")
    xl_d = nc.dram_tensor("xl", [T, H], BF16, kind="ExternalInput")
    wu_d = nc.dram_tensor("wu", [I3, H], F32, kind="ExternalInput")
    wg_d = nc.dram_tensor("wg", [I3, H], F32, kind="ExternalInput")
    ut_d = nc.dram_tensor("ut", [I3, T], F32, kind="ExternalOutput")
    gt_d = nc.dram_tensor("gt", [I3, T], F32, kind="ExternalOutput")
    idb_c = nc.inline_tensor(np.eye(NT, dtype=BF), name="idb")
    idf_c = nc.inline_tensor(np.eye(NT, dtype=np.float32), name="idf")

    p = Plan()
    es = contextlib.ExitStack()
    with es:
        mxt, mxtok = maxes_prologue(nc, es, p, ["m1all"])
        mx1 = mxt["m1all"]
        idb_sb = es.enter_context(nc.sbuf_tensor("idb_sb", [NT, NT], BF16))
        idf_sb = es.enter_context(nc.sbuf_tensor("idf_sb", [NT, NT], F32))
        s1c = es.enter_context(nc.sbuf_tensor("s1c", [NT, 1], F32))
        c1c = es.enter_context(nc.sbuf_tensor("c1c", [NT, 1], F32))
        wut = es.enter_context(nc.sbuf_tensor("wut", [NT, 8 * I3], BF16))
        wgt = es.enter_context(nc.sbuf_tensor("wgt", [NT, 8 * I3], BF16))
        xhg = [es.enter_context(nc.sbuf_tensor(f"xhg{k}", [NT, 4 * H], BF16))
               for k in range(2)]
        xlg = [es.enter_context(nc.sbuf_tensor(f"xlg{k}", [NT, 4 * H], BF16))
               for k in range(2)]
        xqt = [es.enter_context(nc.sbuf_tensor(f"xqt{k}", [NT, 8 * TQ], BF16))
               for k in range(2)]
        xnb = es.enter_context(nc.sbuf_tensor("xnb", [NT, H], F32))
        t1b = es.enter_context(nc.sbuf_tensor("t1b", [NT, H], F32))
        r1i = es.enter_context(nc.sbuf_tensor("r1i", [NT, H], I16))
        bC = es.enter_context(nc.sbuf_tensor("bC", [NT, 1], F32))
        bNC = es.enter_context(nc.sbuf_tensor("bNC", [NT, 1], F32))
        r1f = [es.enter_context(nc.sbuf_tensor(f"r1f{k}", [NT, H], BF16))
               for k in range(4)]
        ue = [es.enter_context(nc.sbuf_tensor(f"ue{k}", [NT, TQ], F32))
              for k in range(3)]
        ge = [es.enter_context(nc.sbuf_tensor(f"ge{k}", [NT, TQ], F32))
              for k in range(3)]
        nch_mm = (T // TQ) * (I3 // NT)
        urd = es.enter_context(nc.sbuf_tensor("urd", [NT, nch_mm], F32))
        grd = es.enter_context(nc.sbuf_tensor("grd", [NT, nch_mm], F32))
        maxacc = es.enter_context(nc.sbuf_tensor("maxacc", [NT, NMAX], F32))
        psu = [es.enter_context(nc.psum_tensor(f"pu{k}", [NT, TQ], F32))
               for k in range(3)]
        psg = [es.enter_context(nc.psum_tensor(f"pg{k}", [NT, TQ], F32))
               for k in range(3)]
        psx = [es.enter_context(nc.psum_tensor(f"px{k}", [NT, 8 * NT], BF16))
               for k in range(1)]
        psxq = es.enter_context(nc.psum_tensor("pxq", [NT, 8 * NT], BF16))

        t_id = p.dma("mis", lambda e: e.dma_start(
            out=idb_sb[:], in_=idb_c.ap()))
        t_idf = p.dma("mis", lambda e: e.dma_start(
            out=idf_sb[:], in_=idf_c.ap()))
        t_mz = p.op("v", lambda e: e.memset(maxacc[:], 0.0))
        t_mz = p.op("v", lambda e: e.memset(bC[:], MAGIC), waits=[t_mz])
        t_mz = p.op("v", lambda e: e.memset(bNC[:], -MAGIC), waits=[t_mz])
        tv = p.op("v", lambda e: e.tensor_scalar(
            s1c[:], mx1[:, 0:1], INVQ, EPS12, Alu.mult, Alu.max),
            waits=[mxtok["m1all"]])
        t_c1 = p.op("v", lambda e: e.reciprocal(c1c[:], s1c[:]),
                    waits=[tv, t_mz])

        tok_w, tok_wmap = dequant_weights(
            nc, es, p, [(wu_d, I3, H, wut), (wg_d, I3, H, wgt)], idb_sb, "w",
            psw=psx, tok_id=t_id)

        tok_mm = {"u": [[None] * ICH for _ in range(ngrp)],
                  "g": [[None] * ICH for _ in range(ngrp)]}
        tok_ev = {"u": [[None] * ICH for _ in range(ngrp)],
                  "g": [[None] * ICH for _ in range(ngrp)]}
        tok_st = {"u": [[None] * ICH for _ in range(ngrp)],
                  "g": [[None] * ICH for _ in range(ngrp)]}
        tok_rd = {"u": [[None] * ICH for _ in range(ngrp)],
                  "g": [[None] * ICH for _ in range(ngrp)]}
        tok_ld = [None] * ngrp
        tok_chainTT = [None] * ngrp
        tok_xqt = [None] * ngrp
        st = {"t1": None, "r1i": None, "xev": tok_w,
              "rf": [None] * 4, "rfw": [None] * 4}

        def emit_loads(q):
            w = [tok_chainTT[q - 2]] if q >= 2 else []
            tl = p.dma(f"xi{q % 2}", lambda e, q=q: e.dma_start(
                out=xhg[q % 2][:].rearrange("p (k h) -> p k h", h=H),
                in_=xh_d.ap()[q * TQ:(q + 1) * TQ, :].rearrange(
                    "(k p) h -> p k h", p=NT)), waits=w)
            tok_ld[q] = p.dma(f"xi{q % 2}", lambda e, q=q: e.dma_start(
                out=xlg[q % 2][:].rearrange("p (k h) -> p k h", h=H),
                in_=xl_d.ap()[q * TQ:(q + 1) * TQ, :].rearrange(
                    "(k p) h -> p k h", p=NT)), waits=[tl])

        def emit_chain_compute(q, k):
            cs = slice(k * H, (k + 1) * H)
            tv = p.op("v", lambda e, q=q, cs=cs: e.tensor_tensor(
                xnb[:], xhg[q % 2][:, cs], xlg[q % 2][:, cs], Alu.add),
                waits=[tok_ld[q], t_c1, st["t1"]])
            if k == 3:
                tok_chainTT[q] = tv
            ta1 = p.op("a", lambda e: e.activation(
                t1b[:], xnb[:], Act.Identity, bias=bC[:], scale=c1c[:]),
                waits=[tv, st["r1i"]])
            st["t1"] = ta1
            ta2 = p.op("a", lambda e: e.activation(
                r1i[:], t1b[:], Act.Identity, bias=bNC[:], scale=1.0),
                waits=[ta1])
            st["r1i"] = ta2
            rf = r1f[k]
            st["rfw"][k] = p.op("a", lambda e, rf=rf: e.activation(
                rf[:], r1i[:], Act.Identity), waits=[ta2, st["rf"][k]])

        def emit_chain_transpose(q, k):
            rf = r1f[k]
            wt = [st["rfw"][k], t_id, st["xev"]]
            if k == 0 and q >= 2:
                wt += [tok_mm["u"][q - 2][ICH - 1], tok_mm["g"][q - 2][ICH - 1]]
            ttr = None
            for ci in range(8):
                ttr = p.op("t", lambda e, rf=rf, ci=ci: e.transpose(
                    psxq[:, ci * NT:(ci + 1) * NT],
                    rf[:, ci * NT:(ci + 1) * NT], idb_sb[:]),
                    waits=wt, quiet=(ci != 7))
                wt = []
            st["rf"][k] = ttr
            xev = p.op("a", lambda e, q=q, k=k: e.activation(
                xqt[q % 2][:].rearrange("p (ci tq) -> p ci tq", tq=TQ)
                [:, :, k * NT:(k + 1) * NT], psxq[:], Act.Identity),
                waits=[ttr])
            st["xev"] = xev
            if k == 3:
                tok_xqt[q] = xev

        emit_loads(0)
        for k in range(4):
            emit_chain_compute(0, k)
            emit_chain_transpose(0, k)

        for q in range(ngrp):
            for ii in range(ICH):
                if q + 1 < ngrp:
                    if ii == 0:
                        emit_loads(q + 1)
                    elif ii in (2, 5, 8, 11):
                        emit_chain_compute(q + 1, (ii - 2) // 3)
                    elif ii in (14, 16, 18, 20):
                        emit_chain_transpose(q + 1, (ii - 14) // 2)
                for si, (key, wt_sb, psv, ev, rd, dst) in enumerate((
                    ("u", wut, psu, ue, urd, ut_d),
                    ("g", wgt, psg, ge, grd, gt_d),
                )):
                    ps = psv[ii % 3]
                    wm = [tok_xqt[q], tok_wmap[(si, ii)]]
                    if ii >= 3:
                        wm.append(tok_ev[key][q][ii - 3])
                    elif q >= 1:
                        wm.append(tok_ev[key][q - 1][ICH - 3 + ii])
                    last = None
                    for ci in range(8):
                        last = p.op("t", lambda e, ps=ps, wt_sb=wt_sb, ci=ci,
                                    ii=ii, q=q: e.matmul(
                                        ps[:],
                                        wt_sb[:, ci * I3 + ii * NT:
                                              ci * I3 + (ii + 1) * NT],
                                        xqt[q % 2][:, ci * TQ:(ci + 1) * TQ],
                                        start=(ci == 0), stop=(ci == 7)),
                                    waits=wm, quiet=(ci != 7))
                        wm = []
                    tok_mm[key][q][ii] = last
                    eb = ev[ii % 3]
                    we = [last]
                    if ii >= 3:
                        we.append(tok_st[key][q][ii - 3])
                        we.extend(tok_rd[key][q][ii - 3])
                    elif q >= 1:
                        we.append(tok_st[key][q - 1][ICH - 3 + ii])
                        we.extend(tok_rd[key][q - 1][ICH - 3 + ii])
                    if key == "u":
                        tev = p.op("a", lambda e, ps=ps, eb=eb: e.activation(
                            eb[:], ps[:], Act.Identity), waits=we)
                    else:
                        tev = p.op("v", lambda e, ps=ps, eb=eb: e.tensor_copy(
                            eb[:], ps[:]), waits=we)
                    tok_ev[key][q][ii] = tev
                    idx = q * ICH + ii
                    tn = p.op("v", lambda e, eb=eb, rd=rd, idx=idx:
                              e.tensor_reduce(
                                  rd[:, idx:idx + 1], eb[:], AX.X, Alu.max,
                                  apply_absolute_value=True), waits=[tev])
                    tok_rd[key][q][ii] = (tn,)
                    tok_st[key][q][ii] = p.dma(
                        f"s{key}{ii % 2}", lambda e, eb=eb, ii=ii, q=q,
                        dst=dst:
                        e.dma_start(out=dst.ap()[ii * NT:(ii + 1) * NT,
                                                 q * TQ:(q + 1) * TQ],
                                    in_=eb[:]), waits=[tev], eng="a")

        wlast = list(tok_rd["u"][ngrp - 1][ICH - 1]) + \
            list(tok_rd["g"][ngrp - 1][ICH - 1])
        tfr = p.op("v", lambda e: e.tensor_reduce(
            maxacc[:, 0:1], urd[:], AX.X, Alu.max, apply_absolute_value=True),
            waits=wlast)
        p.op("v", lambda e: e.tensor_reduce(
            maxacc[:, 1:2], grd[:], AX.X, Alu.max, apply_absolute_value=True),
            waits=[tfr])
        max_finalize(nc, es, p, maxacc, idf_sb, "m23loc",
                     [t_idf, ("a", p.count["a"]), ("t", p.count["t"])],
                     psf=psu[0])
        p.barrier_all()
        p.emit_block(nc, es)
    return nc



def build_L3(T):
    ch = min(1024, T)
    nc = bass.Bass("TRN2", target_bir_lowering=False)
    gt_d = nc.dram_tensor("gt", [I3, T], F32, kind="ExternalInput")
    P_d = nc.dram_tensor("P", [I3, T], F32, kind="ExternalOutput")
    idf_c = nc.inline_tensor(np.eye(NT, dtype=np.float32), name="idf")

    p = Plan()
    es = contextlib.ExitStack()
    with es:
        mxt, mxtok = maxes_prologue(nc, es, p, ["m1all", "m23all"])
        mx1, mx23 = mxt["m1all"], mxt["m23all"]
        idf_sb = es.enter_context(nc.sbuf_tensor("idf_sb", [NT, NT], F32))
        s1c = es.enter_context(nc.sbuf_tensor("s1c", [NT, 1], F32))
        s3c = es.enter_context(nc.sbuf_tensor("s3c", [NT, 1], F32))
        c3n = es.enter_context(nc.sbuf_tensor("c3n", [NT, 1], F32))
        tmpc = es.enter_context(nc.sbuf_tensor("tmpc", [NT, 1], F32))
        bC = es.enter_context(nc.sbuf_tensor("bC", [NT, 1], F32))
        bNC = es.enter_context(nc.sbuf_tensor("bNC", [NT, 1], F32))
        nchunk = (I3 // NT) * (T // ch)
        ncol = T // ch

        def fam(name, par, dt=F32):
            return [es.enter_context(nc.sbuf_tensor(f"{name}{k}", [NT, ch],
                                                    dt)) for k in range(par)]

        gb = fam("gb", 3)
        t3b = fam("t3b", 3)
        r3b = fam("r3b", 3)
        r3ib = fam("r3ib", 5, I16)
        sgb = fam("sgb", 2)
        t4b = fam("t4b", 3)
        r4b = fam("r4b", 3, U16)
        Pb = fam("Pb", 3)
        rdw = es.enter_context(nc.sbuf_tensor("rdw", [NT, nchunk], F32))
        maxacc = es.enter_context(nc.sbuf_tensor("maxacc", [NT, NMAX], F32))

        t_idf = p.dma("mis", lambda e: e.dma_start(
            out=idf_sb[:], in_=idf_c.ap()))
        t_mz = p.op("v", lambda e: e.memset(maxacc[:], 0.0))
        t_mz = p.op("v", lambda e: e.memset(bC[:], MAGIC), waits=[t_mz])
        t_mz = p.op("v", lambda e: e.memset(bNC[:], -MAGIC), waits=[t_mz])
        wv = [mxtok["m1all"], mxtok["m23all"]]
        tv = p.op("v", lambda e: e.tensor_scalar(
            s1c[:], mx1[:, 0:1], INVQ, EPS12, Alu.mult, Alu.max), waits=wv)
        tv = p.op("v", lambda e: e.tensor_tensor(
            tmpc[:], s1c[:], mx23[:, 1:2], Alu.mult), waits=[tv])
        t_s3 = p.op("v", lambda e: e.tensor_scalar(
            s3c[:], tmpc[:], INVQ, EPS12, Alu.mult, Alu.max), waits=[tv])
        tv = p.op("v", lambda e: e.reciprocal(c3n[:], mx23[:, 1:2]),
                  waits=[t_s3])
        t_c3 = p.op("v", lambda e: e.tensor_scalar(
            c3n[:], c3n[:], -Q15, None, Alu.mult), waits=[tv, t_mz])

        tok = {k: [None] * nchunk for k in
               ("ld", "t3", "r3f", "r3i", "sg", "t4", "rm4", "r4", "P",
                "red", "st")}

        def rc(c):
            ii, qq = divmod(c, ncol)
            return slice(ii * NT, (ii + 1) * NT), slice(qq * ch, (qq + 1) * ch)

        def tk(k, c):
            return tok[k][c] if c >= 0 else None

        def ph_ld(c):
            rows, cols = rc(c)
            tok["ld"][c] = p.dma(f"in{c % 3}", lambda e, c=c, rows=rows,
                                 cols=cols: e.dma_start(
                out=gb[c % 3][:], in_=gt_d.ap()[rows, cols]),
                waits=[tk("t3", c - 3)])

        def ph_A1(c):
            tok["t3"][c] = p.op("a", lambda e, c=c: e.activation(
                t3b[c % 3][:], gb[c % 3][:], Act.Identity, bias=bC[:],
                scale=c3n[:]),
                waits=[tok["ld"][c], t_c3, tk("r3f", c - 3)])

        def ph_V1(c):
            tok["r3f"][c] = p.op("v", lambda e, c=c: e.tensor_scalar(
                r3b[c % 3][:], t3b[c % 3][:], -MAGIC, None, Alu.add),
                waits=[tok["t3"][c], tk("sg", c - 3)])
            tok["r3i"][c] = p.op("v", lambda e, c=c: e.tensor_copy(
                r3ib[c % 5][:], r3b[c % 3][:]),
                waits=[tok["r3f"][c], tk("P", c - 5)])

        def ph_A2(c):
            tok["sg"][c] = p.op("a", lambda e, c=c: e.activation(
                sgb[c % 2][:], r3b[c % 3][:], Act.Sigmoid, bias=0.0,
                scale=s3c[:]),
                waits=[tok["r3f"][c], t_s3, tk("t4", c - 2)])

        def ph_V2(c):
            tok["t4"][c] = p.op("v", lambda e, c=c: e.tensor_scalar(
                t4b[c % 3][:], sgb[c % 2][:], 65536.0, MAGIC, Alu.mult,
                Alu.add), waits=[tok["sg"][c], tk("rm4", c - 3)])
            tok["rm4"][c] = p.op("v", lambda e, c=c: e.tensor_scalar(
                t4b[c % 3][:], t4b[c % 3][:], MAGIC + 65535.0, None, Alu.min),
                waits=[tok["t4"][c], tk("r4", c - 3)])

        def ph_A3(c):
            tok["r4"][c] = p.op("a", lambda e, c=c: e.activation(
                r4b[c % 3][:], t4b[c % 3][:], Act.Identity, bias=bNC[:],
                scale=1.0), waits=[tok["rm4"][c], tk("P", c - 3)])

        def ph_G1(c):
            tok["P"][c] = p.op("gp", lambda e, c=c: e.tensor_tensor(
                Pb[c % 3][:], r3ib[c % 5][:], r4b[c % 3][:], Alu.mult),
                waits=[tok["r4"][c], tok["r3i"][c], tk("st", c - 3),
                       tk("red", c - 3)])

        def ph_V3(c):
            rows, cols = rc(c)
            tok["red"][c] = p.op("v", lambda e, c=c: e.tensor_reduce(
                rdw[:, c:c + 1], Pb[c % 3][:], AX.X, Alu.max,
                apply_absolute_value=True), waits=[tok["P"][c]])
            tok["st"][c] = p.dma(f"sP{c % 3}", lambda e, c=c, rows=rows,
                                 cols=cols: e.dma_start(
                out=P_d.ap()[rows, cols], in_=Pb[c % 3][:]),
                waits=[tok["P"][c]], eng="a")

        phases = [(0, ph_ld), (2, ph_A1), (3, ph_V1), (4, ph_A2),
                  (5, ph_V2), (6, ph_A3), (7, ph_G1), (8, ph_V3)]
        for i in range(nchunk + 8):
            for off, fn in phases:
                c = i - off
                if 0 <= c < nchunk:
                    fn(c)

        p.op("v", lambda e: e.tensor_reduce(
            maxacc[:, 0:1], rdw[:], AX.X, Alu.max, apply_absolute_value=True),
            waits=[t_mz] + [tok["red"][c] for c in range(nchunk)])

        max_finalize(nc, es, p, maxacc, idf_sb, "mPloc", [t_idf])
        p.barrier_all()
        p.emit_block(nc, es)
    return nc



def build_L4(T):
    ch = min(1024, T)
    nc = bass.Bass("TRN2", target_bir_lowering=False)
    ut_d = nc.dram_tensor("ut", [I3, T], F32, kind="ExternalInput")
    P_d = nc.dram_tensor("P", [I3, T], F32, kind="ExternalInput")
    P2_d = nc.dram_tensor("P2", [I3, T], F32, kind="ExternalOutput")
    idf_c = nc.inline_tensor(np.eye(NT, dtype=np.float32), name="idf")

    p = Plan()
    es = contextlib.ExitStack()
    with es:
        mxt, mxtok = maxes_prologue(nc, es, p, ["m23all", "mPall"])
        mx23, mxP = mxt["m23all"], mxt["mPall"]
        idf_sb = es.enter_context(nc.sbuf_tensor("idf_sb", [NT, NT], F32))
        c2c = es.enter_context(nc.sbuf_tensor("c2c", [NT, 1], F32))
        c5c = es.enter_context(nc.sbuf_tensor("c5c", [NT, 1], F32))
        bC = es.enter_context(nc.sbuf_tensor("bC", [NT, 1], F32))
        bNC = es.enter_context(nc.sbuf_tensor("bNC", [NT, 1], F32))
        nchunk = (I3 // NT) * (T // ch)
        ncol = T // ch

        def fam(name, par, dt=F32):
            return [es.enter_context(nc.sbuf_tensor(f"{name}{k}", [NT, ch],
                                                    dt)) for k in range(par)]

        ub = fam("ub", 3)
        Pb = fam("Pb", 3)
        t2b = fam("t2b", 3)
        r2ib = fam("r2ib", 4, I16)
        t5b = fam("t5b", 3)
        r5b = fam("r5b", 3)
        r5ib = fam("r5ib", 3, I16)
        P2b = fam("P2b", 3)
        rdw = es.enter_context(nc.sbuf_tensor("rdw", [NT, nchunk], F32))
        maxacc = es.enter_context(nc.sbuf_tensor("maxacc", [NT, NMAX], F32))

        t_idf = p.dma("mis", lambda e: e.dma_start(
            out=idf_sb[:], in_=idf_c.ap()))
        t_mz = p.op("v", lambda e: e.memset(maxacc[:], 0.0))
        t_mz = p.op("v", lambda e: e.memset(bC[:], MAGIC), waits=[t_mz])
        t_mz = p.op("v", lambda e: e.memset(bNC[:], -MAGIC), waits=[t_mz])
        wv = [mxtok["m23all"], mxtok["mPall"]]
        tv = p.op("v", lambda e: e.reciprocal(c2c[:], mx23[:, 0:1]), waits=wv)
        t_c2 = p.op("v", lambda e: e.tensor_scalar(
            c2c[:], c2c[:], Q15, None, Alu.mult), waits=[tv, t_mz])
        tv = p.op("v", lambda e: e.reciprocal(c5c[:], mxP[:, 0:1]),
                  waits=[t_c2])
        t_c5 = p.op("v", lambda e: e.tensor_scalar(
            c5c[:], c5c[:], Q15, None, Alu.mult), waits=[tv])

        tok = {k: [None] * nchunk for k in
               ("ldu", "ldp", "t2", "r2i", "t5", "r5f", "r5i", "P2",
                "red", "st")}

        def rc(c):
            ii, qq = divmod(c, ncol)
            return slice(ii * NT, (ii + 1) * NT), slice(qq * ch, (qq + 1) * ch)

        def tk(k, c):
            return tok[k][c] if c >= 0 else None

        def ph_ldu(c):
            rows, cols = rc(c)
            tok["ldu"][c] = p.dma(f"inu{c % 3}", lambda e, c=c, rows=rows,
                                  cols=cols: e.dma_start(
                out=ub[c % 3][:], in_=ut_d.ap()[rows, cols]),
                waits=[tk("t2", c - 3)])

        def ph_ldp(c):
            rows, cols = rc(c)
            tok["ldp"][c] = p.dma(f"inp{c % 3}", lambda e, c=c, rows=rows,
                                  cols=cols: e.dma_start(
                out=Pb[c % 3][:], in_=P_d.ap()[rows, cols]),
                waits=[tk("t5", c - 3)])

        def ph_A1(c):
            tok["t2"][c] = p.op("a", lambda e, c=c: e.activation(
                t2b[c % 3][:], ub[c % 3][:], Act.Identity, bias=bC[:],
                scale=c2c[:]),
                waits=[tok["ldu"][c], t_c5, tk("r2i", c - 3)])

        def ph_A2(c):
            tok["r2i"][c] = p.op("a", lambda e, c=c: e.activation(
                r2ib[c % 4][:], t2b[c % 3][:], Act.Identity, bias=bNC[:],
                scale=1.0), waits=[tok["t2"][c], tk("P2", c - 4)])

        def ph_A3(c):
            tok["t5"][c] = p.op("a", lambda e, c=c: e.activation(
                t5b[c % 3][:], Pb[c % 3][:], Act.Identity, bias=bC[:],
                scale=c5c[:]),
                waits=[tok["ldp"][c], t_c5, tk("r5f", c - 3)])

        def ph_V1(c):
            tok["r5f"][c] = p.op("v", lambda e, c=c: e.tensor_scalar(
                r5b[c % 3][:], t5b[c % 3][:], -MAGIC, None, Alu.add),
                waits=[tok["t5"][c], tk("r5i", c - 3)])
            tok["r5i"][c] = p.op("v", lambda e, c=c: e.tensor_copy(
                r5ib[c % 3][:], r5b[c % 3][:]),
                waits=[tok["r5f"][c], tk("P2", c - 3)])

        def ph_G1(c):
            tok["P2"][c] = p.op("gp", lambda e, c=c: e.tensor_tensor(
                P2b[c % 3][:], r5ib[c % 3][:], r2ib[c % 4][:], Alu.mult),
                waits=[tok["r5i"][c], tok["r2i"][c], tk("st", c - 3),
                       tk("red", c - 3)])

        def ph_V3(c):
            rows, cols = rc(c)
            tok["red"][c] = p.op("v", lambda e, c=c: e.tensor_reduce(
                rdw[:, c:c + 1], P2b[c % 3][:], AX.X, Alu.max,
                apply_absolute_value=True), waits=[tok["P2"][c]])
            tok["st"][c] = p.dma(f"sP{c % 3}", lambda e, c=c, rows=rows,
                                 cols=cols: e.dma_start(
                out=P2_d.ap()[rows, cols], in_=P2b[c % 3][:]),
                waits=[tok["P2"][c]], eng="a")

        phases = [(0, ph_ldu), (1, ph_ldp), (2, ph_A1), (3, ph_A2),
                  (3, ph_A3), (4, ph_V1), (5, ph_G1), (6, ph_V3)]
        for i in range(nchunk + 6):
            for off, fn in phases:
                c = i - off
                if 0 <= c < nchunk:
                    fn(c)

        p.op("v", lambda e: e.tensor_reduce(
            maxacc[:, 0:1], rdw[:], AX.X, Alu.max, apply_absolute_value=True),
            waits=[t_mz] + [tok["red"][c] for c in range(nchunk)])

        max_finalize(nc, es, p, maxacc, idf_sb, "mP2loc", [t_idf])
        p.barrier_all()
        p.emit_block(nc, es)
    return nc



def build_L5(T):
    ngrp = T // TQ
    ICH = I3 // NT
    nc = bass.Bass("TRN2", target_bir_lowering=False)
    P2_d = nc.dram_tensor("P2", [I3, T], F32, kind="ExternalInput")
    wd_d = nc.dram_tensor("wd", [H, I3], F32, kind="ExternalInput")
    part_d = nc.dram_tensor("partial", [T, H], F32, kind="ExternalInput")
    out_d = nc.dram_tensor("out", [T, H], F32, kind="ExternalOutput")
    idb_c = nc.inline_tensor(np.eye(NT, dtype=BF), name="idb")

    p = Plan()
    es = contextlib.ExitStack()
    with es:
        mxt, mxtok = maxes_prologue(
            nc, es, p, ["m1all", "m23all", "mPall", "mP2all"])
        mx1, mx23 = mxt["m1all"], mxt["m23all"]
        mxP, mxP2 = mxt["mPall"], mxt["mP2all"]
        idb_sb = es.enter_context(nc.sbuf_tensor("idb_sb", [NT, NT], BF16))
        s1c = es.enter_context(nc.sbuf_tensor("s1c", [NT, 1], F32))
        s2c = es.enter_context(nc.sbuf_tensor("s2c", [NT, 1], F32))
        s3c = es.enter_context(nc.sbuf_tensor("s3c", [NT, 1], F32))
        s4c = es.enter_context(nc.sbuf_tensor("s4c", [NT, 1], F32))
        s6c = es.enter_context(nc.sbuf_tensor("s6c", [NT, 1], F32))
        c6c = es.enter_context(nc.sbuf_tensor("c6c", [NT, 1], F32))
        tmpc = es.enter_context(nc.sbuf_tensor("tmpc", [NT, 1], F32))
        bC = es.enter_context(nc.sbuf_tensor("bC", [NT, 1], F32))
        bNC = es.enter_context(nc.sbuf_tensor("bNC", [NT, 1], F32))
        wdt = es.enter_context(nc.sbuf_tensor("wdt", [NT, ICH * H], BF16))
        BW = 2 * TQ
        P2b = [es.enter_context(nc.sbuf_tensor(f"P2b{k}", [NT, BW], F32))
               for k in range(2)]
        t6b = [es.enter_context(nc.sbuf_tensor(f"t6b{k}", [NT, BW], F32))
               for k in range(2)]
        r6i = [es.enter_context(nc.sbuf_tensor(f"r6i{k}", [NT, BW], I16))
               for k in range(2)]
        r6g = [es.enter_context(nc.sbuf_tensor(f"r6g{k}", [NT, ICH * TQ],
                                                BF16)) for k in range(2)]
        prt = [es.enter_context(nc.sbuf_tensor(f"prt{k}", [NT, H], F32))
               for k in range(2)]
        osb = [es.enter_context(nc.sbuf_tensor(f"osb{k}", [NT, H], F32))
               for k in range(2)]
        fsb = [es.enter_context(nc.sbuf_tensor(f"fsb{k}", [NT, H], F32))
               for k in range(2)]
        psd = [es.enter_context(nc.psum_tensor(f"pd{k}", [NT, TQ], F32))
               for k in range(3)]

        t_id = p.dma("mis", lambda e: e.dma_start(
            out=idb_sb[:], in_=idb_c.ap()))
        wv = [mxtok[n] for n in ("m1all", "m23all", "mPall", "mP2all")]
        tv = p.op("v", lambda e: e.tensor_scalar(
            s1c[:], mx1[:, 0:1], INVQ, EPS12, Alu.mult, Alu.max), waits=wv)
        tv = p.op("v", lambda e: e.tensor_tensor(
            tmpc[:], s1c[:], mx23[:, 0:1], Alu.mult), waits=[tv])
        tv = p.op("v", lambda e: e.tensor_scalar(
            s2c[:], tmpc[:], INVQ, EPS12, Alu.mult, Alu.max), waits=[tv])
        tv = p.op("v", lambda e: e.tensor_tensor(
            tmpc[:], s1c[:], mx23[:, 1:2], Alu.mult), waits=[tv])
        tv = p.op("v", lambda e: e.tensor_scalar(
            s3c[:], tmpc[:], INVQ, EPS12, Alu.mult, Alu.max), waits=[tv])
        tv = p.op("v", lambda e: e.tensor_tensor(
            tmpc[:], s3c[:], mxP[:, 0:1], Alu.mult), waits=[tv])
        tv = p.op("v", lambda e: e.tensor_scalar(
            tmpc[:], tmpc[:], INV65536, None, Alu.mult), waits=[tv])
        tv = p.op("v", lambda e: e.tensor_scalar(
            s4c[:], tmpc[:], INVQ, EPS12, Alu.mult, Alu.max), waits=[tv])
        tv = p.op("v", lambda e: e.tensor_tensor(
            tmpc[:], s4c[:], s2c[:], Alu.mult), waits=[tv])
        tv = p.op("v", lambda e: e.tensor_tensor(
            tmpc[:], tmpc[:], mxP2[:, 0:1], Alu.mult), waits=[tv])
        t_s6 = p.op("v", lambda e: e.tensor_scalar(
            s6c[:], tmpc[:], INVQ, EPS12, Alu.mult, Alu.max), waits=[tv])
        tv = p.op("v", lambda e: e.reciprocal(c6c[:], mxP2[:, 0:1]),
                  waits=[t_s6])
        t_c6 = p.op("v", lambda e: e.tensor_scalar(
            c6c[:], c6c[:], Q15, None, Alu.mult), waits=[tv])
        t_c6 = p.op("v", lambda e: e.memset(bC[:], MAGIC), waits=[t_c6])
        t_c6 = p.op("v", lambda e: e.memset(bNC[:], -MAGIC), waits=[t_c6])

        tok_wd, _ = dequant_weights(nc, es, p, [(wd_d, H, I3, wdt)], idb_sb,
                                    "d", tok_id=t_id)

        NGR = 8
        tok_mm = [[None] * NGR for _ in range(ngrp)]
        tok_ev = [[None] * NGR for _ in range(ngrp)]
        tok_out = [[None] * 4 for _ in range(ngrp)]
        tok_stf = [[None] * 4 for _ in range(ngrp)]
        tok_p2read = [None, None]
        tok_r6w = [None, None]
        tok_su6 = [None, None]
        tok_r6pair = [None, None]
        tok_r6last = [None] * ngrp

        def emit_chain(q):
            for blk in range(ICH // 2):
                b = blk % 2
                ii0 = blk * 2
                w = [t_c6, tok_p2read[b]]
                tl = p.dma(f"inp{b}", lambda e, q=q, ii0=ii0, b=b: e.dma_start(
                    out=P2b[b][:].rearrange("p (k t) -> p k t", t=TQ),
                    in_=P2_d.ap()[ii0 * NT:(ii0 + 2) * NT,
                                  q * TQ:(q + 1) * TQ].rearrange(
                        "(k p) t -> p k t", p=NT)), waits=w)
                ta = p.op("a", lambda e, b=b: e.activation(
                    t6b[b][:], P2b[b][:], Act.Identity, bias=bC[:],
                    scale=c6c[:]),
                    waits=[tl, t_c6, tok_r6w[b]])
                tok_p2read[b] = ta
                tsu = p.op("v", lambda e, b=b: e.tensor_scalar(
                    t6b[b][:], t6b[b][:], -MAGIC, None, Alu.add), waits=[ta])
                ti = p.op("v", lambda e, b=b: e.tensor_copy(
                    r6i[b][:], t6b[b][:]), waits=[tsu, tok_su6[b]])
                tok_r6w[b] = ti
                wq = [ti]
                if q >= 2 and blk <= 1:
                    wq.append(tok_mm[q - 2][NGR - 1])
                lastv = p.op("v", lambda e, q=q, ii0=ii0, b=b: e.tensor_copy(
                    r6g[q % 2][:, ii0 * TQ:(ii0 + 2) * TQ], r6i[b][:]),
                    waits=wq)
                tok_su6[b] = lastv
                tok_r6pair[b] = lastv
            tok_r6last[q] = (tok_r6pair[0], tok_r6pair[1])

        def emit_mm(q):
            for tt in range(4):
                wl = []
                if tt >= 2:
                    wl.append(tok_out[q][tt - 2])
                elif q >= 1:
                    wl.append(tok_out[q - 1][2 + tt])
                tok_ld = p.dma(f"inq{tt % 2}", lambda e, q=q, tt=tt:
                               e.dma_start(
                    out=prt[tt % 2][:],
                    in_=part_d.ap()[q * TQ + tt * NT:
                                    q * TQ + (tt + 1) * NT, :]), waits=wl)
                for half in range(2):
                    g = tt * 2 + half
                    ps = psd[g % 3]
                    wm = [tok_r6last[q][0], tok_r6last[q][1], tok_wd]
                    if g >= 3:
                        wm.append(tok_ev[q][g - 3])
                    elif q >= 1:
                        wm.append(tok_ev[q - 1][6 + g if g < 2 else 5])
                    last = None
                    for ii in range(ICH):
                        last = p.op("t", lambda e, ps=ps, ii=ii, tt=tt,
                                    half=half, q=q: e.matmul(
                                        ps[:],
                                        r6g[q % 2][:, ii * TQ + tt * NT:
                                                   ii * TQ + (tt + 1) * NT],
                                        wdt[:, ii * H + half * TQ:
                                            ii * H + (half + 1) * TQ],
                                        start=(ii == 0), stop=(ii == ICH - 1)),
                                    waits=wm, quiet=(ii != ICH - 1))
                        wm = []
                    tok_mm[q][g] = last
                    we = [last, t_s6]
                    if tt >= 2 and half == 0:
                        we.append(tok_out[q][tt - 2])
                    elif q >= 1 and half == 0:
                        we.append(tok_out[q - 1][(tt + 2) % 4])
                    tok_ev[q][g] = p.op(
                        "a", lambda e, ps=ps, tt=tt, half=half: e.activation(
                            osb[tt % 2][:, half * TQ:(half + 1) * TQ], ps[:],
                            Act.Identity, bias=0.0, scale=s6c[:]), waits=we)
                wf = [tok_ev[q][tt * 2], tok_ev[q][tt * 2 + 1], tok_ld]
                if tt >= 2:
                    wf.append(tok_stf[q][tt - 2])
                elif q >= 1:
                    wf.append(tok_stf[q - 1][2 + tt])
                tf = p.op("v", lambda e, tt=tt: e.tensor_tensor(
                    fsb[tt % 2][:], osb[tt % 2][:], prt[tt % 2][:], Alu.add),
                    waits=wf)
                tok_out[q][tt] = tf
                tok_stf[q][tt] = p.dma(f"sou{tt % 2}", lambda e, q=q, tt=tt:
                                       e.dma_start(
                    out=out_d.ap()[q * TQ + tt * NT:q * TQ + (tt + 1) * NT, :],
                    in_=fsb[tt % 2][:]), waits=[tf], eng="gp")

        for q in range(ngrp + 1):
            if q < ngrp:
                emit_chain(q)
            if q >= 1:
                emit_mm(q - 1)

        p.barrier_all()
        p.emit_block(nc, es)
    return nc



def _gather_max(results, key):
    m = np.stack([np.asarray(r[key], np.float32).reshape(NMAX)
                  for r in results])
    return np.ascontiguousarray(m.reshape(1, 8 * NMAX))


def kernel(L0_attn_x, L0_attn_filt, L0_ffn_w_gate, L0_ffn_w_up, L0_ffn_w_down):
    x = np.ascontiguousarray(np.asarray(L0_attn_x, np.float32))
    filt = np.ascontiguousarray(np.asarray(L0_attn_filt, np.float32))
    wg = np.ascontiguousarray(np.asarray(L0_ffn_w_gate, np.float32))
    wu = np.ascontiguousarray(np.asarray(L0_ffn_w_up, np.float32))
    wd = np.ascontiguousarray(np.asarray(L0_ffn_w_down, np.float32))
    B, S, _ = x.shape
    T = S
    cores = list(range(NCORES))
    Fz = build_toeplitz(filt)

    r1 = run_bass_kernel_spmd(
        build_L1(T), [{"xs": x[c], "Fz": Fz} for c in range(NCORES)], cores)
    m1all = _gather_max(r1.results, "m1loc")

    r2 = run_bass_kernel_spmd(
        build_L2(T),
        [{"xh": np.asarray(r["xh"]), "xl": np.asarray(r["xl"]),
          "wu": wu, "wg": wg, "m1all": m1all} for r in r1.results], cores)
    m23all = _gather_max(r2.results, "m23loc")

    r3 = run_bass_kernel_spmd(
        build_L3(T),
        [{"gt": np.asarray(r["gt"]), "m1all": m1all, "m23all": m23all}
         for r in r2.results], cores)
    mPall = _gather_max(r3.results, "mPloc")

    r4 = run_bass_kernel_spmd(
        build_L4(T),
        [{"ut": np.asarray(r2.results[c]["ut"]),
          "P": np.asarray(r3.results[c]["P"]),
          "m23all": m23all, "mPall": mPall} for c in range(NCORES)], cores)
    mP2all = _gather_max(r4.results, "mP2loc")

    r5 = run_bass_kernel_spmd(
        build_L5(T),
        [{"P2": np.asarray(r4.results[c]["P2"]), "wd": wd,
          "partial": np.asarray(r1.results[c]["partial"]),
          "m1all": m1all, "m23all": m23all, "mPall": mPall,
          "mP2all": mP2all} for c in range(NCORES)], cores)

    out = np.stack([np.asarray(r["out"], np.float32) for r in r5.results])
    return np.ascontiguousarray(out)


# version 10
# speedup vs baseline: 2.2317x; 1.0744x over previous
import contextlib

import numpy as np
import ml_dtypes

import concourse.bass as bass
import concourse.mybir as mybir
from concourse.bass_utils import run_bass_kernel_spmd

F32 = mybir.dt.float32
BF16 = mybir.dt.bfloat16
I16 = mybir.dt.int16
U16 = mybir.dt.uint16
Alu = mybir.AluOpType
Act = mybir.ActivationFunctionType
AX = mybir.AxisListType

NCORES = 8
H = 1024
I3 = 3072
NT = 128
LF = 2048
ND = 17
RING = 18
NMAX = 16
TQ = 512
CH = 2048
INV7 = float(np.float32(1.0 / 7.0))
INVQ = float(np.float32(1.0 / 32767.0))
Q15 = 32767.0
Q16 = 65536.0
INV65536 = float(np.float32(1.0 / 65536.0))
INVH = float(np.float32(1.0 / H))
EPS12 = 1e-12
MAGIC = 12582912.0

BF = ml_dtypes.bfloat16


def build_toeplitz(filt):
    out = np.zeros((ND, NT, NT), np.float32)
    j, i = np.meshgrid(np.arange(NT), np.arange(NT), indexing="ij")
    for d in range(ND):
        idx = NT * d + j - i
        m = (idx >= 0) & (idx < LF)
        out[d][m] = filt[idx[m]]
    return np.ascontiguousarray(out.reshape(ND * NT, NT))


class Plan:

    ENGS = ("sy", "gp", "v", "a", "t")

    def __init__(self):
        self.ops = {k: [] for k in self.ENGS}
        self.count = {}
        for k in ("v", "a", "t"):
            self.count[k] = 0
        self.lastw = {k: {} for k in self.ENGS}
        self.chan_eng = {}
        self.chan_last = {}

    def _record(self, eng, sem, emit, waits, inc):
        seen = {}
        for tok in waits:
            if tok is None:
                continue
            s, v = tok
            if v <= 0 or self.lastw[eng].get(s, 0) >= v:
                continue
            seen[s] = max(seen.get(s, 0), v)
        w = []
        for s, v in seen.items():
            w.append((s, v))
            self.lastw[eng][s] = v
        self.count[sem] = self.count.get(sem, 0) + inc
        self.ops[eng].append((w, emit, sem, inc))
        return (sem, self.count[sem])

    def op(self, eng, emit, waits=(), quiet=False):
        if quiet:
            return self._record(eng, eng, emit, waits, 0)
        return self._record(eng, eng, emit, waits, 1)

    def dma(self, chan, emit, waits=(), eng="sy"):
        if chan not in self.chan_eng:
            self.chan_eng[chan] = eng
        eng = self.chan_eng[chan]
        w = list(waits) + [self.chan_last.get(chan)]
        tok = self._record(eng, chan, emit, w, 16)
        self.chan_last[chan] = tok
        return tok

    def barrier_all(self, eng="gp"):
        waits = [(s, c) for s, c in self.count.items() if c > 0]
        self._record(eng, eng if eng in self.count else "v",
                     lambda e: None, waits, 0)

    def emit_block(self, nc, es):
        sems = {s: es.enter_context(nc.semaphore("sem_" + s))
                for s in self.count}
        with nc.Block() as block:
            def runner(engname):
                def run(engobj):
                    for (waits, emit, sem, inc) in self.ops[engname]:
                        for (s, v) in waits:
                            engobj.wait_ge(sems[s], v)
                        ins = emit(engobj)
                        if ins is not None and inc > 0:
                            ins.then_inc(sems[sem], inc)
                return run
            if self.ops["sy"]:
                block.sync(runner("sy"))
            if self.ops["gp"]:
                block.gpsimd(runner("gp"))
            if self.ops["v"]:
                block.vector(runner("v"))
            if self.ops["a"]:
                block.scalar(runner("a"))
            if self.ops["t"]:
                block.tensor(runner("t"))


def maxes_prologue(nc, es, p, names):
    tiles, toks = {}, {}
    for n in names:
        inp = nc.dram_tensor(n, [1, 8 * NMAX], F32, kind="ExternalInput")
        scr = nc.dram_tensor("scr_" + n, [1, NMAX], F32, kind="Internal")
        mh = es.enter_context(nc.sbuf_tensor("mh_" + n, [1, 8 * NMAX], F32))
        m4 = es.enter_context(nc.sbuf_tensor("m4_" + n, [1, 4 * NMAX], F32))
        m2 = es.enter_context(nc.sbuf_tensor("m2_" + n, [1, 2 * NMAX], F32))
        m1 = es.enter_context(nc.sbuf_tensor("m1_" + n, [1, NMAX], F32))
        mx = es.enter_context(nc.sbuf_tensor("mx_" + n, [NT, NMAX], F32))
        t = p.dma("mis", lambda e, inp=inp, mh=mh: e.dma_start(
            out=mh[:], in_=inp.ap()))
        t = p.op("v", lambda e, mh=mh, m4=m4: e.tensor_tensor(
            m4[:], mh[:, :4 * NMAX], mh[:, 4 * NMAX:], Alu.max), waits=[t])
        t = p.op("v", lambda e, m4=m4, m2=m2: e.tensor_tensor(
            m2[:], m4[:, :2 * NMAX], m4[:, 2 * NMAX:], Alu.max), waits=[t])
        t = p.op("v", lambda e, m2=m2, m1=m1: e.tensor_tensor(
            m1[:], m2[:, :NMAX], m2[:, NMAX:], Alu.max), waits=[t])
        t = p.dma("mis", lambda e, scr=scr, m1=m1: e.dma_start(
            out=scr.ap(), in_=m1[:]), waits=[t])
        t = p.dma("mis", lambda e, mx=mx, scr=scr: e.dma_start(
            out=mx[:], in_=scr.ap().broadcast_to([NT, NMAX])), waits=[t])
        tiles[n] = mx
        toks[n] = t
    return tiles, toks


def max_finalize(nc, es, p, maxacc, idf_sb, out_name, extra_waits=(),
                 psf=None):
    mout = nc.dram_tensor(out_name, [NMAX, 1], F32, kind="ExternalOutput")
    if psf is None:
        psf = es.enter_context(nc.psum_tensor("ps_mfin", [NT, NT], F32))
    mfs = es.enter_context(nc.sbuf_tensor("mfin_s", [NMAX, NT], F32))
    mfr = es.enter_context(nc.sbuf_tensor("mfin_r", [NMAX, 1], F32))
    w = list(extra_waits) + [("v", p.count["v"])]
    t = p.op("t", lambda e: e.transpose(psf[:NMAX, :NT], maxacc[:],
                                        idf_sb[:]), waits=w)
    t = p.op("v", lambda e: e.tensor_copy(mfs[:], psf[:NMAX, :NT]), waits=[t])
    t = p.op("v", lambda e: e.tensor_reduce(mfr[:], mfs[:], AX.X, Alu.max),
             waits=[t])
    return p.dma("mis", lambda e: e.dma_start(out=mout.ap(), in_=mfr[:]),
                 waits=[t])



def plan_round_signed(p, t_tok, tbuf, ibuf, dbuf, out_emit):
    tk = p.op("v", lambda e: e.tensor_copy(ibuf, tbuf), waits=[t_tok])
    tk = p.op("v", lambda e: e.tensor_tensor(dbuf, tbuf, ibuf, Alu.subtract),
              waits=[tk])
    tk = p.op("v", lambda e: e.scalar_tensor_tensor(
        tbuf, dbuf, 0.5, ibuf, Alu.is_ge, Alu.add), waits=[tk])
    tk = p.op("v", lambda e: out_emit(e, dbuf, tbuf), waits=[tk])
    return tk


def plan_round_u16(p, t_tok, tbuf, ubuf, dbuf, out, clip_hi):
    tk = p.op("v", lambda e: e.tensor_copy(ubuf, tbuf), waits=[t_tok])
    tk = p.op("v", lambda e: e.tensor_tensor(dbuf, tbuf, ubuf, Alu.subtract),
              waits=[tk])
    tk = p.op("v", lambda e: e.scalar_tensor_tensor(
        tbuf, dbuf, 0.5, ubuf, Alu.is_ge, Alu.add), waits=[tk])
    tk = p.op("v", lambda e: e.tensor_scalar(
        out, tbuf, float(clip_hi), None, Alu.min), waits=[tk])
    return tk


def dequant_weights(nc, es, p, specs, idb_sb, tag, psw=None, tok_id=None):
    CT = 1024
    wbuf = [es.enter_context(nc.sbuf_tensor(f"wb{tag}{k}", [NT, CT], F32))
            for k in range(2)]
    wqb = [es.enter_context(nc.sbuf_tensor(f"wq{tag}{k}", [NT, CT], BF16))
           for k in range(2)]
    absr = es.enter_context(nc.sbuf_tensor(f"ab{tag}", [NT, CT // 32], F32))
    sblk = es.enter_context(nc.sbuf_tensor(f"sb{tag}", [NT, CT // 32], F32))
    sinv = es.enter_context(nc.sbuf_tensor(f"si{tag}", [NT, CT // 32], F32))
    tqb = es.enter_context(nc.sbuf_tensor(f"tq{tag}", [NT, CT], F32))
    dqb = es.enter_context(nc.sbuf_tensor(f"dq{tag}", [NT, CT], F32))
    kqb = es.enter_context(nc.sbuf_tensor(f"kq{tag}", [NT, CT], I16))
    bCw = es.enter_context(nc.sbuf_tensor(f"bCw{tag}", [NT, 1], F32))
    bNCw = es.enter_context(nc.sbuf_tensor(f"bNCw{tag}", [NT, 1], F32))
    t_bw = p.op("v", lambda e: e.memset(bCw[:], MAGIC))
    t_bw = p.op("v", lambda e: e.memset(bNCw[:], -MAGIC), waits=[t_bw])
    if psw is None:
        psw = [es.enter_context(nc.psum_tensor(f"pw{tag}{k}", [NT, 8 * NT],
                                               BF16)) for k in range(2)]
    last_evac = None
    tok_consumed = [None, None]
    tok_wq_read = [None, None]
    tokmap = {}
    nblk = CT // 32
    n = 0
    order = []
    for ii in range(max(nr // NT for (_, nr, _, _) in specs)):
        for si, (w_dram, nrows, ncols, wt_sb) in enumerate(specs):
            if ii < nrows // NT:
                order.append((si, ii, w_dram, nrows, ncols, wt_sb))
    for (si, ii, w_dram, nrows, ncols, wt_sb) in order:
        if True:
            for cc in range(ncols // CT):
                par = n % 2
                n += 1
                wb, wq = wbuf[par], wqb[par]
                tl = p.dma("wld", lambda e, wb=wb, w_dram=w_dram, ii=ii, cc=cc:
                           e.dma_start(
                               out=wb[:],
                               in_=w_dram.ap()[ii * NT:(ii + 1) * NT,
                                               cc * CT:(cc + 1) * CT]),
                           waits=[tok_consumed[par]])
                tv = p.op("v", lambda e, wb=wb: e.tensor_reduce(
                    absr[:].rearrange("p (b o) -> p b o", o=1),
                    wb[:].rearrange("p (b e) -> p b e", e=32),
                    AX.X, Alu.max, apply_absolute_value=True), waits=[tl])
                tv = p.op("v", lambda e: e.tensor_scalar(
                    sblk[:], absr[:], INV7, EPS12, Alu.mult, Alu.max),
                    waits=[tv])
                tv = p.op("v", lambda e: e.reciprocal(sinv[:], sblk[:]),
                          waits=[tv])
                tv = p.op("v", lambda e: e.tensor_scalar(
                    sinv[:], sinv[:], -1.0, None, Alu.mult), waits=[tv])
                tv = p.op("v", lambda e, wb=wb: e.tensor_tensor(
                    tqb[:].rearrange("p (b e) -> p b e", e=32),
                    wb[:].rearrange("p (b e) -> p b e", e=32),
                    sinv[:].rearrange("p (b o) -> p b o", o=1)
                    .broadcast_to([NT, nblk, 32]), Alu.mult), waits=[tv])
                tok_consumed[par] = tv
                tv = p.op("a", lambda e: e.activation(
                    dqb[:], tqb[:], Act.Identity, bias=bCw[:], scale=1.0),
                    waits=[tv, t_bw])
                tv = p.op("a", lambda e: e.activation(
                    kqb[:], dqb[:], Act.Identity, bias=bNCw[:], scale=1.0),
                    waits=[tv])
                tv = p.op("v", lambda e, wq=wq: e.tensor_tensor(
                    wq[:].rearrange("p (b e) -> p b e", e=32),
                    kqb[:].rearrange("p (b e) -> p b e", e=32),
                    sblk[:].rearrange("p (b o) -> p b o", o=1)
                    .broadcast_to([NT, nblk, 32]), Alu.mult),
                    waits=[tv, tok_wq_read[par]])
                ps = psw[n % len(psw)]
                ttr = None
                for ci in range(CT // NT):
                    ttr = p.op("t", lambda e, ps=ps, wq=wq, ci=ci:
                               e.transpose(
                        ps[:, ci * NT:(ci + 1) * NT],
                        wq[:, ci * NT:(ci + 1) * NT], idb_sb[:]),
                        waits=[tv, last_evac, tok_id], quiet=(ci != 7))
                last_evac = p.op(
                    "a", lambda e, ps=ps, wt_sb=wt_sb, cc=cc, ii=ii,
                    nrows=nrows: e.activation(
                        wt_sb[:].rearrange("p (cig m) -> p cig m", m=nrows)
                        [:, cc * 8:(cc + 1) * 8, ii * NT:(ii + 1) * NT],
                        ps[:], Act.Identity), waits=[ttr])
                tok_wq_read[par] = last_evac
                tokmap[(si, ii)] = last_evac
    return last_evac, tokmap



def build_L1(T):
    ntiles = T // NT
    nc = bass.Bass("TRN2", target_bir_lowering=False)
    xs = nc.dram_tensor("xs", [T, H], F32, kind="ExternalInput")
    Fz = nc.dram_tensor("Fz", [ND * NT, NT], F32, kind="ExternalInput")
    xh_d = nc.dram_tensor("xh", [T, H], BF16, kind="ExternalOutput")
    xl_d = nc.dram_tensor("xl", [T, H], BF16, kind="ExternalOutput")
    part_d = nc.dram_tensor("partial", [T, H], F32, kind="ExternalOutput")
    idf_c = nc.inline_tensor(np.eye(NT, dtype=np.float32), name="idf")

    p = Plan()
    es = contextlib.ExitStack()
    with es:
        xr = es.enter_context(nc.sbuf_tensor("xr", [NT, RING * H], F32))
        ff = es.enter_context(nc.sbuf_tensor("ff", [NT, ND * NT], F32))
        fh = es.enter_context(nc.sbuf_tensor("fh", [NT, ND * NT], BF16))
        fl = es.enter_context(nc.sbuf_tensor("fl", [NT, ND * NT], BF16))
        xhr = es.enter_context(nc.sbuf_tensor("xhr", [NT, RING * H], BF16))
        xlr = es.enter_context(nc.sbuf_tensor("xlr", [NT, RING * H], BF16))
        xn = es.enter_context(nc.sbuf_tensor("xn", [NT, H], F32))
        dump = es.enter_context(nc.sbuf_tensor("dump", [NT, H], F32))
        smv = es.enter_context(nc.sbuf_tensor("smv", [NT, 1], F32))
        sqv = es.enter_context(nc.sbuf_tensor("sqv", [NT, 1], F32))
        muv = es.enter_context(nc.sbuf_tensor("muv", [NT, 1], F32))
        ex2 = es.enter_context(nc.sbuf_tensor("ex2", [NT, 1], F32))
        mu2 = es.enter_context(nc.sbuf_tensor("mu2", [NT, 1], F32))
        varv = es.enter_context(nc.sbuf_tensor("varv", [NT, 1], F32))
        rrv = es.enter_context(nc.sbuf_tensor("rrv", [NT, 1], F32))
        rsv = es.enter_context(nc.sbuf_tensor("rsv", [NT, 1], F32))
        mtv = es.enter_context(nc.sbuf_tensor("mtv", [NT, 1], F32))
        maxacc = es.enter_context(nc.sbuf_tensor("maxacc", [NT, NMAX], F32))
        idf_sb = es.enter_context(nc.sbuf_tensor("idf_sb", [NT, NT], F32))
        parts = [es.enter_context(nc.sbuf_tensor(f"part{k}", [NT, H], F32))
                 for k in range(2)]
        psc = [es.enter_context(nc.psum_tensor(f"psc{k}", [NT, H], F32))
               for k in range(3)]

        t_id = p.dma("mis", lambda e: e.dma_start(
            out=idf_sb[:], in_=idf_c.ap()))
        t_f = p.dma("mis", lambda e: e.dma_start(
            out=ff[:].rearrange("p (d i) -> p d i", i=NT),
            in_=Fz.ap().rearrange("(d j) i -> j d i", j=NT)))
        t_fh = p.op("v", lambda e: e.tensor_copy(fh[:], ff[:]), waits=[t_f])
        t_fl = p.op("v", lambda e: e.tensor_tensor(
            fl[:], ff[:], fh[:], Alu.subtract), waits=[t_fh])
        t_mz = p.op("v", lambda e: e.memset(maxacc[:], 0.0))

        tok_load = [None] * ntiles
        tok_xl = [None] * ntiles
        tok_store_hl = [None] * ntiles
        tok_evac = [None] * ntiles
        tok_store = [None] * ntiles
        tok_statsfree = [None] * ntiles

        def emit_stats(i):
            sl = (i % RING) * H
            xb = xr[:, sl:sl + H]
            w = [tok_load[i]]
            if i >= 1:
                w.append(tok_statsfree[i - 1])
            ta = p.op("a", lambda e, xb=xb: e.activation(
                dump[:], xb, Act.Square, accum_out=sqv[:]), waits=w)
            ta = p.op("a", lambda e, xb=xb: e.activation(
                dump[:], xb, Act.Identity, accum_out=smv[:]), waits=[ta])
            tv = p.op("v", lambda e: e.tensor_scalar(
                muv[:], smv[:], INVH, None, Alu.mult), waits=[ta])
            tv = p.op("v", lambda e: e.tensor_scalar(
                ex2[:], sqv[:], INVH, None, Alu.mult), waits=[tv])
            tok_statsfree[i] = tv
            tv = p.op("v", lambda e: e.tensor_tensor(
                mu2[:], muv[:], muv[:], Alu.mult), waits=[tv])
            tv = p.op("v", lambda e: e.scalar_tensor_tensor(
                varv[:], ex2[:], 1e-5, mu2[:], Alu.add, Alu.subtract),
                waits=[tv])
            ta2 = p.op("a", lambda e: e.activation(
                rrv[:], varv[:], Act.Sqrt), waits=[tv])
            tv = p.op("v", lambda e: e.reciprocal(rsv[:], rrv[:]), waits=[ta2])
            tv = p.op("v", lambda e, xb=xb: e.scalar_tensor_tensor(
                xn[:], xb, muv[:], rsv[:].broadcast_to([NT, H]),
                Alu.subtract, Alu.mult), waits=[tv])
            tv = p.op("v", lambda e: e.tensor_reduce(
                mtv[:], xn[:], AX.X, Alu.max, apply_absolute_value=True),
                waits=[tv])
            tv = p.op("v", lambda e: e.tensor_tensor(
                maxacc[:, 0:1], maxacc[:, 0:1], mtv[:], Alu.max),
                waits=[tv, t_mz])
            wh = [tv, t_fl]
            if i >= RING:
                wh.extend(tok_store_hl[i - RING])
            th = p.op("v", lambda e, sl=sl: e.tensor_copy(
                xhr[:, sl:sl + H], xn[:]), waits=wh)
            tl = p.op("v", lambda e, sl=sl: e.tensor_tensor(
                xlr[:, sl:sl + H], xn[:], xhr[:, sl:sl + H], Alu.subtract),
                waits=[th])
            tok_xl[i] = tl
            t1 = p.dma("sxh", lambda e, sl=sl, i=i: e.dma_start(
                out=xh_d.ap()[i * NT:(i + 1) * NT, :], in_=xhr[:, sl:sl + H]),
                waits=[tl], eng="gp")
            t2 = p.dma("sxl", lambda e, sl=sl, i=i: e.dma_start(
                out=xl_d.ap()[i * NT:(i + 1) * NT, :], in_=xlr[:, sl:sl + H]),
                waits=[tl], eng="gp")
            tok_store_hl[i] = (t1, t2)

        def emit_conv(j):
            ps = psc[j % 3]
            dmax = min(ND - 1, ntiles - 1 - j)
            w = [tok_xl[j + dmax], t_fl]
            if j >= 3:
                w.append(tok_evac[j - 3])
            n_terms = 3 * (dmax + 1)
            cnt = 0
            first = [True, True]
            last = None
            for d in range(dmax + 1):
                sl = ((j + d) % RING) * H
                for (fm, xm) in ((fh, xhr), (fl, xhr), (fh, xlr)):
                    cnt += 1
                    stop = (cnt == n_terms)
                    for half in range(2):
                        cs = slice(half * 512, half * 512 + 512)
                        st = first[half]
                        first[half] = False
                        is_last = (cnt == n_terms and half == 1)
                        last = p.op("t", lambda e, ps=ps, fm=fm, xm=xm, d=d,
                                    sl=sl, cs=cs, st=st, stop=stop: e.matmul(
                                        ps[:, cs], fm[:, d * NT:(d + 1) * NT],
                                        xm[:, sl + cs.start:sl + cs.stop],
                                        start=st, stop=stop), waits=w,
                                    quiet=not is_last)
                        w = []
            sl = (j % RING) * H
            pb = parts[j % 2]
            we = [last, tok_load[j]]
            if j >= 2:
                we.append(tok_store[j - 2])
            te = p.op("v", lambda e, ps=ps, sl=sl, pb=pb: e.tensor_tensor(
                pb[:], ps[:], xr[:, sl:sl + H], Alu.add), waits=we)
            tok_evac[j] = te
            tok_store[j] = p.dma("spt", lambda e, pb=pb, j=j: e.dma_start(
                out=part_d.ap()[j * NT:(j + 1) * NT, :], in_=pb[:]),
                waits=[te], eng="gp")

        for i in range(ntiles):
            w = []
            if i >= RING:
                w.append(tok_evac[i - RING])
            tok_load[i] = p.dma(f"xi{i % 2}", lambda e, i=i: e.dma_start(
                out=xr[:, (i % RING) * H:(i % RING) * H + H],
                in_=xs.ap()[i * NT:(i + 1) * NT, :]), waits=w)
            emit_stats(i)
            if i >= ND:
                emit_conv(i - ND)
        for j in range(max(0, ntiles - ND), ntiles):
            emit_conv(j)

        max_finalize(nc, es, p, maxacc, idf_sb, "m1loc", [t_id])
        p.barrier_all()
        p.emit_block(nc, es)
    return nc



def build_L2(T):
    ngrp = T // TQ
    ICH = I3 // NT
    nc = bass.Bass("TRN2", target_bir_lowering=False)
    xh_d = nc.dram_tensor("xh", [T, H], BF16, kind="ExternalInput")
    xl_d = nc.dram_tensor("xl", [T, H], BF16, kind="ExternalInput")
    wu_d = nc.dram_tensor("wu", [I3, H], F32, kind="ExternalInput")
    wg_d = nc.dram_tensor("wg", [I3, H], F32, kind="ExternalInput")
    ut_d = nc.dram_tensor("ut", [I3, T], F32, kind="ExternalOutput")
    gt_d = nc.dram_tensor("gt", [I3, T], F32, kind="ExternalOutput")
    idb_c = nc.inline_tensor(np.eye(NT, dtype=BF), name="idb")
    idf_c = nc.inline_tensor(np.eye(NT, dtype=np.float32), name="idf")

    p = Plan()
    es = contextlib.ExitStack()
    with es:
        mxt, mxtok = maxes_prologue(nc, es, p, ["m1all"])
        mx1 = mxt["m1all"]
        idb_sb = es.enter_context(nc.sbuf_tensor("idb_sb", [NT, NT], BF16))
        idf_sb = es.enter_context(nc.sbuf_tensor("idf_sb", [NT, NT], F32))
        s1c = es.enter_context(nc.sbuf_tensor("s1c", [NT, 1], F32))
        c1c = es.enter_context(nc.sbuf_tensor("c1c", [NT, 1], F32))
        wut = es.enter_context(nc.sbuf_tensor("wut", [NT, 8 * I3], BF16))
        wgt = es.enter_context(nc.sbuf_tensor("wgt", [NT, 8 * I3], BF16))
        xhg = [es.enter_context(nc.sbuf_tensor(f"xhg{k}", [NT, 4 * H], BF16))
               for k in range(2)]
        xlg = [es.enter_context(nc.sbuf_tensor(f"xlg{k}", [NT, 4 * H], BF16))
               for k in range(2)]
        xqt = [es.enter_context(nc.sbuf_tensor(f"xqt{k}", [NT, 8 * TQ], BF16))
               for k in range(2)]
        xnb = es.enter_context(nc.sbuf_tensor("xnb", [NT, H], F32))
        t1b = es.enter_context(nc.sbuf_tensor("t1b", [NT, H], F32))
        r1i = es.enter_context(nc.sbuf_tensor("r1i", [NT, H], I16))
        bC = es.enter_context(nc.sbuf_tensor("bC", [NT, 1], F32))
        bNC = es.enter_context(nc.sbuf_tensor("bNC", [NT, 1], F32))
        r1f = [es.enter_context(nc.sbuf_tensor(f"r1f{k}", [NT, H], BF16))
               for k in range(4)]
        ue = [es.enter_context(nc.sbuf_tensor(f"ue{k}", [NT, TQ], F32))
              for k in range(3)]
        ge = [es.enter_context(nc.sbuf_tensor(f"ge{k}", [NT, TQ], F32))
              for k in range(3)]
        nch_mm = (T // TQ) * (I3 // NT)
        urd = es.enter_context(nc.sbuf_tensor("urd", [NT, nch_mm], F32))
        grd = es.enter_context(nc.sbuf_tensor("grd", [NT, nch_mm], F32))
        maxacc = es.enter_context(nc.sbuf_tensor("maxacc", [NT, NMAX], F32))
        psu = [es.enter_context(nc.psum_tensor(f"pu{k}", [NT, TQ], F32))
               for k in range(3)]
        psg = [es.enter_context(nc.psum_tensor(f"pg{k}", [NT, TQ], F32))
               for k in range(3)]
        psx = [es.enter_context(nc.psum_tensor(f"px{k}", [NT, 8 * NT], BF16))
               for k in range(1)]
        psxq = es.enter_context(nc.psum_tensor("pxq", [NT, 8 * NT], BF16))

        t_id = p.dma("mis", lambda e: e.dma_start(
            out=idb_sb[:], in_=idb_c.ap()))
        t_idf = p.dma("mis", lambda e: e.dma_start(
            out=idf_sb[:], in_=idf_c.ap()))
        t_mz = p.op("v", lambda e: e.memset(maxacc[:], 0.0))
        t_mz = p.op("v", lambda e: e.memset(bC[:], MAGIC), waits=[t_mz])
        t_mz = p.op("v", lambda e: e.memset(bNC[:], -MAGIC), waits=[t_mz])
        tv = p.op("v", lambda e: e.tensor_scalar(
            s1c[:], mx1[:, 0:1], INVQ, EPS12, Alu.mult, Alu.max),
            waits=[mxtok["m1all"]])
        t_c1 = p.op("v", lambda e: e.reciprocal(c1c[:], s1c[:]),
                    waits=[tv, t_mz])

        tok_w, tok_wmap = dequant_weights(
            nc, es, p, [(wu_d, I3, H, wut), (wg_d, I3, H, wgt)], idb_sb, "w",
            psw=psx, tok_id=t_id)

        tok_mm = {"u": [[None] * ICH for _ in range(ngrp)],
                  "g": [[None] * ICH for _ in range(ngrp)]}
        tok_ev = {"u": [[None] * ICH for _ in range(ngrp)],
                  "g": [[None] * ICH for _ in range(ngrp)]}
        tok_st = {"u": [[None] * ICH for _ in range(ngrp)],
                  "g": [[None] * ICH for _ in range(ngrp)]}
        tok_rd = {"u": [[None] * ICH for _ in range(ngrp)],
                  "g": [[None] * ICH for _ in range(ngrp)]}
        tok_ld = [None] * ngrp
        tok_chainTT = [None] * ngrp
        tok_xqt = [None] * ngrp
        st = {"t1": None, "r1i": None, "xev": tok_w,
              "rf": [None] * 4, "rfw": [None] * 4}

        def emit_loads(q):
            w = [tok_chainTT[q - 2]] if q >= 2 else []
            tl = p.dma(f"xi{q % 2}", lambda e, q=q: e.dma_start(
                out=xhg[q % 2][:].rearrange("p (k h) -> p k h", h=H),
                in_=xh_d.ap()[q * TQ:(q + 1) * TQ, :].rearrange(
                    "(k p) h -> p k h", p=NT)), waits=w)
            tok_ld[q] = p.dma(f"xi{q % 2}", lambda e, q=q: e.dma_start(
                out=xlg[q % 2][:].rearrange("p (k h) -> p k h", h=H),
                in_=xl_d.ap()[q * TQ:(q + 1) * TQ, :].rearrange(
                    "(k p) h -> p k h", p=NT)), waits=[tl])

        def emit_chain_compute(q, k):
            cs = slice(k * H, (k + 1) * H)
            tv = p.op("v", lambda e, q=q, cs=cs: e.tensor_tensor(
                xnb[:], xhg[q % 2][:, cs], xlg[q % 2][:, cs], Alu.add),
                waits=[tok_ld[q], t_c1, st["t1"]])
            if k == 3:
                tok_chainTT[q] = tv
            ta1 = p.op("a", lambda e: e.activation(
                t1b[:], xnb[:], Act.Identity, bias=bC[:], scale=c1c[:]),
                waits=[tv, st["r1i"]])
            st["t1"] = ta1
            ta2 = p.op("a", lambda e: e.activation(
                r1i[:], t1b[:], Act.Identity, bias=bNC[:], scale=1.0),
                waits=[ta1])
            st["r1i"] = ta2
            rf = r1f[k]
            st["rfw"][k] = p.op("a", lambda e, rf=rf: e.activation(
                rf[:], r1i[:], Act.Identity), waits=[ta2, st["rf"][k]])

        def emit_chain_transpose(q, k):
            rf = r1f[k]
            wt = [st["rfw"][k], t_id, st["xev"]]
            if k == 0 and q >= 2:
                wt += [tok_mm["u"][q - 2][ICH - 1], tok_mm["g"][q - 2][ICH - 1]]
            ttr = None
            for ci in range(8):
                ttr = p.op("t", lambda e, rf=rf, ci=ci: e.transpose(
                    psxq[:, ci * NT:(ci + 1) * NT],
                    rf[:, ci * NT:(ci + 1) * NT], idb_sb[:]),
                    waits=wt, quiet=(ci != 7))
                wt = []
            st["rf"][k] = ttr
            xev = p.op("a", lambda e, q=q, k=k: e.activation(
                xqt[q % 2][:].rearrange("p (ci tq) -> p ci tq", tq=TQ)
                [:, :, k * NT:(k + 1) * NT], psxq[:], Act.Identity),
                waits=[ttr])
            st["xev"] = xev
            if k == 3:
                tok_xqt[q] = xev

        emit_loads(0)
        for k in range(4):
            emit_chain_compute(0, k)
            emit_chain_transpose(0, k)

        for q in range(ngrp):
            for ii in range(ICH):
                if q + 1 < ngrp:
                    if ii == 0:
                        emit_loads(q + 1)
                    elif ii in (2, 5, 8, 11):
                        emit_chain_compute(q + 1, (ii - 2) // 3)
                    elif ii in (14, 16, 18, 20):
                        emit_chain_transpose(q + 1, (ii - 14) // 2)
                for si, (key, wt_sb, psv, ev, rd, dst) in enumerate((
                    ("u", wut, psu, ue, urd, ut_d),
                    ("g", wgt, psg, ge, grd, gt_d),
                )):
                    ps = psv[ii % 3]
                    wm = [tok_xqt[q], tok_wmap[(si, ii)]]
                    if ii >= 3:
                        wm.append(tok_ev[key][q][ii - 3])
                    elif q >= 1:
                        wm.append(tok_ev[key][q - 1][ICH - 3 + ii])
                    last = None
                    for ci in range(8):
                        last = p.op("t", lambda e, ps=ps, wt_sb=wt_sb, ci=ci,
                                    ii=ii, q=q: e.matmul(
                                        ps[:],
                                        wt_sb[:, ci * I3 + ii * NT:
                                              ci * I3 + (ii + 1) * NT],
                                        xqt[q % 2][:, ci * TQ:(ci + 1) * TQ],
                                        start=(ci == 0), stop=(ci == 7)),
                                    waits=wm, quiet=(ci != 7))
                        wm = []
                    tok_mm[key][q][ii] = last
                    eb = ev[ii % 3]
                    we = [last]
                    if ii >= 3:
                        we.append(tok_st[key][q][ii - 3])
                        we.extend(tok_rd[key][q][ii - 3])
                    elif q >= 1:
                        we.append(tok_st[key][q - 1][ICH - 3 + ii])
                        we.extend(tok_rd[key][q - 1][ICH - 3 + ii])
                    if key == "u":
                        tev = p.op("a", lambda e, ps=ps, eb=eb: e.activation(
                            eb[:], ps[:], Act.Identity), waits=we)
                    else:
                        tev = p.op("v", lambda e, ps=ps, eb=eb: e.tensor_copy(
                            eb[:], ps[:]), waits=we)
                    tok_ev[key][q][ii] = tev
                    idx = q * ICH + ii
                    tn = p.op("v", lambda e, eb=eb, rd=rd, idx=idx:
                              e.tensor_reduce(
                                  rd[:, idx:idx + 1], eb[:], AX.X, Alu.max,
                                  apply_absolute_value=True), waits=[tev])
                    tok_rd[key][q][ii] = (tn,)
                    tok_st[key][q][ii] = p.dma(
                        f"s{key}{ii % 2}", lambda e, eb=eb, ii=ii, q=q,
                        dst=dst:
                        e.dma_start(out=dst.ap()[ii * NT:(ii + 1) * NT,
                                                 q * TQ:(q + 1) * TQ],
                                    in_=eb[:]), waits=[tev], eng="a")

        wlast = list(tok_rd["u"][ngrp - 1][ICH - 1]) + \
            list(tok_rd["g"][ngrp - 1][ICH - 1])
        tfr = p.op("v", lambda e: e.tensor_reduce(
            maxacc[:, 0:1], urd[:], AX.X, Alu.max, apply_absolute_value=True),
            waits=wlast)
        p.op("v", lambda e: e.tensor_reduce(
            maxacc[:, 1:2], grd[:], AX.X, Alu.max, apply_absolute_value=True),
            waits=[tfr])
        max_finalize(nc, es, p, maxacc, idf_sb, "m23loc",
                     [t_idf, ("a", p.count["a"]), ("t", p.count["t"])],
                     psf=psu[0])
        p.barrier_all()
        p.emit_block(nc, es)
    return nc



def build_L3(T):
    ch = min(1024, T)
    nc = bass.Bass("TRN2", target_bir_lowering=False)
    gt_d = nc.dram_tensor("gt", [I3, T], F32, kind="ExternalInput")
    P_d = nc.dram_tensor("P", [I3, T], F32, kind="ExternalOutput")
    idf_c = nc.inline_tensor(np.eye(NT, dtype=np.float32), name="idf")

    p = Plan()
    es = contextlib.ExitStack()
    with es:
        mxt, mxtok = maxes_prologue(nc, es, p, ["m1all", "m23all"])
        mx1, mx23 = mxt["m1all"], mxt["m23all"]
        idf_sb = es.enter_context(nc.sbuf_tensor("idf_sb", [NT, NT], F32))
        s1c = es.enter_context(nc.sbuf_tensor("s1c", [NT, 1], F32))
        s3c = es.enter_context(nc.sbuf_tensor("s3c", [NT, 1], F32))
        c3n = es.enter_context(nc.sbuf_tensor("c3n", [NT, 1], F32))
        tmpc = es.enter_context(nc.sbuf_tensor("tmpc", [NT, 1], F32))
        bC = es.enter_context(nc.sbuf_tensor("bC", [NT, 1], F32))
        bNC = es.enter_context(nc.sbuf_tensor("bNC", [NT, 1], F32))
        nchunk = (I3 // NT) * (T // ch)
        ncol = T // ch

        def fam(name, par, dt=F32):
            return [es.enter_context(nc.sbuf_tensor(f"{name}{k}", [NT, ch],
                                                    dt)) for k in range(par)]

        gb = fam("gb", 3)
        t3b = fam("t3b", 3)
        r3b = fam("r3b", 3)
        r3ib = fam("r3ib", 5, I16)
        sgb = fam("sgb", 2)
        t4b = fam("t4b", 3)
        r4b = fam("r4b", 3, U16)
        Pb = fam("Pb", 3)
        rdw = es.enter_context(nc.sbuf_tensor("rdw", [NT, nchunk], F32))
        maxacc = es.enter_context(nc.sbuf_tensor("maxacc", [NT, NMAX], F32))

        t_idf = p.dma("mis", lambda e: e.dma_start(
            out=idf_sb[:], in_=idf_c.ap()))
        t_mz = p.op("v", lambda e: e.memset(maxacc[:], 0.0))
        t_mz = p.op("v", lambda e: e.memset(bC[:], MAGIC), waits=[t_mz])
        t_mz = p.op("v", lambda e: e.memset(bNC[:], -MAGIC), waits=[t_mz])
        wv = [mxtok["m1all"], mxtok["m23all"]]
        tv = p.op("v", lambda e: e.tensor_scalar(
            s1c[:], mx1[:, 0:1], INVQ, EPS12, Alu.mult, Alu.max), waits=wv)
        tv = p.op("v", lambda e: e.tensor_tensor(
            tmpc[:], s1c[:], mx23[:, 1:2], Alu.mult), waits=[tv])
        t_s3 = p.op("v", lambda e: e.tensor_scalar(
            s3c[:], tmpc[:], INVQ, EPS12, Alu.mult, Alu.max), waits=[tv])
        tv = p.op("v", lambda e: e.reciprocal(c3n[:], mx23[:, 1:2]),
                  waits=[t_s3])
        t_c3 = p.op("v", lambda e: e.tensor_scalar(
            c3n[:], c3n[:], -Q15, None, Alu.mult), waits=[tv, t_mz])

        tok = {k: [None] * nchunk for k in
               ("ld", "t3", "r3f", "r3i", "sg", "t4", "rm4", "r4", "P",
                "red", "st")}

        def rc(c):
            ii, qq = divmod(c, ncol)
            return slice(ii * NT, (ii + 1) * NT), slice(qq * ch, (qq + 1) * ch)

        def tk(k, c):
            return tok[k][c] if c >= 0 else None

        def ph_ld(c):
            rows, cols = rc(c)
            tok["ld"][c] = p.dma(f"in{c % 3}", lambda e, c=c, rows=rows,
                                 cols=cols: e.dma_start(
                out=gb[c % 3][:], in_=gt_d.ap()[rows, cols]),
                waits=[tk("t3", c - 3)])

        def ph_A1(c):
            tok["t3"][c] = p.op("a", lambda e, c=c: e.activation(
                t3b[c % 3][:], gb[c % 3][:], Act.Identity, bias=bC[:],
                scale=c3n[:]),
                waits=[tok["ld"][c], t_c3, tk("r3f", c - 3)])

        def ph_V1(c):
            tok["r3f"][c] = p.op("v", lambda e, c=c: e.tensor_scalar(
                r3b[c % 3][:], t3b[c % 3][:], -MAGIC, None, Alu.add),
                waits=[tok["t3"][c], tk("sg", c - 3)])
            tok["r3i"][c] = p.op("v", lambda e, c=c: e.tensor_copy(
                r3ib[c % 5][:], r3b[c % 3][:]),
                waits=[tok["r3f"][c], tk("P", c - 5)])

        def ph_A2(c):
            tok["sg"][c] = p.op("a", lambda e, c=c: e.activation(
                sgb[c % 2][:], r3b[c % 3][:], Act.Sigmoid, bias=0.0,
                scale=s3c[:]),
                waits=[tok["r3f"][c], t_s3, tk("t4", c - 2)])

        def ph_V2(c):
            tok["t4"][c] = p.op("v", lambda e, c=c: e.tensor_scalar(
                t4b[c % 3][:], sgb[c % 2][:], 65536.0, MAGIC, Alu.mult,
                Alu.add), waits=[tok["sg"][c], tk("rm4", c - 3)])
            tok["rm4"][c] = p.op("v", lambda e, c=c: e.tensor_scalar(
                t4b[c % 3][:], t4b[c % 3][:], MAGIC + 65535.0, None, Alu.min),
                waits=[tok["t4"][c], tk("r4", c - 3)])

        def ph_A3(c):
            tok["r4"][c] = p.op("a", lambda e, c=c: e.activation(
                r4b[c % 3][:], t4b[c % 3][:], Act.Identity, bias=bNC[:],
                scale=1.0), waits=[tok["rm4"][c], tk("P", c - 3)])

        def ph_G1(c):
            tok["P"][c] = p.op("gp", lambda e, c=c: e.tensor_tensor(
                Pb[c % 3][:], r3ib[c % 5][:], r4b[c % 3][:], Alu.mult),
                waits=[tok["r4"][c], tok["r3i"][c], tk("st", c - 3),
                       tk("red", c - 3)])

        def ph_V3(c):
            rows, cols = rc(c)
            tok["red"][c] = p.op("v", lambda e, c=c: e.tensor_reduce(
                rdw[:, c:c + 1], Pb[c % 3][:], AX.X, Alu.max,
                apply_absolute_value=True), waits=[tok["P"][c]])
            tok["st"][c] = p.dma(f"sP{c % 3}", lambda e, c=c, rows=rows,
                                 cols=cols: e.dma_start(
                out=P_d.ap()[rows, cols], in_=Pb[c % 3][:]),
                waits=[tok["P"][c]], eng="a")

        phases = [(0, ph_ld), (2, ph_A1), (3, ph_V1), (4, ph_A2),
                  (5, ph_V2), (6, ph_A3), (7, ph_G1), (8, ph_V3)]
        for i in range(nchunk + 8):
            for off, fn in phases:
                c = i - off
                if 0 <= c < nchunk:
                    fn(c)

        p.op("v", lambda e: e.tensor_reduce(
            maxacc[:, 0:1], rdw[:], AX.X, Alu.max, apply_absolute_value=True),
            waits=[t_mz] + [tok["red"][c] for c in range(nchunk)])

        max_finalize(nc, es, p, maxacc, idf_sb, "mPloc", [t_idf])
        p.barrier_all()
        p.emit_block(nc, es)
    return nc



def build_L4(T):
    ch = min(1024, T)
    nc = bass.Bass("TRN2", target_bir_lowering=False)
    ut_d = nc.dram_tensor("ut", [I3, T], F32, kind="ExternalInput")
    P_d = nc.dram_tensor("P", [I3, T], F32, kind="ExternalInput")
    P2_d = nc.dram_tensor("P2", [I3, T], F32, kind="ExternalOutput")
    idf_c = nc.inline_tensor(np.eye(NT, dtype=np.float32), name="idf")

    p = Plan()
    es = contextlib.ExitStack()
    with es:
        mxt, mxtok = maxes_prologue(nc, es, p, ["m23all", "mPall"])
        mx23, mxP = mxt["m23all"], mxt["mPall"]
        idf_sb = es.enter_context(nc.sbuf_tensor("idf_sb", [NT, NT], F32))
        c2c = es.enter_context(nc.sbuf_tensor("c2c", [NT, 1], F32))
        c5c = es.enter_context(nc.sbuf_tensor("c5c", [NT, 1], F32))
        bC = es.enter_context(nc.sbuf_tensor("bC", [NT, 1], F32))
        bNC = es.enter_context(nc.sbuf_tensor("bNC", [NT, 1], F32))
        nchunk = (I3 // NT) * (T // ch)
        ncol = T // ch

        def fam(name, par, dt=F32):
            return [es.enter_context(nc.sbuf_tensor(f"{name}{k}", [NT, ch],
                                                    dt)) for k in range(par)]

        ub = fam("ub", 3)
        Pb = fam("Pb", 3)
        t2b = fam("t2b", 3)
        r2ib = fam("r2ib", 4, I16)
        t5b = fam("t5b", 3)
        r5b = fam("r5b", 3)
        r5ib = fam("r5ib", 3, I16)
        P2b = fam("P2b", 3)
        rdw = es.enter_context(nc.sbuf_tensor("rdw", [NT, nchunk], F32))
        maxacc = es.enter_context(nc.sbuf_tensor("maxacc", [NT, NMAX], F32))

        t_idf = p.dma("mis", lambda e: e.dma_start(
            out=idf_sb[:], in_=idf_c.ap()))
        t_mz = p.op("v", lambda e: e.memset(maxacc[:], 0.0))
        t_mz = p.op("v", lambda e: e.memset(bC[:], MAGIC), waits=[t_mz])
        t_mz = p.op("v", lambda e: e.memset(bNC[:], -MAGIC), waits=[t_mz])
        wv = [mxtok["m23all"], mxtok["mPall"]]
        tv = p.op("v", lambda e: e.reciprocal(c2c[:], mx23[:, 0:1]), waits=wv)
        t_c2 = p.op("v", lambda e: e.tensor_scalar(
            c2c[:], c2c[:], Q15, None, Alu.mult), waits=[tv, t_mz])
        tv = p.op("v", lambda e: e.reciprocal(c5c[:], mxP[:, 0:1]),
                  waits=[t_c2])
        t_c5 = p.op("v", lambda e: e.tensor_scalar(
            c5c[:], c5c[:], Q15, None, Alu.mult), waits=[tv])

        tok = {k: [None] * nchunk for k in
               ("ldu", "ldp", "t2", "r2i", "t5", "r5f", "r5i", "P2",
                "red", "st")}

        def rc(c):
            ii, qq = divmod(c, ncol)
            return slice(ii * NT, (ii + 1) * NT), slice(qq * ch, (qq + 1) * ch)

        def tk(k, c):
            return tok[k][c] if c >= 0 else None

        def ph_ldu(c):
            rows, cols = rc(c)
            tok["ldu"][c] = p.dma(f"inu{c % 3}", lambda e, c=c, rows=rows,
                                  cols=cols: e.dma_start(
                out=ub[c % 3][:], in_=ut_d.ap()[rows, cols]),
                waits=[tk("t2", c - 3)])

        def ph_ldp(c):
            rows, cols = rc(c)
            tok["ldp"][c] = p.dma(f"inp{c % 3}", lambda e, c=c, rows=rows,
                                  cols=cols: e.dma_start(
                out=Pb[c % 3][:], in_=P_d.ap()[rows, cols]),
                waits=[tk("t5", c - 3)])

        def ph_A1(c):
            tok["t2"][c] = p.op("a", lambda e, c=c: e.activation(
                t2b[c % 3][:], ub[c % 3][:], Act.Identity, bias=bC[:],
                scale=c2c[:]),
                waits=[tok["ldu"][c], t_c5, tk("r2i", c - 3)])

        def ph_A2(c):
            tok["r2i"][c] = p.op("a", lambda e, c=c: e.activation(
                r2ib[c % 4][:], t2b[c % 3][:], Act.Identity, bias=bNC[:],
                scale=1.0), waits=[tok["t2"][c], tk("P2", c - 4)])

        def ph_A3(c):
            tok["t5"][c] = p.op("a", lambda e, c=c: e.activation(
                t5b[c % 3][:], Pb[c % 3][:], Act.Identity, bias=bC[:],
                scale=c5c[:]),
                waits=[tok["ldp"][c], t_c5, tk("r5f", c - 3)])

        def ph_V1(c):
            tok["r5f"][c] = p.op("v", lambda e, c=c: e.tensor_scalar(
                r5b[c % 3][:], t5b[c % 3][:], -MAGIC, None, Alu.add),
                waits=[tok["t5"][c], tk("r5i", c - 3)])
            tok["r5i"][c] = p.op("v", lambda e, c=c: e.tensor_copy(
                r5ib[c % 3][:], r5b[c % 3][:]),
                waits=[tok["r5f"][c], tk("P2", c - 3)])

        def ph_G1(c):
            tok["P2"][c] = p.op("gp", lambda e, c=c: e.tensor_tensor(
                P2b[c % 3][:], r5ib[c % 3][:], r2ib[c % 4][:], Alu.mult),
                waits=[tok["r5i"][c], tok["r2i"][c], tk("st", c - 3),
                       tk("red", c - 3)])

        def ph_V3(c):
            rows, cols = rc(c)
            tok["red"][c] = p.op("v", lambda e, c=c: e.tensor_reduce(
                rdw[:, c:c + 1], P2b[c % 3][:], AX.X, Alu.max,
                apply_absolute_value=True), waits=[tok["P2"][c]])
            tok["st"][c] = p.dma(f"sP{c % 3}", lambda e, c=c, rows=rows,
                                 cols=cols: e.dma_start(
                out=P2_d.ap()[rows, cols], in_=P2b[c % 3][:]),
                waits=[tok["P2"][c]], eng="a")

        phases = [(0, ph_ldu), (1, ph_ldp), (2, ph_A1), (3, ph_A2),
                  (3, ph_A3), (4, ph_V1), (5, ph_G1), (6, ph_V3)]
        for i in range(nchunk + 6):
            for off, fn in phases:
                c = i - off
                if 0 <= c < nchunk:
                    fn(c)

        p.op("v", lambda e: e.tensor_reduce(
            maxacc[:, 0:1], rdw[:], AX.X, Alu.max, apply_absolute_value=True),
            waits=[t_mz] + [tok["red"][c] for c in range(nchunk)])

        max_finalize(nc, es, p, maxacc, idf_sb, "mP2loc", [t_idf])
        p.barrier_all()
        p.emit_block(nc, es)
    return nc



def build_L5(T):
    ngrp = T // TQ
    ICH = I3 // NT
    nc = bass.Bass("TRN2", target_bir_lowering=False)
    P2_d = nc.dram_tensor("P2", [I3, T], F32, kind="ExternalInput")
    wd_d = nc.dram_tensor("wd", [H, I3], F32, kind="ExternalInput")
    part_d = nc.dram_tensor("partial", [T, H], F32, kind="ExternalInput")
    out_d = nc.dram_tensor("out", [T, H], F32, kind="ExternalOutput")
    idb_c = nc.inline_tensor(np.eye(NT, dtype=BF), name="idb")

    p = Plan()
    es = contextlib.ExitStack()
    with es:
        mxt, mxtok = maxes_prologue(
            nc, es, p, ["m1all", "m23all", "mPall", "mP2all"])
        mx1, mx23 = mxt["m1all"], mxt["m23all"]
        mxP, mxP2 = mxt["mPall"], mxt["mP2all"]
        idb_sb = es.enter_context(nc.sbuf_tensor("idb_sb", [NT, NT], BF16))
        s1c = es.enter_context(nc.sbuf_tensor("s1c", [NT, 1], F32))
        s2c = es.enter_context(nc.sbuf_tensor("s2c", [NT, 1], F32))
        s3c = es.enter_context(nc.sbuf_tensor("s3c", [NT, 1], F32))
        s4c = es.enter_context(nc.sbuf_tensor("s4c", [NT, 1], F32))
        s6c = es.enter_context(nc.sbuf_tensor("s6c", [NT, 1], F32))
        c6c = es.enter_context(nc.sbuf_tensor("c6c", [NT, 1], F32))
        tmpc = es.enter_context(nc.sbuf_tensor("tmpc", [NT, 1], F32))
        bC = es.enter_context(nc.sbuf_tensor("bC", [NT, 1], F32))
        bNC = es.enter_context(nc.sbuf_tensor("bNC", [NT, 1], F32))
        wdt = es.enter_context(nc.sbuf_tensor("wdt", [NT, ICH * H], BF16))
        BW = 2 * TQ
        P2b = [es.enter_context(nc.sbuf_tensor(f"P2b{k}", [NT, BW], F32))
               for k in range(2)]
        t6b = [es.enter_context(nc.sbuf_tensor(f"t6b{k}", [NT, BW], F32))
               for k in range(2)]
        r6i = [es.enter_context(nc.sbuf_tensor(f"r6i{k}", [NT, BW], I16))
               for k in range(2)]
        r6g = [es.enter_context(nc.sbuf_tensor(f"r6g{k}", [NT, ICH * TQ],
                                                BF16)) for k in range(2)]
        prt = [es.enter_context(nc.sbuf_tensor(f"prt{k}", [NT, H], F32))
               for k in range(2)]
        osb = [es.enter_context(nc.sbuf_tensor(f"osb{k}", [NT, H], F32))
               for k in range(2)]
        fsb = [es.enter_context(nc.sbuf_tensor(f"fsb{k}", [NT, H], F32))
               for k in range(2)]
        psd = [es.enter_context(nc.psum_tensor(f"pd{k}", [NT, TQ], F32))
               for k in range(3)]

        t_id = p.dma("mis", lambda e: e.dma_start(
            out=idb_sb[:], in_=idb_c.ap()))
        wv = [mxtok[n] for n in ("m1all", "m23all", "mPall", "mP2all")]
        tv = p.op("v", lambda e: e.tensor_scalar(
            s1c[:], mx1[:, 0:1], INVQ, EPS12, Alu.mult, Alu.max), waits=wv)
        tv = p.op("v", lambda e: e.tensor_tensor(
            tmpc[:], s1c[:], mx23[:, 0:1], Alu.mult), waits=[tv])
        tv = p.op("v", lambda e: e.tensor_scalar(
            s2c[:], tmpc[:], INVQ, EPS12, Alu.mult, Alu.max), waits=[tv])
        tv = p.op("v", lambda e: e.tensor_tensor(
            tmpc[:], s1c[:], mx23[:, 1:2], Alu.mult), waits=[tv])
        tv = p.op("v", lambda e: e.tensor_scalar(
            s3c[:], tmpc[:], INVQ, EPS12, Alu.mult, Alu.max), waits=[tv])
        tv = p.op("v", lambda e: e.tensor_tensor(
            tmpc[:], s3c[:], mxP[:, 0:1], Alu.mult), waits=[tv])
        tv = p.op("v", lambda e: e.tensor_scalar(
            tmpc[:], tmpc[:], INV65536, None, Alu.mult), waits=[tv])
        tv = p.op("v", lambda e: e.tensor_scalar(
            s4c[:], tmpc[:], INVQ, EPS12, Alu.mult, Alu.max), waits=[tv])
        tv = p.op("v", lambda e: e.tensor_tensor(
            tmpc[:], s4c[:], s2c[:], Alu.mult), waits=[tv])
        tv = p.op("v", lambda e: e.tensor_tensor(
            tmpc[:], tmpc[:], mxP2[:, 0:1], Alu.mult), waits=[tv])
        t_s6 = p.op("v", lambda e: e.tensor_scalar(
            s6c[:], tmpc[:], INVQ, EPS12, Alu.mult, Alu.max), waits=[tv])
        tv = p.op("v", lambda e: e.reciprocal(c6c[:], mxP2[:, 0:1]),
                  waits=[t_s6])
        t_c6 = p.op("v", lambda e: e.tensor_scalar(
            c6c[:], c6c[:], Q15, None, Alu.mult), waits=[tv])
        t_c6 = p.op("v", lambda e: e.memset(bC[:], MAGIC), waits=[t_c6])
        t_c6 = p.op("v", lambda e: e.memset(bNC[:], -MAGIC), waits=[t_c6])

        tok_wd, _ = dequant_weights(nc, es, p, [(wd_d, H, I3, wdt)], idb_sb,
                                    "d", tok_id=t_id)

        NGR = 8
        tok_mm = [[None] * NGR for _ in range(ngrp)]
        tok_ev = [[None] * NGR for _ in range(ngrp)]
        tok_out = [[None] * 4 for _ in range(ngrp)]
        tok_stf = [[None] * 4 for _ in range(ngrp)]
        tok_p2read = [None, None]
        tok_r6w = [None, None]
        tok_su6 = [None, None]
        tok_r6pair = [None, None]
        tok_r6last = [None] * ngrp

        def emit_chain(q):
            for blk in range(ICH // 2):
                b = blk % 2
                ii0 = blk * 2
                w = [t_c6, tok_p2read[b]]
                tl = p.dma(f"inp{b}", lambda e, q=q, ii0=ii0, b=b: e.dma_start(
                    out=P2b[b][:].rearrange("p (k t) -> p k t", t=TQ),
                    in_=P2_d.ap()[ii0 * NT:(ii0 + 2) * NT,
                                  q * TQ:(q + 1) * TQ].rearrange(
                        "(k p) t -> p k t", p=NT)), waits=w)
                ta = p.op("a", lambda e, b=b: e.activation(
                    t6b[b][:], P2b[b][:], Act.Identity, bias=bC[:],
                    scale=c6c[:]),
                    waits=[tl, t_c6, tok_r6w[b]])
                tok_p2read[b] = ta
                tsu = p.op("v", lambda e, b=b: e.tensor_scalar(
                    t6b[b][:], t6b[b][:], -MAGIC, None, Alu.add), waits=[ta])
                ti = p.op("v", lambda e, b=b: e.tensor_copy(
                    r6i[b][:], t6b[b][:]), waits=[tsu, tok_su6[b]])
                tok_r6w[b] = ti
                wq = [ti]
                if q >= 2 and blk <= 1:
                    wq.append(tok_mm[q - 2][NGR - 1])
                lastv = p.op("v", lambda e, q=q, ii0=ii0, b=b: e.tensor_copy(
                    r6g[q % 2][:, ii0 * TQ:(ii0 + 2) * TQ], r6i[b][:]),
                    waits=wq)
                tok_su6[b] = lastv
                tok_r6pair[b] = lastv
            tok_r6last[q] = (tok_r6pair[0], tok_r6pair[1])

        def emit_mm(q):
            for tt in range(4):
                wl = []
                if tt >= 2:
                    wl.append(tok_out[q][tt - 2])
                elif q >= 1:
                    wl.append(tok_out[q - 1][2 + tt])
                tok_ld = p.dma(f"inq{tt % 2}", lambda e, q=q, tt=tt:
                               e.dma_start(
                    out=prt[tt % 2][:],
                    in_=part_d.ap()[q * TQ + tt * NT:
                                    q * TQ + (tt + 1) * NT, :]), waits=wl)
                for half in range(2):
                    g = tt * 2 + half
                    ps = psd[g % 3]
                    wm = [tok_r6last[q][0], tok_r6last[q][1], tok_wd]
                    if g >= 3:
                        wm.append(tok_ev[q][g - 3])
                    elif q >= 1:
                        wm.append(tok_ev[q - 1][6 + g if g < 2 else 5])
                    last = None
                    for ii in range(ICH):
                        last = p.op("t", lambda e, ps=ps, ii=ii, tt=tt,
                                    half=half, q=q: e.matmul(
                                        ps[:],
                                        r6g[q % 2][:, ii * TQ + tt * NT:
                                                   ii * TQ + (tt + 1) * NT],
                                        wdt[:, ii * H + half * TQ:
                                            ii * H + (half + 1) * TQ],
                                        start=(ii == 0), stop=(ii == ICH - 1)),
                                    waits=wm, quiet=(ii != ICH - 1))
                        wm = []
                    tok_mm[q][g] = last
                    we = [last, t_s6]
                    if tt >= 2 and half == 0:
                        we.append(tok_out[q][tt - 2])
                    elif q >= 1 and half == 0:
                        we.append(tok_out[q - 1][(tt + 2) % 4])
                    tok_ev[q][g] = p.op(
                        "a", lambda e, ps=ps, tt=tt, half=half: e.activation(
                            osb[tt % 2][:, half * TQ:(half + 1) * TQ], ps[:],
                            Act.Identity, bias=0.0, scale=s6c[:]), waits=we)
                wf = [tok_ev[q][tt * 2], tok_ev[q][tt * 2 + 1], tok_ld]
                if tt >= 2:
                    wf.append(tok_stf[q][tt - 2])
                elif q >= 1:
                    wf.append(tok_stf[q - 1][2 + tt])
                tf = p.op("v", lambda e, tt=tt: e.tensor_tensor(
                    fsb[tt % 2][:], osb[tt % 2][:], prt[tt % 2][:], Alu.add),
                    waits=wf)
                tok_out[q][tt] = tf
                tok_stf[q][tt] = p.dma(f"sou{tt % 2}", lambda e, q=q, tt=tt:
                                       e.dma_start(
                    out=out_d.ap()[q * TQ + tt * NT:q * TQ + (tt + 1) * NT, :],
                    in_=fsb[tt % 2][:]), waits=[tf], eng="gp")

        for q in range(ngrp + 1):
            if q < ngrp:
                emit_chain(q)
            if q >= 1:
                emit_mm(q - 1)

        p.barrier_all()
        p.emit_block(nc, es)
    return nc



def _gather_max(results, key):
    m = np.stack([np.asarray(r[key], np.float32).reshape(NMAX)
                  for r in results])
    return np.ascontiguousarray(m.reshape(1, 8 * NMAX))


def kernel(L0_attn_x, L0_attn_filt, L0_ffn_w_gate, L0_ffn_w_up, L0_ffn_w_down):
    x = np.ascontiguousarray(np.asarray(L0_attn_x, np.float32))
    filt = np.ascontiguousarray(np.asarray(L0_attn_filt, np.float32))
    wg = np.ascontiguousarray(np.asarray(L0_ffn_w_gate, np.float32))
    wu = np.ascontiguousarray(np.asarray(L0_ffn_w_up, np.float32))
    wd = np.ascontiguousarray(np.asarray(L0_ffn_w_down, np.float32))
    B, S, _ = x.shape
    T = S
    cores = list(range(NCORES))
    Fz = build_toeplitz(filt)

    r1 = run_bass_kernel_spmd(
        build_L1(T), [{"xs": x[c], "Fz": Fz} for c in range(NCORES)], cores)
    m1all = _gather_max(r1.results, "m1loc")

    r2 = run_bass_kernel_spmd(
        build_L2(T),
        [{"xh": np.asarray(r["xh"]), "xl": np.asarray(r["xl"]),
          "wu": wu, "wg": wg, "m1all": m1all} for r in r1.results], cores)
    m23all = _gather_max(r2.results, "m23loc")

    r3 = run_bass_kernel_spmd(
        build_L3(T),
        [{"gt": np.asarray(r["gt"]), "m1all": m1all, "m23all": m23all}
         for r in r2.results], cores)
    mPall = _gather_max(r3.results, "mPloc")

    r4 = run_bass_kernel_spmd(
        build_L4(T),
        [{"ut": np.asarray(r2.results[c]["ut"]),
          "P": np.asarray(r3.results[c]["P"]),
          "m23all": m23all, "mPall": mPall} for c in range(NCORES)], cores)
    mP2all = _gather_max(r4.results, "mP2loc")

    r5 = run_bass_kernel_spmd(
        build_L5(T),
        [{"P2": np.asarray(r4.results[c]["P2"]), "wd": wd,
          "partial": np.asarray(r1.results[c]["partial"]),
          "m1all": m1all, "m23all": m23all, "mPall": mPall,
          "mP2all": mP2all} for c in range(NCORES)], cores)

    out = np.stack([np.asarray(r["out"], np.float32) for r in r5.results])
    return np.ascontiguousarray(out)
